# version 07
# baseline (speedup 1.0000x reference)
import contextlib
import numpy as np
import ml_dtypes
import concourse.bass as bass
import concourse.mybir as mybir
from concourse.ap import AP
from concourse.bass_utils import run_bass_kernel_spmd

F32 = mybir.dt.float32
F16 = mybir.dt.float16
BF16 = mybir.dt.bfloat16
AF = mybir.ActivationFunctionType
ALU = mybir.AluOpType

B, T, D = 4, 4096, 1024
H, HD = 16, 64
DH = 4096
NCORE = 8
TC = 2048
NG = 4
GT = 512
C = 64
NCH = GT // C
NP = 8
EPS = 1e-5

TAPS = set()
PHASES = {"p1", "mix", "coll", "p2", "p3"}
MIX_GROUPS = NG
MIX_PAIRS = NP
MIX_STAGE = "all"
MIX_SOLVE_K = 6
MIX_SOLVE_SUB = "full"
DBG_J = 0
P3_HALVES = 2
P2_STAGE = "all"
P2_SWEEP_MM = 3



def _lhsT_blocks(w_in_out, kchunks, cchunks):
    a = w_in_out.reshape(kchunks, 128, cchunks, 128).transpose(1, 2, 0, 3)
    return np.ascontiguousarray(a.reshape(128, cchunks * kchunks * 128))


def _consts():
    c = {}
    c["c_ones1c"] = np.ones((128, 1), np.float32)
    s64 = np.zeros((128, 65), np.float32); s64[:, 64] = 1.0
    c["c_sel64"] = s64
    o65 = np.zeros((128, 65), np.float32); o65[:, 0] = 1.0
    c["c_ones65"] = o65
    sh = np.zeros((128, 65), np.float32)
    sh[:64, 0] = 1.0; sh[64:, 64] = 1.0
    c["c_selH65"] = sh
    c["c_ones1"] = np.ones((1, 128), np.float32)
    hA = np.zeros((1, 128), np.float32); hA[0, :64] = 1.0
    hB = np.zeros((1, 128), np.float32); hB[0, 64:] = 1.0
    c["c_hA"], c["c_hB"] = hA, hB
    sp = np.zeros((16, NP * 128), np.float32)
    for f in range(NP):
        sp[2 * f, f * 128: f * 128 + 64] = -1.0
        sp[2 * f + 1, f * 128 + 64: f * 128 + 128] = -1.0
    c["c_selPair"] = sp
    s = np.arange(64)
    mU = (s[:, None] < s[None, :]).astype(np.float32)
    mI = (s[:, None] <= s[None, :]).astype(np.float32)
    mL = (s[None, :] < s[:, None]).astype(np.float32)
    c["c_maskU8"] = np.tile(np.tile(mU, (2, 1)), (1, NCH))
    c["c_maskI8"] = np.tile(np.tile(mI, (2, 1)), (1, NCH))
    c["c_maskL8"] = np.tile(np.tile(mL, (2, 1)), (1, NCH))
    c["c_ident2"] = np.tile(np.eye(64, dtype=np.float32), (2, 1))
    c["c_ident"] = np.eye(128, dtype=np.float32)
    c["c_eps"] = np.full((128, 1), EPS, np.float32)
    c["c_id2b"] = np.tile(np.eye(64, dtype=np.float32), (2, 1))
    return c


def _prep_inputs(core, inp):
    b, half = core // 2, core % 2
    t0 = half * TC
    d = {}
    xs = np.asarray(inp["L0_attn_x"])[b, t0:t0 + TC, :]
    d["xfm"] = np.ascontiguousarray(xs.T.reshape(8, 128, TC))
    d["wq"] = _lhsT_blocks(np.asarray(inp["L0_attn_Wq"]), 8, 8)
    d["wk"] = _lhsT_blocks(np.asarray(inp["L0_attn_Wk"]), 8, 8)
    d["wv"] = _lhsT_blocks(np.asarray(inp["L0_attn_Wv"]), 8, 8)
    d["wo"] = _lhsT_blocks(np.asarray(inp["L0_attn_Wo"]), 8, 8)
    d["wf1"] = np.ascontiguousarray(np.asarray(inp["L0_attn_Wf1"]).reshape(8, 128, 64)
                                    .transpose(1, 0, 2).reshape(128, 512))
    d["wf2"] = np.ascontiguousarray(np.asarray(inp["L0_attn_Wf2"]))
    d["wog1"] = np.ascontiguousarray(np.asarray(inp["L0_attn_Wog1"]).reshape(8, 128, 64)
                                     .transpose(1, 0, 2).reshape(128, 512))
    d["wog2"] = np.ascontiguousarray(np.asarray(inp["L0_attn_Wog2"]))
    d["wgam"] = np.ascontiguousarray(np.asarray(inp["L0_attn_Wgamma"]).reshape(8, 128, 16)
                                     .transpose(1, 0, 2).reshape(128, 128))
    d["nw"] = np.ascontiguousarray(np.asarray(inp["L0_attn_norm_w"]).reshape(8, 128).T)
    d["w1"] = _lhsT_blocks(np.asarray(inp["L0_ffn_w1_w"]).T, 8, 32)
    d["w2"] = _lhsT_blocks(np.asarray(inp["L0_ffn_w2_w"]).T, 8, 32)
    d["h1"] = _lhsT_blocks(np.asarray(inp["L0_ffn_h1_w"]).T, 32, 32)
    d["h2"] = _lhsT_blocks(np.asarray(inp["L0_ffn_h2_w"]).T, 32, 32)
    d["w3"] = _lhsT_blocks(np.asarray(inp["L0_ffn_w3_w"]).T, 32, 8)
    d["s1"] = np.ascontiguousarray(np.asarray(inp["L0_ffn_w1_s"]).reshape(32, 128).T)
    d["s2"] = np.ascontiguousarray(np.asarray(inp["L0_ffn_w2_s"]).reshape(32, 128).T)
    d["sh1"] = np.ascontiguousarray(np.asarray(inp["L0_ffn_h1_s"]).reshape(32, 128).T)
    d["sh2"] = np.ascontiguousarray(np.asarray(inp["L0_ffn_h2_s"]).reshape(32, 128).T)
    d["s3"] = np.ascontiguousarray(np.asarray(inp["L0_ffn_w3_s"]).reshape(8, 128).T)
    sel = np.zeros((128, NCORE), np.float32)
    if half == 1:
        sel[:, core - 1] = 1.0
    d["sel"] = sel
    d.update(_consts())
    return d



ENGINES = ("tensor", "vector", "scalar", "gpsimd", "sync")


class _Sem:
    __slots__ = ("name", "n", "h")

    def __init__(self, name):
        self.name, self.n, self.h = name, 0, None


class Sched:
    def __init__(self):
        self.streams = {e: [] for e in ENGINES}
        self.sems = {}

    def sem(self, name):
        if name not in self.sems:
            self.sems[name] = _Sem(name)
        return self.sems[name]

    def dma(self, eng, fn, qsem, waits=()):
        w = tuple(waits) + ((qsem, qsem.n),)
        return self.post(eng, fn, waits=w, inc=qsem, amt=16)

    def post(self, eng, fn, waits=(), inc=None, amt=1):
        val = None
        if inc is not None:
            inc.n += amt
            val = inc.n
        self.streams[eng].append((tuple(waits), fn, inc, amt))
        return val

    def mark(self, name):
        for e in ENGINES:
            self.streams[e].append(("MARK", name, None, None))

    def replay(self, eng_name, eng, nc=None):
        import contextlib as _ctx
        cur = None
        for entry in self.streams[eng_name]:
            if entry[0] == "MARK":
                if cur is not None and nc is not None:
                    nc.leave_named_scope(cur[0], cur[1], False)
                    cur = None
                if nc is not None and entry[1] is not None:
                    sid, _ = nc.enter_named_scope(entry[1], False)
                    cur = (entry[1], sid)
                continue
            waits, fn, inc, amt = entry
            for (s, v) in waits:
                if v > 0:
                    eng.wait_ge(s.h, v)
            inst = fn(eng) if fn is not None else None
            if inc is not None:
                if inst is None:
                    inst = eng.nop()
                inst.then_inc(inc.h, amt)
        if cur is not None and nc is not None:
            nc.leave_named_scope(cur[0], cur[1], False)


def _sv(tile, width, count, stride, offset=0):
    base = tile[:, 0:1]
    return AP(base.tensor, base.offset + offset,
              [list(base.ap[0]), [stride, count], [1, width]])



def build(taps=TAPS, phases=PHASES):
    nc = bass.Bass("TRN2", target_bir_lowering=False)
    sc = Sched()
    sA = sc.sem("sA")
    sW = sc.sem("sW")
    sT = sc.sem("sT")
    sV = sc.sem("sV")
    sC = sc.sem("sC")
    sG = sc.sem("sG")
    sS = sc.sem("sS")

    din = {}

    def dram_in(name, shape):
        din[name] = nc.dram_tensor(name, list(shape), F32, kind="ExternalInput").ap()
        return din[name]

    x_d = dram_in("xfm", (8, 128, TC))
    for nm, shp in [("wq", (128, 8192)), ("wk", (128, 8192)), ("wv", (128, 8192)),
                    ("wo", (128, 8192)), ("wf1", (128, 512)), ("wf2", (64, 1024)),
                    ("wog1", (128, 512)), ("wog2", (64, 1024)), ("wgam", (128, 128)),
                    ("nw", (128, 8)),
                    ("w1", (128, 32768)), ("w2", (128, 32768)),
                    ("h1", (128, 131072)), ("h2", (128, 131072)), ("w3", (128, 32768)),
                    ("s1", (128, 32)), ("s2", (128, 32)), ("sh1", (128, 32)),
                    ("sh2", (128, 32)), ("s3", (128, 8)), ("sel", (128, 8)),
                    ("c_ones1c", (128, 1)), ("c_sel64", (128, 65)), ("c_ones65", (128, 65)), ("c_selH65", (128, 65)),
                    ("c_ones1", (1, 128)), ("c_hA", (1, 128)), ("c_hB", (1, 128)),
                    ("c_selPair", (16, NP * 128)),
                    ("c_maskU8", (128, 512)), ("c_maskI8", (128, 512)),
                    ("c_maskL8", (128, 512)),
                    ("c_ident2", (128, 64)), ("c_ident", (128, 128)), ("c_eps", (128, 1)), ("c_id2b", (128, 64))]:
        dram_in(nm, shp)

    out_d = nc.dram_tensor("out", [8, 128, TC], F32, kind="ExternalOutput").ap()
    tap_d = {}

    def tap(name, shape, dtype=F32):
        if name in taps and name not in tap_d:
            tap_d[name] = nc.dram_tensor("tap_" + name, list(shape), dtype,
                                         kind="ExternalOutput").ap()
        return tap_d.get(name)

    x1_d = nc.dram_tensor("x1buf", [8, 128, TC], F32).ap()
    stWT_d = nc.dram_tensor("stWT", [128, NP, TC // C, 128], BF16).ap()
    stO_d = nc.dram_tensor("stO", [NP, 128, TC], BF16).ap()
    cc_in = nc.dram_tensor("cc_in", [NP * 128, 64], F32).ap()
    cc_out = nc.dram_tensor("cc_out", [NCORE * NP * 128, 64], F32).ap()

    stack = contextlib.ExitStack()
    p12stack = contextlib.ExitStack()
    p1stack = contextlib.ExitStack()

    def sb(name, shape, dtype):
        return stack.enter_context(nc.sbuf_tensor(name, list(shape), dtype))

    def sb1(name, shape, dtype):
        return p1stack.enter_context(nc.sbuf_tensor(name, list(shape), dtype))

    def ps(name, shape, dtype=F32):
        return stack.enter_context(nc.psum_tensor(name, list(shape), dtype))

    def ps1(name, shape, dtype=F32):
        return p1stack.enter_context(nc.psum_tensor(name, list(shape), dtype))

    def sb12(name, shape, dtype):
        return p12stack.enter_context(nc.sbuf_tensor(name, list(shape), dtype))

    def ps12(name, shape, dtype=F32):
        return p12stack.enter_context(nc.psum_tensor(name, list(shape), dtype))

    cst = {}
    for nm, shp, dt in [("c_ones1c", (128, 1), BF16), ("c_sel64", (128, 65), BF16),
                        ("c_ones65", (128, 65), BF16),
                        ("c_selH65", (128, 65), BF16),
                        ("c_ones1", (1, 128), BF16), ("c_hA", (1, 128), BF16),
                        ("c_hB", (1, 128), BF16), ("c_selPair", (16, NP * 128), BF16),
                        ("c_maskU8", (128, 512), F32), ("c_maskI8", (128, 512), F32),
                        ("c_maskL8", (128, 512), F32), ("c_ident2", (128, 64), F32),
                        ("c_ident", (128, 128), BF16), ("c_eps", (128, 1), F32),
                        ("c_id2b", (128, 64), BF16)]:
        cst[nm] = sb("sb_" + nm, shp, dt)
    scur = sb("scur", (128, NP * 64), BF16)
    zeros_sb = sb("zeros_sb", (128, C), F32)
    sel_sb = sb("sel_sb", (128, 8), F32)
    x1ln_sb = sb("x1ln_sb", (128, 8 * TC), BF16)
    wo_sb = sb12("wo_sb", (128, 8192), BF16)
    wog2_sb = sb12("wog2_sb", (64, 1024), BF16)
    nw_sb = sb12("nw_sb", (128, 8), F32)
    gpre_sb = sb12("gpre_sb", (64, TC), F32)

    for nm in cst:
        sc.dma("gpsimd", lambda g, t=cst[nm], nm=nm: g.dma_start(t[:, :], din[nm][:, :]), sW)
    sc.dma("gpsimd", lambda g: g.dma_start(wo_sb[:, :], din["wo"][:, :]), sW)
    sc.dma("gpsimd", lambda g: g.dma_start(wog2_sb[:, :], din["wog2"][:, :]), sW)
    sc.dma("sync", lambda s: s.dma_start(nw_sb[:, :], din["nw"][:, :]), sA)
    sc.dma("sync", lambda s: s.dma_start(sel_sb[:, :], din["sel"][:, :]), sA)
    nc.const_aps.register(F32, EPS, cst['c_eps'][:, 0:1])
    w_base, a_base = sW.n, sA.n
    wo_ready = 0
    for k in range(8):
        ap_w = wo_sb[:, :].rearrange("p (c k m) -> p c k m", c=8, k=8)[:, :, k, :]
        wo_ready = sc.post("vector",
                           lambda vv, ap_w=ap_w, k=k: vv.tensor_scalar_mul(
                               ap_w, ap_w, nw_sb[:, k:k + 1]),
                           waits=[(sW, w_base), (sA, a_base)] if k == 0 else (), inc=sV)
    sc.post("vector", lambda v: v.memset(scur[:, :], 0.0), inc=sV)
    zv0 = sc.post("vector", lambda v: v.memset(zeros_sb[:, :], 0.0), inc=sV)

    wq_sb = sb1("wq_sb", (128, 8192), BF16)
    wk_sb = sb1("wk_sb", (128, 8192), BF16)
    wv_sb = sb1("wv_sb", (128, 8192), BF16)
    wf1_sb = sb1("wf1_sb", (128, 512), BF16)
    wf2_sb = sb1("wf2_sb", (64, 1024), BF16)
    wog1_sb = sb1("wog1_sb", (128, 512), BF16)
    wgam_sb = sb1("wgam_sb", (128, 128), BF16)
    for nm, t in [("wq", wq_sb), ("wk", wk_sb), ("wv", wv_sb), ("wf1", wf1_sb),
                  ("wf2", wf2_sb), ("wog1", wog1_sb), ("wgam", wgam_sb)]:
        sc.dma("gpsimd", lambda g, t=t, nm=nm: g.dma_start(t[:, :], din[nm][:, :]), sW)
    pw_ready = sW.n

    xgb = sb1("xgb", (128, 8 * GT), BF16)
    xln = sb1("xln", (128, 8 * GT), BF16)
    sqb = xln
    q_g = sb1("q_g", (128, 8 * GT), BF16)
    k_g = sb1("k_g", (128, 8 * GT), BF16)
    v_g = sb1("v_g", (128, 8 * GT), BF16)
    ngk_g = sb1("ngk_g", (128, 8 * GT), F16)
    f1t = sb1("f1t", (64, GT), BF16)
    gs_g = sb1("gs_g", (16, GT), BF16)
    r_m2 = sb1("r_m2", (1, GT), F32)
    r_a = sb1("r_a", (1, GT), F32)
    r_b = sb1("r_b", (1, GT), F32)
    murt = sb1("murt", (1, 2 * GT), BF16)
    ebuf = sb1("ebuf", (128, GT), F32)

    ka = sb1("ka", (128, GT), BF16)
    khat = sb1("khat", (128, GT), BF16)
    AQs = [sb1(f"AQ{i}", (128, 2 * GT), BF16) for i in range(2)]
    nL = sb1("nL", (128, GT), F32)
    eDCs = [sb1(f"eDC{i}", (128, NCH), F32) for i in range(2)]
    nLCs = [sb1(f"nLC{i}", (128, NCH), F32) for i in range(2)]
    XU = [sb1(f"XU{i}", (128, NCH * 192), BF16) for i in range(2)]
    Tt = [sb1(f"Tt{i}", (128, GT), BF16) for i in range(2)]
    LK = sb1("LK", (128, 2 * GT), BF16)
    T0m = sb1("T0m", (128, GT), BF16)
    ktmp = sb1("ktmp", (128, GT), BF16)
    J_g = sb1("J_g", (128, GT), BF16)
    WTst = sb1("WTst", (128, NCH * 128), BF16)
    Ost = ktmp

    psA = ps1("psA", (128, 512), F32)
    psB = ps1("psB", (128, 512), F32)
    psW_ = ps1("psW", (128, 1024), F32)
    psT4 = ps1("psT4", (128, 1024), BF16)
    psX = ps1("psX", (128, 1024), F32)
    psU = ps1("psU", (128, 512), F32)

    ones1c = cst["c_ones1c"]; sel64 = cst["c_sel64"]; selH65 = cst["c_selH65"]
    ones65 = cst["c_ones65"]
    ones1 = cst["c_ones1"]; hA = cst["c_hA"]; hB = cst["c_hB"]
    selPair = cst["c_selPair"]
    maskU8 = cst["c_maskU8"]; maskI8 = cst["c_maskI8"]; maskL8 = cst["c_maskL8"]
    ident2 = cst["c_ident2"]; ident = cst["c_ident"]; id2b = cst["c_id2b"]

    def mm(psl, lhsT, rhs, start, stop, waits=(), inc=None, tp=None, sgc=False):
        return sc.post("tensor",
                       lambda t, psl=psl, lhsT=lhsT, rhs=rhs, start=start, stop=stop, tp=tp,
                       sgc=sgc: t.matmul(psl, lhsT, rhs, start=start, stop=stop,
                                         tile_position=tp, skip_group_check=sgc),
                       waits=waits, inc=inc)

    QUAD = ((0, 64, (0, 0)), (64, 128, (64, 64)))

    st8 = {"prev_group_v": 0, "stWT_dma": 0, "stO_dma": 0, "tap_dma": 0, "psT4_free": 0}

    def mixer_prep(g, f, prev):
        fsl = slice(f * GT, (f + 1) * GT)
        aq = AQs[f % 2]
        edc = eDCs[f % 2]
        nlc = nLCs[f % 2]
        P = {"aq": aq, "edc": edc}
        gb = mm(psW_[:, 0:GT], selPair[0:16, f * 128:(f + 1) * 128], gs_g[0:16, 0:GT],
                start=True, stop=True, waits=[(sV, prev["psw_free"])], inc=sT)
        kq = sc.post("scalar", lambda s, fsl=fsl: s.activation(sqb[:, 0:GT], k_g[:, fsl],
                                                               AF.Square), inc=sC)
        ssv = mm(psU[0:65, 0:GT], selH65[:, 0:65], sqb[:, 0:GT], start=True, stop=True,
                 waits=[(sC, kq), (sV, prev["psu_free"])], inc=sT)
        v6 = sc.post("vector", lambda v: v.tensor_scalar_max(r_a[0:1, 0:GT], psU[0:1, 0:GT],
                                                             1e-24), waits=[(sT, ssv)], inc=sV)
        v7 = sc.post("vector", lambda v: v.tensor_scalar_max(r_b[0:1, 0:GT], psU[64:65, 0:GT],
                                                             1e-24), inc=sV)
        P["v7"] = v7
        c2a = sc.post("scalar", lambda s: s.activation(r_a[0:1, 0:GT], r_a[0:1, 0:GT], AF.Ln),
                      waits=[(sV, v7)], inc=sC)
        sc.post("scalar", lambda s: s.activation(r_b[0:1, 0:GT], r_b[0:1, 0:GT], AF.Ln))
        sc.post("scalar", lambda s: s.activation(murt[0:1, 0:GT], r_a[0:1, 0:GT], AF.Exp,
                                                 scale=-0.5))
        c2d = sc.post("scalar", lambda s: s.activation(murt[0:1, GT:2 * GT], r_b[0:1, 0:GT],
                                                       AF.Exp, scale=-0.5), inc=sC)
        rb0 = mm(psW_[:, GT:2 * GT], hA[0:1, :], murt[0:1, 0:GT], start=True, stop=False,
                 waits=[(sC, c2d)])
        rb = mm(psW_[:, GT:2 * GT], hB[0:1, :], murt[0:1, GT:2 * GT], start=False, stop=True,
                inc=sT)
        v8 = sc.post("vector", lambda v, fsl=fsl: v.tensor_mul(k_g[:, fsl], k_g[:, fsl],
                                                               psW_[:, GT:2 * GT]),
                     waits=[(sT, rb)], inc=sV)
        v9 = sc.post("vector", lambda v, fsl=fsl: v.tensor_mul(ka[:, 0:GT], k_g[:, fsl],
                                                               psW_[:, 0:GT]), inc=sV)
        P["v9"] = v9
        scv = None
        for j in range(NCH):
            sl = slice(j * C, (j + 1) * C)
            scv = sc.post("vector", lambda v, sl=sl, f=f: v.tensor_tensor_scan(
                nL[:, sl], ngk_g[:, f * GT + sl.start: f * GT + sl.stop],
                zeros_sb[:, 0:C], 0.0, ALU.add, ALU.add),
                waits=[(sV, zv0)] if (g == 0 and f == 0 and j == 0) else (), inc=sV)
        vg = sc.post("vector", lambda v, nlc=nlc: v.tensor_copy(
            nlc[:, 0:NCH], _sv(nL, 1, NCH, C, C - 1)), waits=[(sV, scv)], inc=sV)
        ce = sc.post("scalar", lambda s, nlc=nlc, edc=edc: s.activation(
            edc[:, 0:NCH], nlc[:, 0:NCH], AF.Exp, scale=-1.0), waits=[(sV, vg)], inc=sC)
        P["ce"] = ce
        e1 = sc.post("scalar", lambda s: s.activation(ebuf[:, 0:GT], nL[:, 0:GT], AF.Exp),
                     waits=[(sV, scv)], inc=sC)
        vKH = sc.post("vector", lambda v, fsl=fsl: v.tensor_mul(khat[:, 0:GT], k_g[:, fsl],
                                                                ebuf[:, 0:GT]),
                      waits=[(sC, e1), (sV, v9)], inc=sV)
        e2 = sc.post("scalar", lambda s: s.activation(ebuf[:, 0:GT], nL[:, 0:GT], AF.Exp,
                                                      scale=-1.0), waits=[(sV, vKH)], inc=sC)
        vQH = sc.post("vector", lambda v, aq=aq, f=f: v.tensor_mul(
            _sv(aq, C, NCH, 2 * C, C), _sv(q_g, C, NCH, C, f * GT),
            _sv(ebuf, C, NCH, C)), waits=[(sC, e2)], inc=sV)
        vt = sc.post("vector", lambda v, fsl=fsl: v.tensor_add(nL[:, 0:GT], nL[:, 0:GT],
                                                               ngk_g[:, fsl]),
                     waits=[(sV, vQH)], inc=sV)
        e3 = sc.post("scalar", lambda s: s.activation(ebuf[:, 0:GT], nL[:, 0:GT], AF.Exp,
                                                      scale=-1.0), waits=[(sV, vt)], inc=sC)
        vAH = sc.post("vector", lambda v, aq=aq: v.tensor_mul(
            _sv(aq, C, NCH, 2 * C, 0), _sv(ka, C, NCH, C),
            _sv(ebuf, C, NCH, C)), waits=[(sC, e3)], inc=sV)
        P["vAH"] = vAH
        return P

    def mixer_core1(g, f, P):
        aq, edc = P["aq"], P["edc"]
        gl_done = g0_done = trv = None
        for j in range(NCH):
            for (r0, r1, tp) in QUAD:
                gl_done = mm(psW_[r0:r1, j * 128:(j + 1) * 128],
                             khat[r0:r1, j * C:(j + 1) * C],
                             aq[r0:r1, j * 128:(j + 1) * 128], True, True,
                             waits=[(sV, P["vAH"])] if (j == 0 and r0 == 0) else (),
                             inc=sT if (j == NCH - 1 and r0 == 64) else None, tp=tp)
        for j in range(NCH):
            for (r0, r1, tp) in QUAD:
                g0_done = mm(psU[r0:r1, j * C:(j + 1) * C],
                             aq[r0:r1, j * 128: j * 128 + C],
                             khat[r0:r1, j * C:(j + 1) * C], True, True,
                             inc=sT if (j == NCH - 1 and r0 == 64) else None, tp=tp)
        for j in range(NCH):
            for (r0, r1, tp) in QUAD:
                sc.post("tensor", lambda t, j=j, r0=r0, r1=r1, tp=tp, f=f: t.transpose(
                    psT4[r0:r1, j * 128: j * 128 + 64],
                    v_g[r0:r1, f * GT + j * C: f * GT + (j + 1) * C],
                    id2b[r0:r1, 0:64], tile_position=tp),
                    waits=[(sC, P["psT4_free"])] if (j == 0 and r0 == 0) else ())
                trv = sc.post("tensor", lambda t, j=j, r0=r0, r1=r1, tp=tp, aq=aq: t.transpose(
                    psT4[r0:r1, j * 128 + 64: j * 128 + 128],
                    aq[r0:r1, j * 128: j * 128 + C],
                    id2b[r0:r1, 0:64], tile_position=tp),
                    inc=sT if (j == NCH - 1 and r0 == 64) else None)
        evT0 = sc.post("vector", lambda v: v.tensor_mul(
            _sv(T0m, C, NCH, C), _sv(psW_, C, NCH, 128), _sv(maskU8, C, NCH, C)),
            waits=[(sT, gl_done)], inc=sV)
        evLam = sc.post("vector", lambda v: v.tensor_mul(
            _sv(LK, C, NCH, 2 * C), _sv(psW_, C, NCH, 128, C),
            _sv(maskI8, C, NCH, C)), inc=sV)
        evU0 = sc.post("vector", lambda v: v.tensor_mul(
            _sv(XU[0], C, NCH, 192, 128), _sv(psU, C, NCH, C),
            _sv(maskL8, C, NCH, C)), waits=[(sT, g0_done)], inc=sV)
        evX0 = sc.post("scalar", lambda sy: sy.copy(
            _sv(XU[0], 128, NCH, 192), _sv(psT4, 128, NCH, 128)),
            waits=[(sT, trv)], inc=sC)
        kv = None
        for j in range(NCH):
            kv = sc.post("vector", lambda v, j=j, edc=edc: v.tensor_scalar_mul(
                ktmp[:, j * C:(j + 1) * C], khat[:, j * C:(j + 1) * C], edc[:, j:j + 1]),
                waits=[(sC, P["ce"]), (sA, st8["stO_dma"])] if j == 0 else (),
                inc=sV if j == NCH - 1 else None)
        ktv = None
        for j in range(NCH):
            for (r0, r1, tp) in QUAD:
                ktv = sc.post("tensor", lambda t, j=j, r0=r0, r1=r1, tp=tp: t.transpose(
                    psT4[r0:r1, j * 64:(j + 1) * 64], ktmp[r0:r1, j * C:(j + 1) * C],
                    id2b[r0:r1, 0:64], tile_position=tp),
                    waits=[(sV, kv), (sC, evX0)] if (j == 0 and r0 == 0) else (),
                    inc=sT if (j == NCH - 1 and r0 == 64) else None)
        evKT = sc.post("scalar", lambda sy: sy.copy(
            _sv(LK, C, NCH, 2 * C, C), _sv(psT4, C, NCH, C)),
            waits=[(sT, ktv)], inc=sC)
        P.update(evT0=evT0, evLam=evLam, evU0=evU0, evX0=evX0, evKT=evKT)

    def mixer_core2(g, f, P, nxt):
        aq = P["aq"]
        edc = P["edc"]
        curT = T0m
        cur = 0
        last_solve_v = P["evU0"]
        last_solve_c = P["evKT"]
        for k in range(MIX_SOLVE_K):
            do_u, do_t = k < 4, k < 5
            if MIX_SOLVE_SUB == 'x':
                do_u = do_t = False
            elif MIX_SOLVE_SUB == 'xu':
                do_t = False
            nbatch = 1 + (1 if do_u else 0) + (1 if do_t else 0)
            last_mm = None

            def _is_last(b, j, r0):
                return b == nbatch - 1 and j == NCH - 1 and r0 == 64
            for j in range(NCH):
                for (r0, r1, tp) in QUAD:
                    last_mm = mm(psX[r0:r1, j * 128:(j + 1) * 128],
                                 curT[r0:r1, j * C:(j + 1) * C],
                                 XU[cur][r0:r1, j * 192: j * 192 + 128], True, True,
                                 waits=([(sV, last_solve_v)] +
                                        ([(sC, last_solve_c)] if last_solve_c else []))
                                 if (j == 0 and r0 == 0) else (),
                                 inc=sT if _is_last(0, j, r0) else None, tp=tp)
            if do_u:
                for j in range(NCH):
                    for (r0, r1, tp) in QUAD:
                        last_mm = mm(psU[r0:r1, j * C:(j + 1) * C],
                                     curT[r0:r1, j * C:(j + 1) * C],
                                     XU[cur][r0:r1, j * 192 + 128: j * 192 + 192],
                                     True, True, tp=tp,
                                     inc=sT if _is_last(1, j, r0) else None)
            if do_t:
                for j in range(NCH):
                    for (r0, r1, tp) in QUAD:
                        last_mm = mm(psB[r0:r1, j * C:(j + 1) * C],
                                     XU[cur][r0:r1, j * 192 + 128: j * 192 + 192],
                                     curT[r0:r1, j * C:(j + 1) * C], True, True, tp=tp,
                                     inc=sT if _is_last(nbatch - 1, j, r0) else None)
            nxt_b = 1 - cur
            ev1 = sc.post("vector", lambda v, cur=cur, nxt_b=nxt_b: v.tensor_add(
                _sv(XU[nxt_b], 128, NCH, 192), _sv(psX, 128, NCH, 128),
                _sv(XU[cur], 128, NCH, 192)), waits=[(sT, last_mm)], inc=sV)
            last_solve_v = ev1
            last_solve_c = 0
            if do_u:
                last_solve_c = sc.post("scalar", lambda sy, nxt_b=nxt_b: sy.copy(
                    _sv(XU[nxt_b], C, NCH, 192, 128), _sv(psU, C, NCH, C)),
                    waits=[(sT, last_mm)], inc=sC)
            if do_t:
                newT = Tt[k % 2]
                last_solve_c = sc.post("scalar", lambda sy, newT=newT: sy.copy(
                    newT[:, 0:GT], psB[:, 0:GT]), waits=[(sT, last_mm)] if not do_u else (),
                    inc=sC)
                curT = newT
            cur = nxt_b
        last_solve_c = max(last_solve_c, 1)

        def MV_ap(r0, r1, j):
            return XU[cur][r0:r1, j * 192: j * 192 + 64]

        def MA_ap(r0, r1, j):
            return XU[cur][r0:r1, j * 192 + 64: j * 192 + 128]

        wt = None
        wtw = [(sC, last_solve_c)]
        if nxt is not None:
            wtw.append((sV, nxt["v9"]))
        for j in range(NCH):
            for (r0, r1, tp) in QUAD:
                wt = mm(psW_[r0:r1, j * 128:(j + 1) * 128], MA_ap(r0, r1, j),
                        LK[r0:r1, j * 128:(j + 1) * 128], True, True,
                        waits=wtw if (j == 0 and r0 == 0) else (),
                        inc=sT if (j == NCH - 1 and r0 == 64) else None, tp=tp)
        jv = None
        jw = [(sV, last_solve_v)]
        if nxt is not None:
            jw.append((sV, nxt["v7"]))
        for j in range(NCH):
            for (r0, r1, tp) in QUAD:
                jv = mm(psU[r0:r1, j * C:(j + 1) * C],
                        LK[r0:r1, j * 128 + C: j * 128 + 128],
                        MV_ap(r0, r1, j), True, True,
                        waits=jw if (j == 0 and r0 == 0) else (),
                        inc=sT if (j == NCH - 1 and r0 == 64) else None, tp=tp)
        evW = sc.post("vector", lambda v, aq=aq: v.tensor_add(
            _sv(WTst, C, NCH, 128), _sv(psW_, C, NCH, 128),
            _sv(aq, C, NCH, 2 * C, C)),
            waits=[(sT, wt), (sA, st8["stWT_dma"])], inc=sV)
        evTc = None
        for j in range(NCH):
            evTc = sc.post("vector", lambda v, j=j, edc=edc: v.scalar_tensor_tensor(
                WTst[:, j * 128 + 64: j * 128 + 128], ident2[:, 0:64], edc[:, j:j + 1],
                psW_[:, j * 128 + 64: j * 128 + 128], ALU.mult, ALU.add), inc=sV)
        evJ = sc.post("vector", lambda v: v.tensor_copy(J_g[:, 0:GT], psU[:, 0:GT]),
                      waits=[(sT, jv)], inc=sV)
        st8["stWT_dma"] = sc.dma("sync", lambda s, g=g, f=f: s.dma_start(
            stWT_d[:, f, g * NCH:(g + 1) * NCH, :],
            WTst[:, :].rearrange("p (j m) -> p j m", j=NCH)), sA, waits=[(sV, evTc)])
        if "wtst" in taps:
            tap("wtst", (128, NCH * 128), BF16)
            st8["tap_dma"] = sc.dma("sync", lambda s: s.dma_start(
                tap_d["wtst"][:, :], WTst[:, :]), sA, waits=[(sV, evTc)])
        if "jg" in taps:
            tap("jg", (128, GT), F32)
            st8["tap_dma"] = sc.dma("sync", lambda s: s.dma_start(
                tap_d["jg"][:, :], J_g[:, :]), sA, waits=[(sV, evJ)])

        swv = evTc
        obl = None
        for j in range(NCH):
            for (r0, r1, tp) in QUAD:
                mm(psA[r0:r1, j * C:(j + 1) * C], MV_ap(r0, r1, j),
                   LK[r0:r1, j * 128: j * 128 + C], True, False,
                   waits=[(sV, swv)] if r0 == 0 else (), tp=tp, sgc=True)
            for (r0, r1, tp) in QUAD:
                obl = mm(psA[r0:r1, j * C:(j + 1) * C], scur[r0:r1, f * 64:(f + 1) * 64],
                         WTst[r0:r1, j * 128: j * 128 + 64], False, True,
                         inc=sT if r0 == 64 else None, tp=tp, sgc=True)
            smm = None
            for (r0, r1, tp) in QUAD:
                smm = mm(psX[r0:r1, 0:64], WTst[r0:r1, j * 128 + 64: j * 128 + 128],
                         scur[r0:r1, f * 64:(f + 1) * 64], True, True,
                         inc=sT if r0 == 64 else None, tp=tp)
            swv = sc.post("vector", lambda v, j=j, f=f: v.tensor_add(
                scur[:, f * 64:(f + 1) * 64], psX[:, 0:64], J_g[:, j * C:(j + 1) * C]),
                waits=[(sT, smm), (sV, evJ)], inc=sV)
        evO = sc.post("vector", lambda v: v.tensor_copy(Ost[:, 0:GT], psA[:, 0:GT]),
                      waits=[(sT, obl), (sV, swv), (sA, st8["stO_dma"])], inc=sV)
        st8["stO_dma"] = sc.dma("sync", lambda s, g=g, f=f: s.dma_start(
            stO_d[f, :, g * GT:(g + 1) * GT], Ost[:, 0:GT]), sA, waits=[(sV, evO)])
        if "op1" in taps:
            tap("op1", (NP, 128, TC), BF16)
            st8["stO_dma"] = sc.dma("sync", lambda s, g=g, f=f: s.dma_start(
                tap_d["op1"][f, :, g * GT:(g + 1) * GT], Ost[:, 0:GT]), sA)
        P["evW"] = evW
        P["evTc"] = evTc
        P["evJ"] = evJ

    def p1_group(g):
        g0 = g * GT
        src = x_d[:, :, g0:g0 + GT].rearrange("f p t -> p f t")
        ld = sc.dma("gpsimd", lambda gg: gg.dma_start(
            xgb[:, :].rearrange("p (f t) -> p f t", f=8), src),
            sW, waits=[(sV, st8["prev_group_v"]), (sA, st8["tap_dma"])])
        sq = sc.post("scalar", lambda s: s.activation(sqb[:, :], xgb[:, :], AF.Square),
                     waits=[(sW, ld)], inc=sC)
        stv = None
        for f8 in range(8):
            mm(psU[0:65, 0:GT], sel64[:, 0:65], sqb[:, f8 * GT:(f8 + 1) * GT],
               f8 == 0, False,
               waits=[(sC, sq), (sW, pw_ready), (sV, st8["prev_group_v"])] if f8 == 0 else ())
        for f8 in range(8):
            stv = mm(psU[0:65, 0:GT], ones65[:, 0:65], xgb[:, f8 * GT:(f8 + 1) * GT],
                     False, f8 == 7,
                     waits=(), inc=sT if f8 == 7 else None)
        v1 = sc.post("vector", lambda v: v.tensor_scalar_mul(
            r_a[0:1, 0:GT], psU[0:1, 0:GT], 1.0 / D), waits=[(sT, stv)], inc=sV)
        sc.post("vector", lambda v: v.tensor_copy(murt[0:1, 0:GT], r_a[0:1, 0:GT]))
        sc.post("vector", lambda v: v.tensor_scalar_mul(
            r_m2[0:1, 0:GT], psU[64:65, 0:GT], 1.0 / D))
        sc.post("vector", lambda v: v.tensor_mul(
            r_a[0:1, 0:GT], r_a[0:1, 0:GT], r_a[0:1, 0:GT]))
        v4 = sc.post("vector", lambda v: v.tensor_sub(
            r_m2[0:1, 0:GT], r_m2[0:1, 0:GT], r_a[0:1, 0:GT]), inc=sV)
        c1 = sc.post("scalar", lambda s: s.activation(
            r_b[0:1, 0:GT], r_m2[0:1, 0:GT], AF.Ln, bias=EPS), waits=[(sV, v4)], inc=sC)
        v5 = sc.post("scalar", lambda s: s.activation(
            murt[0:1, GT:2 * GT], r_b[0:1, 0:GT], AF.Exp, scale=-0.5), inc=sC)
        bc = None
        for hh in range(2):
            bc = mm(psW_[:, hh * GT:(hh + 1) * GT], ones1[0:1, :],
                    murt[0:1, hh * GT:(hh + 1) * GT], True, True,
                    waits=[(sC, v5), (sV, v1 + 1)] if hh == 0 else (),
                    inc=sT if hh == 1 else None)
        xv = None
        for f8 in range(8):
            sl = slice(f8 * GT, (f8 + 1) * GT)
            sc.post("vector", lambda v, sl=sl: v.tensor_sub(
                ebuf[:, 0:GT], xgb[:, sl], psW_[:, 0:GT]),
                waits=[(sT, bc)] if f8 == 0 else ())
            xv = sc.post("vector", lambda v, sl=sl: v.tensor_mul(
                xln[:, sl], ebuf[:, 0:GT], psW_[:, GT:2 * GT]), inc=sV)
        if tap("xln", (NG, 128, 8 * GT), BF16) is not None:
            st8["tap_dma"] = sc.dma("sync", lambda s, g=g: s.dma_start(
                tap_d["xln"][g, :, :], xln[:, :]), sA, waits=[(sV, xv)])

        ping = [psA, psB]
        prev_ev = {0: 0, 1: 0}
        nmm = 0
        for (wsb, dst, actf) in [(wq_sb, q_g, AF.Silu), (wk_sb, k_g, AF.Silu),
                                 (wv_sb, v_g, None)]:
            for cc in range(8):
                bank = ping[nmm % 2]
                w0 = ([(sC, prev_ev[nmm % 2])] if prev_ev[nmm % 2] else [])
                if nmm == 0:
                    w0 = w0 + [(sV, xv)]
                mv = None
                for kk in range(8):
                    mv = mm(bank[:, 0:GT],
                            wsb[:, cc * 1024 + kk * 128: cc * 1024 + (kk + 1) * 128],
                            xln[:, kk * GT:(kk + 1) * GT], kk == 0, kk == 7,
                            waits=w0 if kk == 0 else (), inc=sT if kk == 7 else None)
                if actf is not None:
                    ev = sc.post("scalar", lambda s, bank=bank, dst=dst, cc=cc, actf=actf:
                                 s.activation(dst[:, cc * GT:(cc + 1) * GT], bank[:, 0:GT], actf),
                                 waits=[(sT, mv)], inc=sC)
                else:
                    ev = sc.post("scalar", lambda s, bank=bank, dst=dst, cc=cc:
                                 s.copy(dst[:, cc * GT:(cc + 1) * GT], bank[:, 0:GT]),
                                 waits=[(sT, mv)], inc=sC)
                prev_ev[nmm % 2] = ev
                nmm += 1
        mv = None
        for kk in range(8):
            mv = mm(psA[0:64, 0:GT], wf1_sb[:, kk * 64:(kk + 1) * 64],
                    xln[:, kk * GT:(kk + 1) * GT], kk == 0, kk == 7,
                    waits=[(sC, prev_ev[0])] if kk == 0 else (), inc=sT if kk == 7 else None)
        evf1 = sc.post("scalar", lambda s: s.copy(f1t[0:64, 0:GT], psA[0:64, 0:GT]),
                       waits=[(sT, mv)], inc=sC)
        mv = None
        for kk in range(8):
            mv = mm(psB[0:64, 0:GT], wog1_sb[:, kk * 64:(kk + 1) * 64],
                    xln[:, kk * GT:(kk + 1) * GT], kk == 0, kk == 7,
                    waits=[(sC, prev_ev[1])] if kk == 0 else (), inc=sT if kk == 7 else None)
        evog = sc.post("scalar", lambda s, g=g: s.copy(
            gpre_sb[0:64, g * GT:(g + 1) * GT], psB[0:64, 0:GT]), waits=[(sT, mv)], inc=sC)
        mv = None
        for kk in range(8):
            mv = mm(psA[0:16, 0:GT], wgam_sb[:, kk * 16:(kk + 1) * 16],
                    xln[:, kk * GT:(kk + 1) * GT], kk == 0, kk == 7,
                    waits=[(sC, evf1)] if kk == 0 else (), inc=sT if kk == 7 else None)
        evgam = sc.post("scalar", lambda s: s.activation(gs_g[0:16, 0:GT], psA[0:16, 0:GT],
                                                         AF.Sigmoid), waits=[(sT, mv)], inc=sC)
        zv = evgam
        for cc in range(8):
            bank = ping[cc % 2]
            mv = mm(bank[:, 0:GT], wf2_sb[0:64, cc * 128:(cc + 1) * 128], f1t[0:64, 0:GT],
                    True, True, waits=[(sC, zv)], inc=sT)
            e1 = sc.post("scalar", lambda s, bank=bank: s.activation(
                ebuf[:, 0:GT], bank[:, 0:GT], AF.Exp, scale=-1.0), waits=[(sT, mv)], inc=sC)
            zv = sc.post("scalar", lambda s, cc=cc: s.activation(
                ngk_g[:, cc * GT:(cc + 1) * GT], ebuf[:, 0:GT], AF.Ln, bias=1.0), inc=sC)
        proj_done_c = sC.n

        if "q" in taps:
            for nm, t in [("q", q_g), ("k", k_g), ("v", v_g)]:
                tap(nm, (NG, 128, 8 * GT), BF16)
                st8["tap_dma"] = sc.dma("sync", lambda s, g=g, t=t, nm=nm: s.dma_start(
                    tap_d[nm][g, :, :], t[:, :]), sA, waits=[(sC, proj_done_c)])
        if "ngk" in taps:
            tap("ngk", (NG, 128, 8 * GT), F32)
            for cc in range(8):
                cv = sc.post("vector", lambda v, cc=cc: v.tensor_copy(
                    ebuf[:, 0:GT], ngk_g[:, cc * GT:(cc + 1) * GT]),
                    waits=[(sC, proj_done_c), (sA, st8["tap_dma"])], inc=sV)
                st8["tap_dma"] = sc.dma("sync", lambda s, g=g, cc=cc: s.dma_start(
                    tap_d["ngk"][g, :, cc * GT:(cc + 1) * GT], ebuf[:, 0:GT]),
                    sA, waits=[(sV, cv)])
        if "gam" in taps:
            tap("gam", (NG, 16, GT), BF16)
            st8["tap_dma"] = sc.dma("sync", lambda s, g=g: s.dma_start(
                tap_d["gam"][g, :, :], gs_g[:, :]), sA, waits=[(sC, proj_done_c)])
        if "gpre" in taps:
            tap("gpre", (NG, 64, GT), F32)
            st8["tap_dma"] = sc.dma("sync", lambda s, g=g: s.dma_start(
                tap_d["gpre"][g, :, :], gpre_sb[0:64, g * GT:(g + 1) * GT]),
                sA, waits=[(sC, proj_done_c)])

        if "mix" in phases:
            Pd = [None] * (MIX_PAIRS + 1)
            Pd[0] = mixer_prep(g, 0, {"psw_free": xv, "psu_free": xv, "psT4": 0})
            for f in range(MIX_PAIRS):
                Pd[f]["psT4_free"] = st8["psT4_free"]
                mixer_core1(g, f, Pd[f])
                st8["psT4_free"] = Pd[f]["evKT"]
                if f + 1 < MIX_PAIRS:
                    Pd[f + 1] = mixer_prep(g, f + 1, {"psw_free": Pd[f]["evLam"],
                                                      "psu_free": Pd[f]["evU0"]})
                mixer_core2(g, f, Pd[f], Pd[f + 1])
        st8["prev_group_v"] = sV.n

    if "p1" in phases:
        for g in range(MIX_GROUPS):
            sc.mark(f"p1g{g}")
            p1_group(g)

    p1stack.close()

    if "p2" in phases:
        p2stack = contextlib.ExitStack()

        def sb2(name, shape, dtype):
            return p2stack.enter_context(nc.sbuf_tensor(name, list(shape), dtype))

        def ps2(name, shape, dtype=F32):
            return p2stack.enter_context(nc.psum_tensor(name, list(shape), dtype))

        xg_sb = sb2("xg_sb", (128, 8 * GT), F32)
        sgath = sb2("sgath", (128, NCORE * NP * 64), F32)
        stil = sb2("stil", (128, NP * 64), BF16)
        wtb = [sb2(f"wtb{i}", (128, NCH * 128), BF16) for i in range(2)]
        ob1 = [sb2(f"ob1_{i}", (128, GT), BF16) for i in range(2)]
        ofin = sb2("ofin", (128, 8 * GT), F32)
        ogb = sb2("ogb", (128, 8 * GT), BF16)
        bbuf = sb2("bbuf", (128, 8 * GT), BF16)
        x1g = sb2("x1g", (128, 8 * GT), F32)
        gbuf = sb2("gbuf", (128, GT), F32)
        gpb = sb2("gpb", (64, GT), BF16)
        eb2 = sb2("eb2", (128, GT), F32)
        r2a = sb2("r2a", (1, GT), F32)
        r2b = sb2("r2b", (1, GT), F32)
        mur2 = sb2("mur2", (1, 2 * GT), BF16)
        psA2 = ps2("psA2", (128, 512), F32)
        psB2 = ps2("psB2", (128, 512), F32)
        psW2 = ps2("psW2", (128, 1024), F32)
        psU2 = ps2("psU2", (128, 512), F32)
        psO2 = ps2("psO2", (128, 512), F32)

        p1_end_v = sV.n
        sc.mark("coll")
        if "coll" in phases:
            cc1 = sc.dma("gpsimd", lambda g: g.dma_start(
                cc_in[:, :].rearrange("(f p) v -> p f v", p=128),
                scur[:, :].rearrange("p (f v) -> p f v", v=64)),
                sW, waits=[(sV, p1_end_v)])
            cg = sc.post("gpsimd", lambda g: g.collective_compute(
                "AllGather", ALU.bypass, replica_groups=[list(range(NCORE))],
                ins=[cc_in], outs=[cc_out]), waits=[(sW, cc1)], inc=sG)
            sc.dma("sync", lambda sy: sy.dma_start(
                sgath[:, :].rearrange("p (r f v) -> p r f v", f=NP, v=64),
                cc_out[:, :].rearrange("(r f p) v -> p r f v", f=NP, p=128)),
                sA, waits=[(sG, cg)])
            gath_a = sA.n
            stv2 = 0
            for f in range(NP):
                sc.post("vector", lambda v, f=f: v.tensor_scalar_mul(
                    eb2[:, 0:64], sgath[:, f * 64: f * 64 + 64], sel_sb[:, 0:1]),
                    waits=[(sA, gath_a)] if f == 0 else ())
                for r in range(1, NCORE):
                    sc.post("vector", lambda v, f=f, r=r: v.scalar_tensor_tensor(
                        eb2[:, 0:64], sgath[:, r * NP * 64 + f * 64: r * NP * 64 + f * 64 + 64],
                        sel_sb[:, r:r + 1], eb2[:, 0:64], ALU.mult, ALU.add))
                stv2 = sc.post("vector", lambda v, f=f: v.tensor_copy(
                    stil[:, f * 64:(f + 1) * 64], eb2[:, 0:64]), inc=sV)
        else:
            stv2 = sc.post("vector", lambda v: v.memset(stil[:, :], 0.0),
                           waits=[(sV, p1_end_v)], inc=sV)

        st2 = {"wt_dma": [0, 0], "ob_dma": [0, 0], "of_ev": 0, "xg_dma": 0,
               "x1_dma": 0, "stil_v": stv2, "ep_c": 0, "ep_v": stv2, "ep_t": 0}
        for g in range(NG):
            sc.mark(f"p2g{g}")
            st2["xg_dma"] = sc.dma("sync", lambda sy, g=g: sy.dma_start(
                xg_sb[:, :].rearrange("p (f t) -> p f t", f=8),
                x_d[:, :, g * GT:(g + 1) * GT].rearrange("f p t -> p f t")),
                sA, waits=[(sV, st2["ep_v"]), (sT, st2["ep_t"])])
            for f in range(NP):
                par = f % 2
                wld = sc.dma("sync", lambda sy, g=g, f=f, par=par: sy.dma_start(
                    wtb[par][:, :].rearrange("p (j m) -> p j m", j=NCH),
                    stWT_d[:, f, g * NCH:(g + 1) * NCH, :]),
                    sA, waits=[(sT, st2["wt_dma"][par])])
                old = sc.dma("sync", lambda sy, g=g, f=f, par=par: sy.dma_start(
                    ob1[par][:, :], stO_d[f, :, g * GT:(g + 1) * GT]),
                    sA, waits=[(sV, st2["ob_dma"][par])])
                ocl = None
                sschain = max(st2["stil_v"], st2["of_ev"])
                for j8 in (range(NCH) if P2_SWEEP_MM else []):
                    for (r0, r1, tp) in QUAD:
                        ocl = mm(psA2[r0:r1, j8 * C:(j8 + 1) * C],
                                 stil[r0:r1, f * 64:(f + 1) * 64],
                                 wtb[par][r0:r1, j8 * 128: j8 * 128 + 64], True, True,
                                 waits=([(sA, wld), (sV, sschain)] if r0 == 0 else ()),
                                 tp=tp)
                    smm2 = None
                    for (r0, r1, tp) in (QUAD if P2_SWEEP_MM >= 2 else []):
                        smm2 = mm(psB2[r0:r1, 0:64],
                                  wtb[par][r0:r1, j8 * 128 + 64: j8 * 128 + 128],
                                  stil[r0:r1, f * 64:(f + 1) * 64], True, True,
                                  inc=sT if r0 == 64 else None, tp=tp)
                    if P2_SWEEP_MM >= 3:
                        sschain = sc.post("vector", lambda v, f=f: v.tensor_copy(
                            stil[:, f * 64:(f + 1) * 64], psB2[:, 0:64]),
                            waits=[(sT, smm2)], inc=sV)
                st2["stil_v"] = sschain
                st2["wt_dma"][par] = sT.n
                ocl = sc.post("tensor", lambda t: t.engine_nop(), inc=sT) if False else ocl
                ev = sc.post("vector", lambda v, f=f, par=par: v.tensor_add(
                    ofin[:, f * GT:(f + 1) * GT], psA2[:, 0:GT], ob1[par][:, :]),
                    waits=[(sT, sT.n), (sA, old), (sV, sschain)], inc=sV)
                st2["ob_dma"][par] = ev
                st2["of_ev"] = ev

            if P2_STAGE == "sweep":
                st2["ep_v"], st2["ep_t"] = sV.n, sT.n
                continue
            gc = sc.post("scalar", lambda sy, g=g: sy.copy(
                gpb[0:64, 0:GT], gpre_sb[0:64, g * GT:(g + 1) * GT]),
                waits=[(sV, st2["of_ev"])], inc=sC)
            gmul = st2["of_ev"]
            gm_t = 0
            for f8 in range(8):
                half = f8 % 2
                gmm = mm(psW2[:, half * GT:(half + 1) * GT],
                         wog2_sb[0:64, f8 * 128:(f8 + 1) * 128], gpb[0:64, 0:GT],
                         True, True, waits=[(sC, gc), (sV, gmul)] if f8 < 2 else
                         [(sV, gmul)], inc=sT)
                gsig = sc.post("scalar", lambda sy, half=half: sy.activation(
                    gbuf[:, 0:GT], psW2[:, half * GT:(half + 1) * GT], AF.Sigmoid),
                    waits=[(sT, gmm)], inc=sC)
                gmul = sc.post("vector", lambda v, f8=f8: v.tensor_mul(
                    ofin[:, f8 * GT:(f8 + 1) * GT], ofin[:, f8 * GT:(f8 + 1) * GT],
                    gbuf[:, 0:GT]), waits=[(sC, gsig)], inc=sV)
            if P2_STAGE == "gate":
                st2["ep_v"], st2["ep_t"] = sV.n, sT.n
                continue
            sq2 = sc.post("scalar", lambda sy: sy.activation(bbuf[:, :], ofin[:, :],
                                                             AF.Square),
                          waits=[(sV, gmul)], inc=sC)
            cp2 = sc.post("scalar", lambda sy: sy.copy(ogb[:, :], ofin[:, :]), inc=sC)
            stv = None
            for f8 in range(8):
                mm(psU2[0:65, 0:GT], sel64[:, 0:65], bbuf[:, f8 * GT:(f8 + 1) * GT],
                   f8 == 0, False, waits=[(sC, sq2), (sV, gmul)] if f8 == 0 else ())
            for f8 in range(8):
                stv = mm(psU2[0:65, 0:GT], ones65[:, 0:65], ogb[:, f8 * GT:(f8 + 1) * GT],
                         False, f8 == 7, waits=[(sC, cp2)] if f8 == 0 else (),
                         inc=sT if f8 == 7 else None)
            v1 = sc.post("vector", lambda v: v.tensor_scalar_mul(
                r2a[0:1, 0:GT], psU2[0:1, 0:GT], 1.0 / D), waits=[(sT, stv)], inc=sV)
            sc.post("vector", lambda v: v.tensor_copy(mur2[0:1, 0:GT], r2a[0:1, 0:GT]))
            sc.post("vector", lambda v: v.tensor_scalar_mul(
                r2b[0:1, 0:GT], psU2[64:65, 0:GT], 1.0 / D))
            sc.post("vector", lambda v: v.tensor_mul(
                r2a[0:1, 0:GT], r2a[0:1, 0:GT], r2a[0:1, 0:GT]))
            v4 = sc.post("vector", lambda v: v.tensor_sub(
                r2b[0:1, 0:GT], r2b[0:1, 0:GT], r2a[0:1, 0:GT]), inc=sV)
            c1 = sc.post("scalar", lambda sy: sy.activation(
                r2a[0:1, 0:GT], r2b[0:1, 0:GT], AF.Ln, bias=EPS), waits=[(sV, v4)], inc=sC)
            v5 = sc.post("scalar", lambda sy: sy.activation(
                mur2[0:1, GT:2 * GT], r2a[0:1, 0:GT], AF.Exp, scale=-0.5), inc=sC)
            bc = None
            for hh in range(2):
                bc = mm(psW2[:, hh * GT:(hh + 1) * GT], ones1[0:1, :],
                        mur2[0:1, hh * GT:(hh + 1) * GT], True, True,
                        waits=[(sC, v5), (sV, v4)] if hh == 0 else (),
                        inc=sT if hh == 1 else None)
            onv = None
            for f8 in range(8):
                sl = slice(f8 * GT, (f8 + 1) * GT)
                sc.post("vector", lambda v, sl=sl: v.tensor_sub(
                    eb2[:, 0:GT], ofin[:, sl], psW2[:, 0:GT]),
                    waits=[(sT, bc)] if f8 == 0 else ())
                onv = sc.post("vector", lambda v, sl=sl: v.tensor_mul(
                    ogb[:, sl], eb2[:, 0:GT], psW2[:, GT:2 * GT]), inc=sV)
            if P2_STAGE == "stats":
                st2["ep_v"], st2["ep_t"] = sV.n, sT.n
                continue
            prev_ev = 0
            x1v = None
            for cc in range(8):
                mv = None
                for kk in range(8):
                    mv = mm(psO2[:, 0:GT],
                            wo_sb[:, cc * 1024 + kk * 128: cc * 1024 + (kk + 1) * 128],
                            ogb[:, kk * GT:(kk + 1) * GT], kk == 0, kk == 7,
                            waits=([(sV, onv), (sA, st2["xg_dma"])] if cc == 0 else
                                   [(sV, prev_ev)]) if kk == 0 else (),
                            inc=sT if kk == 7 else None)
                x1v = sc.post("vector", lambda v, cc=cc: v.tensor_add(
                    x1g[:, cc * GT:(cc + 1) * GT], psO2[:, 0:GT],
                    xg_sb[:, cc * GT:(cc + 1) * GT]), waits=[(sT, mv)], inc=sV)
                prev_ev = x1v
            st2["x1_dma"] = sc.dma("sync", lambda sy, g=g: sy.dma_start(
                x1_d[:, :, g * GT:(g + 1) * GT].rearrange("f p t -> p f t"),
                x1g[:, :].rearrange("p (f t) -> p f t", f=8)),
                sA, waits=[(sV, x1v)])
            if "x1" in taps:
                tap("x1", (NG, 128, 8 * GT), F32)
                st8["tap_dma"] = sc.dma("sync", lambda sy, g=g: sy.dma_start(
                    tap_d["x1"][g, :, :], x1g[:, :]), sA, waits=[(sV, x1v)])
            if P2_STAGE == "wo":
                st2["ep_v"], st2["ep_t"] = sV.n, sT.n
                continue
            sq3 = sc.post("scalar", lambda sy: sy.activation(bbuf[:, :], x1g[:, :],
                                                             AF.Square),
                          waits=[(sV, x1v)], inc=sC)
            cp3 = sc.post("scalar", lambda sy: sy.copy(ogb[:, :], x1g[:, :]), inc=sC)
            stv = None
            for f8 in range(8):
                mm(psU2[0:65, 0:GT], sel64[:, 0:65], bbuf[:, f8 * GT:(f8 + 1) * GT],
                   f8 == 0, False, waits=[(sC, sq3)] if f8 == 0 else ())
            for f8 in range(8):
                stv = mm(psU2[0:65, 0:GT], ones65[:, 0:65], ogb[:, f8 * GT:(f8 + 1) * GT],
                         False, f8 == 7, waits=[(sC, cp3)] if f8 == 0 else (),
                         inc=sT if f8 == 7 else None)
            v1 = sc.post("vector", lambda v: v.tensor_scalar_mul(
                r2a[0:1, 0:GT], psU2[0:1, 0:GT], 1.0 / D), waits=[(sT, stv)], inc=sV)
            sc.post("vector", lambda v: v.tensor_copy(mur2[0:1, 0:GT], r2a[0:1, 0:GT]))
            sc.post("vector", lambda v: v.tensor_scalar_mul(
                r2b[0:1, 0:GT], psU2[64:65, 0:GT], 1.0 / D))
            sc.post("vector", lambda v: v.tensor_mul(
                r2a[0:1, 0:GT], r2a[0:1, 0:GT], r2a[0:1, 0:GT]))
            v4 = sc.post("vector", lambda v: v.tensor_sub(
                r2b[0:1, 0:GT], r2b[0:1, 0:GT], r2a[0:1, 0:GT]), inc=sV)
            c1 = sc.post("scalar", lambda sy: sy.activation(
                r2a[0:1, 0:GT], r2b[0:1, 0:GT], AF.Ln, bias=EPS), waits=[(sV, v4)], inc=sC)
            v5 = sc.post("scalar", lambda sy: sy.activation(
                mur2[0:1, GT:2 * GT], r2a[0:1, 0:GT], AF.Exp, scale=-0.5), inc=sC)
            bc = None
            for hh in range(2):
                bc = mm(psW2[:, hh * GT:(hh + 1) * GT], ones1[0:1, :],
                        mur2[0:1, hh * GT:(hh + 1) * GT], True, True,
                        waits=[(sC, v5), (sV, v4)] if hh == 0 else (),
                        inc=sT if hh == 1 else None)
            lnv = None
            for f8 in range(8):
                sl = slice(f8 * GT, (f8 + 1) * GT)
                sc.post("vector", lambda v, sl=sl: v.tensor_sub(
                    eb2[:, 0:GT], x1g[:, sl], psW2[:, 0:GT]),
                    waits=[(sT, bc)] if f8 == 0 else ())
                lnv = sc.post("vector", lambda v, f8=f8, g=g: v.tensor_mul(
                    x1ln_sb[:, f8 * TC + g * GT: f8 * TC + (g + 1) * GT],
                    eb2[:, 0:GT], psW2[:, GT:2 * GT]), inc=sV)
            st2["ep_v"] = lnv
            st2["ep_t"] = sT.n
        if "x1ln" in taps:
            tap("x1ln", (128, 8 * TC), BF16)
            st8["tap_dma"] = sc.dma("sync", lambda sy: sy.dma_start(
                tap_d["x1ln"][:, :], x1ln_sb[:, :]), sA, waits=[(sV, st2["ep_v"])])
        p2_end_v, p2_end_t, p2_end_c = sV.n, sT.n, sC.n
        p2stack.close()
        p12stack.close()

    if "p3" in phases:
        p3stack = contextlib.ExitStack()

        def sb3(name, shape, dtype):
            return p3stack.enter_context(nc.sbuf_tensor(name, list(shape), dtype))

        def ps3_(name, shape, dtype=F32):
            return p3stack.enter_context(nc.psum_tensor(name, list(shape), dtype))

        HT = TC // 2
        NS = HT // GT
        g1b = sb3("g1b", (128, 32 * HT), BF16)
        gHb = sb3("gHb", (128, 32 * HT), BF16)
        wsa = [sb3(f"wsa{i}", (128, 4096), BF16) for i in range(2)]
        wsb_ = [sb3(f"wsb{i}", (128, 2048), BF16) for i in range(2)]
        x1r = [sb3(f"x1r{i}", (128, GT), F32) for i in range(2)]
        outbd = [sb3(f"outb{i}", (128, GT), F32) for i in range(2)]
        sc1_sb = sb3("sc1_sb", (128, 32), F32)
        sc2_sb = sb3("sc2_sb", (128, 32), F32)
        sch1_sb = sb3("sch1_sb", (128, 32), F32)
        sch2_sb = sb3("sch2_sb", (128, 32), F32)
        sc3_sb = sb3("sc3_sb", (128, 8), F32)
        teb = sb3("teb", (128, GT), F32)
        pp = [ps3_(f"pp{i}", (128, 512), F32) for i in range(4)]

        p2v = p2_end_v if "p2" in phases else sV.n
        first = [(sV, p2v)]
        for nm, t in [("s1", sc1_sb), ("s2", sc2_sb), ("sh1", sch1_sb),
                      ("sh2", sch2_sb), ("s3", sc3_sb)]:
            sc.dma("sync", lambda sy, t=t, nm=nm: sy.dma_start(t[:, :], din[nm][:, :]),
                   sA, waits=first)
            first = []
        sc_ld = sA.n
        st3 = {"wd": [0, 0], "wd2": [0, 0], "pev": [0, 0, 0, 0], "pevC": [0, 0, 0, 0],
               "out_dma": 0, "rfree": [0, 0], "ob_dma": [0, 0],
               "bar_v": (p2_end_v if "p2" in phases else sV.n),
               "bar_c": (p2_end_c if "p2" in phases else sC.n)}

        def halves():
            return range(P3_HALVES)

        for h in halves():
            th = h * HT
            sc.mark(f"p3A{h}")
            for c in range(32):
                par = c % 2
                w1l = sc.dma("gpsimd", lambda g, c=c, par=par: g.dma_start(
                    wsb_[par][:, 0:1024], din["w1"][:, c * 1024:(c + 1) * 1024]),
                    sW, waits=[(sT, st3["wd"][par])] + ([(sV, p2v)] if (h == 0 and c < 2) else []))
                w2l = sc.dma("gpsimd", lambda g, c=c, par=par: g.dma_start(
                    wsb_[par][:, 1024:2048], din["w2"][:, c * 1024:(c + 1) * 1024]),
                    sW, waits=[])
                for sp in range(NS):
                    pa, pb = pp[2 * par], pp[2 * par + 1]
                    tsl = slice(th + sp * GT, th + (sp + 1) * GT)
                    mv1 = mv2 = None
                    for kk in range(8):
                        w0 = []
                        if kk == 0:
                            w0 = [(sW, w2l), (sC, st3["pevC"][2 * par]),
                                  (sV, st3["pev"][2 * par]), (sA, sc_ld),
                                  (sV, st3["bar_v"]), (sC, st3["bar_c"])]
                        mv1 = mm(pa[:, 0:GT], wsb_[par][:, kk * 128: kk * 128 + 128],
                                 x1ln_sb[:, kk * TC + tsl.start: kk * TC + tsl.stop],
                                 kk == 0, kk == 7, waits=w0,
                                 inc=sT if kk == 7 else None)
                    for kk in range(8):
                        mv2 = mm(pb[:, 0:GT], wsb_[par][:, 1024 + kk * 128: 1024 + kk * 128 + 128],
                                 x1ln_sb[:, kk * TC + tsl.start: kk * TC + tsl.stop],
                                 kk == 0, kk == 7,
                                 waits=[(sC, st3["pevC"][2 * par + 1]),
                                        (sV, st3["pev"][2 * par + 1])] if kk == 0 else (),
                                 inc=sT if kk == 7 else None)
                    e1 = sc.post("scalar", lambda sy, pa=pa, c=c: sy.activation(
                        teb[:, 0:GT], pa[:, 0:GT], AF.Silu, scale=sc1_sb[:, c:c + 1]),
                        waits=[(sT, mv1)], inc=sC)
                    st3["pevC"][2 * par] = e1
                    gv = sc.post("vector", lambda v, pb=pb, c=c, sp=sp, h=h: v.scalar_tensor_tensor(
                        g1b[:, c * HT + sp * GT: c * HT + (sp + 1) * GT],
                        pb[:, 0:GT], sc2_sb[:, c:c + 1], teb[:, 0:GT],
                        ALU.mult, ALU.mult), waits=[(sT, mv2), (sC, e1)], inc=sV)
                    st3["pev"][2 * par + 1] = gv
                st3["wd"][par] = sT.n
            st3["bar_v"], st3["bar_c"] = sV.n, sC.n
            for (stage, wnm, ssb, srcb, dstb) in [("B", "h1", sch1_sb, g1b, gHb),
                                                  ("C", "h2", sch2_sb, gHb, g1b)]:
                sc.mark(f"p3{stage}{h}")
                for c in range(32):
                    par = c % 2
                    wl = sc.dma("gpsimd", lambda g, c=c, par=par, wnm=wnm: g.dma_start(
                        wsa[par][:, :], din[wnm][:, c * 4096:(c + 1) * 4096]),
                        sW, waits=[(sT, st3["wd2"][par])])
                    pa, pb = pp[2 * par], pp[2 * par + 1]
                    mv = None
                    for kk in range(32):
                        w0 = ()
                        if kk == 0:
                            w0 = [(sW, wl), (sC, st3["pevC"][2 * par]),
                                  (sC, st3["pevC"][2 * par + 1]),
                                  (sV, st3["bar_v"]), (sC, st3["bar_c"])]
                        for sp in range(NS):
                            mv = mm(pa[:, 0:GT] if sp == 0 else pb[:, 0:GT],
                                    wsa[par][:, kk * 128:(kk + 1) * 128],
                                    srcb[:, kk * HT + sp * GT: kk * HT + (sp + 1) * GT],
                                    kk == 0, kk == 31, waits=w0 if sp == 0 else (),
                                    inc=sT if (kk == 31 and sp == NS - 1) else None)
                            w0 = ()
                    for sp in range(NS):
                        px = pa if sp == 0 else pb
                        ge = sc.post("scalar", lambda sy, px=px, c=c, sp=sp, ssb=ssb,
                                     dstb=dstb: sy.activation(
                            dstb[:, c * HT + sp * GT: c * HT + (sp + 1) * GT],
                            px[:, 0:GT], AF.Gelu, scale=ssb[:, c:c + 1]),
                            waits=[(sT, mv)] if sp == 0 else (), inc=sC)
                        st3["pevC"][2 * par + sp] = ge
                    st3["wd2"][par] = sT.n
                st3["bar_v"], st3["bar_c"] = sV.n, sC.n
            sc.mark(f"p3D{h}")
            for c in range(8):
                par = c % 2
                w3l = sc.dma("gpsimd", lambda g, c=c, par=par: g.dma_start(
                    wsa[par][:, :], din["w3"][:, c * 4096:(c + 1) * 4096]),
                    sW, waits=[(sT, st3["wd2"][par])])
                x1ls = []
                for sp in range(NS):
                    rp = sp
                    x1ls.append(sc.dma("sync", lambda sy, c=c, sp=sp, th=th, rp=rp:
                                       sy.dma_start(
                        x1r[rp][:, :], x1_d[c, :, th + sp * GT: th + (sp + 1) * GT]),
                        sA, waits=[(sV, st3["rfree"][rp])]))
                pa, pb = pp[2 * par], pp[2 * par + 1]
                mv = None
                for kk in range(32):
                    w0 = ()
                    if kk == 0:
                        w0 = [(sW, w3l), (sV, st3["pev"][2 * par]),
                              (sV, st3["pev"][2 * par + 1]),
                              (sV, st3["bar_v"]), (sC, st3["bar_c"])]
                    for sp in range(NS):
                        mv = mm(pa[:, 0:GT] if sp == 0 else pb[:, 0:GT],
                                wsa[par][:, kk * 128:(kk + 1) * 128],
                                g1b[:, kk * HT + sp * GT: kk * HT + (sp + 1) * GT],
                                kk == 0, kk == 31, waits=w0 if sp == 0 else (),
                                inc=sT if (kk == 31 and sp == NS - 1) else None)
                        w0 = ()
                for sp in range(NS):
                    px = pa if sp == 0 else pb
                    rp = sp
                    op = sp
                    ov = sc.post("vector", lambda v, px=px, c=c, rp=rp, op=op:
                                 v.scalar_tensor_tensor(
                        outbd[op][:, 0:GT], px[:, 0:GT], sc3_sb[:, c:c + 1],
                        x1r[rp][:, :], ALU.mult, ALU.add),
                        waits=[(sT, mv), (sA, x1ls[sp]), (sS, st3["ob_dma"][op])], inc=sV)
                    st3["pev"][2 * par + sp] = ov
                    st3["rfree"][rp] = ov
                    st3["ob_dma"][op] = sc.dma("scalar", lambda sy, c=c, sp=sp, th=th,
                                               op=op: sy.dma_start(
                        out_d[c, :, th + sp * GT: th + (sp + 1) * GT], outbd[op][:, 0:GT]),
                        sS, waits=[(sV, ov)])
                st3["wd2"][par] = sT.n
            st3["wd2"][0] = st3["wd2"][1] = sT.n
            st3["bar_v"], st3["bar_c"] = sV.n, sC.n
        sc.mark(None)
        p3stack.close()
    else:
        pass


    if "send" in taps:
        tap("send", (128, NP * 64), F32)
        sbuf_send = sb("sbuf_send", (128, NP * 64), F32)
        cvv = sc.post("vector", lambda v: v.tensor_copy(sbuf_send[:, :], scur[:, :]),
                      waits=[(sV, st8["prev_group_v"])], inc=sV)
        sc.dma("sync", lambda s: s.dma_start(tap_d["send"][:, :], sbuf_send[:, :]),
               sA, waits=[(sV, cvv)])

    sc.post("sync", None, waits=[(sA, sA.n), (sW, sW.n), (sS, sS.n)])

    with nc.Block() as block:
        for s in sc.sems.values():
            s.h = stack.enter_context(nc.semaphore(s.name))

        @block.sync
        def _(e):
            sc.replay("sync", e, nc)

        @block.tensor
        def _(e):
            sc.replay("tensor", e, nc)

        @block.vector
        def _(e):
            sc.replay("vector", e, nc)

        @block.scalar
        def _(e):
            sc.replay("scalar", e, nc)

        @block.gpsimd
        def _(e):
            sc.replay("gpsimd", e, nc)

    stack.close()
    return nc, tap_d



def kernel(**inputs):
    nc, tap_d = build()
    in_maps = [_prep_inputs(i, inputs) for i in range(NCORE)]
    res = run_bass_kernel_spmd(nc, in_maps, core_ids=list(range(NCORE)))
    out = np.empty((B, T, D), np.float32)
    for i in range(NCORE):
        b, half = i // 2, i % 2
        o = res.results[i]["out"].reshape(8 * 128, TC)
        out[b, half * TC:(half + 1) * TC, :] = o.T
    return out


def run_taps(inputs, taps, phases=None, trace=False):
    nc, tap_d = build(taps=set(taps), phases=set(phases or PHASES))
    in_maps = [_prep_inputs(i, inputs) for i in range(NCORE)]
    res = run_bass_kernel_spmd(nc, in_maps, core_ids=list(range(NCORE)), trace=trace)
    return res


# version 08
# speedup vs baseline: 1.0153x; 1.0153x over previous
import contextlib
import numpy as np
import ml_dtypes
import concourse.bass as bass
import concourse.mybir as mybir
from concourse.ap import AP
from concourse.bass_utils import run_bass_kernel_spmd

F32 = mybir.dt.float32
F16 = mybir.dt.float16
BF16 = mybir.dt.bfloat16
AF = mybir.ActivationFunctionType
ALU = mybir.AluOpType

B, T, D = 4, 4096, 1024
H, HD = 16, 64
DH = 4096
NCORE = 8
TC = 2048
NG = 4
GT = 512
C = 64
NCH = GT // C
NP = 8
EPS = 1e-5

TAPS = set()
PHASES = {"p1", "mix", "coll", "p2", "p3"}
MIX_GROUPS = NG
MIX_PAIRS = NP
MIX_STAGE = "all"
MIX_SOLVE_K = 6
MIX_SOLVE_SUB = "full"
DBG_J = 0
P3_HALVES = 2
P2_STAGE = "all"
P2_SWEEP_MM = 3



def _lhsT_blocks(w_in_out, kchunks, cchunks):
    a = w_in_out.reshape(kchunks, 128, cchunks, 128).transpose(1, 2, 0, 3)
    return np.ascontiguousarray(a.reshape(128, cchunks * kchunks * 128))


def _consts():
    c = {}
    c["c_ones1c"] = np.ones((128, 1), np.float32)
    s64 = np.zeros((128, 65), np.float32); s64[:, 64] = 1.0
    c["c_sel64"] = s64
    o65 = np.zeros((128, 65), np.float32); o65[:, 0] = 1.0
    c["c_ones65"] = o65
    sh = np.zeros((128, 65), np.float32)
    sh[:64, 0] = 1.0; sh[64:, 64] = 1.0
    c["c_selH65"] = sh
    c["c_ones1"] = np.ones((1, 128), np.float32)
    hA = np.zeros((1, 128), np.float32); hA[0, :64] = 1.0
    hB = np.zeros((1, 128), np.float32); hB[0, 64:] = 1.0
    c["c_hA"], c["c_hB"] = hA, hB
    sp = np.zeros((16, NP * 128), np.float32)
    for f in range(NP):
        sp[2 * f, f * 128: f * 128 + 64] = -1.0
        sp[2 * f + 1, f * 128 + 64: f * 128 + 128] = -1.0
    c["c_selPair"] = sp
    s = np.arange(64)
    mU = (s[:, None] < s[None, :]).astype(np.float32)
    mI = (s[:, None] <= s[None, :]).astype(np.float32)
    mL = (s[None, :] < s[:, None]).astype(np.float32)
    c["c_maskU8"] = np.tile(np.tile(mU, (2, 1)), (1, NCH))
    c["c_maskI8"] = np.tile(np.tile(mI, (2, 1)), (1, NCH))
    c["c_maskL8"] = np.tile(np.tile(mL, (2, 1)), (1, NCH))
    c["c_ident2"] = np.tile(np.eye(64, dtype=np.float32), (2, 1))
    c["c_ident"] = np.eye(128, dtype=np.float32)
    c["c_eps"] = np.full((128, 1), EPS, np.float32)
    c["c_id2b"] = np.tile(np.eye(64, dtype=np.float32), (2, 1))
    return c


def _prep_inputs(core, inp):
    b, half = core // 2, core % 2
    t0 = half * TC
    d = {}
    xs = np.asarray(inp["L0_attn_x"])[b, t0:t0 + TC, :]
    d["xfm"] = np.ascontiguousarray(xs.T.reshape(8, 128, TC))
    d["wq"] = _lhsT_blocks(np.asarray(inp["L0_attn_Wq"]), 8, 8)
    d["wk"] = _lhsT_blocks(np.asarray(inp["L0_attn_Wk"]), 8, 8)
    d["wv"] = _lhsT_blocks(np.asarray(inp["L0_attn_Wv"]), 8, 8)
    d["wo"] = _lhsT_blocks(np.asarray(inp["L0_attn_Wo"]), 8, 8)
    d["wf1"] = np.ascontiguousarray(np.asarray(inp["L0_attn_Wf1"]).reshape(8, 128, 64)
                                    .transpose(1, 0, 2).reshape(128, 512))
    d["wf2"] = np.ascontiguousarray(np.asarray(inp["L0_attn_Wf2"]))
    d["wog1"] = np.ascontiguousarray(np.asarray(inp["L0_attn_Wog1"]).reshape(8, 128, 64)
                                     .transpose(1, 0, 2).reshape(128, 512))
    d["wog2"] = np.ascontiguousarray(np.asarray(inp["L0_attn_Wog2"]))
    d["wgam"] = np.ascontiguousarray(np.asarray(inp["L0_attn_Wgamma"]).reshape(8, 128, 16)
                                     .transpose(1, 0, 2).reshape(128, 128))
    d["nw"] = np.ascontiguousarray(np.asarray(inp["L0_attn_norm_w"]).reshape(8, 128).T)
    d["w1"] = _lhsT_blocks(np.asarray(inp["L0_ffn_w1_w"]).T, 8, 32)
    d["w2"] = _lhsT_blocks(np.asarray(inp["L0_ffn_w2_w"]).T, 8, 32)
    d["h1"] = _lhsT_blocks(np.asarray(inp["L0_ffn_h1_w"]).T, 32, 32)
    d["h2"] = _lhsT_blocks(np.asarray(inp["L0_ffn_h2_w"]).T, 32, 32)
    d["w3"] = _lhsT_blocks(np.asarray(inp["L0_ffn_w3_w"]).T, 32, 8)
    d["s1"] = np.ascontiguousarray(np.asarray(inp["L0_ffn_w1_s"]).reshape(32, 128).T)
    d["s2"] = np.ascontiguousarray(np.asarray(inp["L0_ffn_w2_s"]).reshape(32, 128).T)
    d["sh1"] = np.ascontiguousarray(np.asarray(inp["L0_ffn_h1_s"]).reshape(32, 128).T)
    d["sh2"] = np.ascontiguousarray(np.asarray(inp["L0_ffn_h2_s"]).reshape(32, 128).T)
    d["s3"] = np.ascontiguousarray(np.asarray(inp["L0_ffn_w3_s"]).reshape(8, 128).T)
    sel = np.zeros((128, NCORE), np.float32)
    if half == 1:
        sel[:, core - 1] = 1.0
    d["sel"] = sel
    d.update(_consts())
    return d



ENGINES = ("tensor", "vector", "scalar", "gpsimd", "sync")


class _Sem:
    __slots__ = ("name", "n", "h")

    def __init__(self, name):
        self.name, self.n, self.h = name, 0, None


class Sched:
    def __init__(self):
        self.streams = {e: [] for e in ENGINES}
        self.sems = {}

    def sem(self, name):
        if name not in self.sems:
            self.sems[name] = _Sem(name)
        return self.sems[name]

    def dma(self, eng, fn, qsem, waits=()):
        w = tuple(waits) + ((qsem, qsem.n),)
        return self.post(eng, fn, waits=w, inc=qsem, amt=16)

    def post(self, eng, fn, waits=(), inc=None, amt=1):
        val = None
        if inc is not None:
            inc.n += amt
            val = inc.n
        self.streams[eng].append((tuple(waits), fn, inc, amt))
        return val

    def mark(self, name):
        for e in ENGINES:
            self.streams[e].append(("MARK", name, None, None))

    def replay(self, eng_name, eng, nc=None):
        import contextlib as _ctx
        cur = None
        for entry in self.streams[eng_name]:
            if entry[0] == "MARK":
                if cur is not None and nc is not None:
                    nc.leave_named_scope(cur[0], cur[1], False)
                    cur = None
                if nc is not None and entry[1] is not None:
                    sid, _ = nc.enter_named_scope(entry[1], False)
                    cur = (entry[1], sid)
                continue
            waits, fn, inc, amt = entry
            for (s, v) in waits:
                if v > 0:
                    eng.wait_ge(s.h, v)
            inst = fn(eng) if fn is not None else None
            if inc is not None:
                if inst is None:
                    inst = eng.nop()
                inst.then_inc(inc.h, amt)
        if cur is not None and nc is not None:
            nc.leave_named_scope(cur[0], cur[1], False)


def _sv(tile, width, count, stride, offset=0):
    base = tile[:, 0:1]
    return AP(base.tensor, base.offset + offset,
              [list(base.ap[0]), [stride, count], [1, width]])



def build(taps=TAPS, phases=PHASES):
    nc = bass.Bass("TRN2", target_bir_lowering=False)
    sc = Sched()
    sA = sc.sem("sA")
    sW = sc.sem("sW")
    sT = sc.sem("sT")
    sV = sc.sem("sV")
    sC = sc.sem("sC")
    sG = sc.sem("sG")
    sS = sc.sem("sS")

    din = {}

    def dram_in(name, shape):
        din[name] = nc.dram_tensor(name, list(shape), F32, kind="ExternalInput").ap()
        return din[name]

    x_d = dram_in("xfm", (8, 128, TC))
    for nm, shp in [("wq", (128, 8192)), ("wk", (128, 8192)), ("wv", (128, 8192)),
                    ("wo", (128, 8192)), ("wf1", (128, 512)), ("wf2", (64, 1024)),
                    ("wog1", (128, 512)), ("wog2", (64, 1024)), ("wgam", (128, 128)),
                    ("nw", (128, 8)),
                    ("w1", (128, 32768)), ("w2", (128, 32768)),
                    ("h1", (128, 131072)), ("h2", (128, 131072)), ("w3", (128, 32768)),
                    ("s1", (128, 32)), ("s2", (128, 32)), ("sh1", (128, 32)),
                    ("sh2", (128, 32)), ("s3", (128, 8)), ("sel", (128, 8)),
                    ("c_ones1c", (128, 1)), ("c_sel64", (128, 65)), ("c_ones65", (128, 65)), ("c_selH65", (128, 65)),
                    ("c_ones1", (1, 128)), ("c_hA", (1, 128)), ("c_hB", (1, 128)),
                    ("c_selPair", (16, NP * 128)),
                    ("c_maskU8", (128, 512)), ("c_maskI8", (128, 512)),
                    ("c_maskL8", (128, 512)),
                    ("c_ident2", (128, 64)), ("c_ident", (128, 128)), ("c_eps", (128, 1)), ("c_id2b", (128, 64))]:
        dram_in(nm, shp)

    out_d = nc.dram_tensor("out", [8, 128, TC], F32, kind="ExternalOutput").ap()
    tap_d = {}

    def tap(name, shape, dtype=F32):
        if name in taps and name not in tap_d:
            tap_d[name] = nc.dram_tensor("tap_" + name, list(shape), dtype,
                                         kind="ExternalOutput").ap()
        return tap_d.get(name)

    x1_d = nc.dram_tensor("x1buf", [8, 128, TC], F32).ap()
    stWT_d = nc.dram_tensor("stWT", [128, NP, TC // C, 128], BF16).ap()
    stO_d = nc.dram_tensor("stO", [NP, 128, TC], BF16).ap()
    cc_in = nc.dram_tensor("cc_in", [NP * 128, 64], F32).ap()
    cc_out = nc.dram_tensor("cc_out", [NCORE * NP * 128, 64], F32).ap()

    stack = contextlib.ExitStack()
    p12stack = contextlib.ExitStack()
    p1stack = contextlib.ExitStack()

    def sb(name, shape, dtype):
        return stack.enter_context(nc.sbuf_tensor(name, list(shape), dtype))

    def sb1(name, shape, dtype):
        return p1stack.enter_context(nc.sbuf_tensor(name, list(shape), dtype))

    def ps(name, shape, dtype=F32):
        return stack.enter_context(nc.psum_tensor(name, list(shape), dtype))

    def ps1(name, shape, dtype=F32):
        return p1stack.enter_context(nc.psum_tensor(name, list(shape), dtype))

    def sb12(name, shape, dtype):
        return p12stack.enter_context(nc.sbuf_tensor(name, list(shape), dtype))

    def ps12(name, shape, dtype=F32):
        return p12stack.enter_context(nc.psum_tensor(name, list(shape), dtype))

    cst = {}
    for nm, shp, dt in [("c_ones1c", (128, 1), BF16), ("c_sel64", (128, 65), BF16),
                        ("c_ones65", (128, 65), BF16),
                        ("c_selH65", (128, 65), BF16),
                        ("c_ones1", (1, 128), BF16), ("c_hA", (1, 128), BF16),
                        ("c_hB", (1, 128), BF16), ("c_selPair", (16, NP * 128), BF16),
                        ("c_maskU8", (128, 512), F32), ("c_maskI8", (128, 512), F32),
                        ("c_maskL8", (128, 512), F32), ("c_ident2", (128, 64), F32),
                        ("c_ident", (128, 128), BF16), ("c_eps", (128, 1), F32),
                        ("c_id2b", (128, 64), BF16)]:
        cst[nm] = sb("sb_" + nm, shp, dt)
    scur = sb("scur", (128, NP * 64), BF16)
    zeros_sb = sb("zeros_sb", (128, C), F32)
    sel_sb = sb("sel_sb", (128, 8), F32)
    x1ln_sb = sb("x1ln_sb", (128, 8 * TC), BF16)
    wo_sb = sb12("wo_sb", (128, 8192), BF16)
    wog2_sb = sb12("wog2_sb", (64, 1024), BF16)
    nw_sb = sb12("nw_sb", (128, 8), F32)
    gpre_sb = sb12("gpre_sb", (64, TC), F32)

    for nm in cst:
        sc.dma("gpsimd", lambda g, t=cst[nm], nm=nm: g.dma_start(t[:, :], din[nm][:, :]), sW)
    sc.dma("gpsimd", lambda g: g.dma_start(wo_sb[:, :], din["wo"][:, :]), sW)
    sc.dma("gpsimd", lambda g: g.dma_start(wog2_sb[:, :], din["wog2"][:, :]), sW)
    sc.dma("sync", lambda s: s.dma_start(nw_sb[:, :], din["nw"][:, :]), sA)
    sc.dma("sync", lambda s: s.dma_start(sel_sb[:, :], din["sel"][:, :]), sA)
    nc.const_aps.register(F32, EPS, cst['c_eps'][:, 0:1])
    w_base, a_base = sW.n, sA.n
    wo_ready = 0
    for k in range(8):
        ap_w = wo_sb[:, :].rearrange("p (c k m) -> p c k m", c=8, k=8)[:, :, k, :]
        wo_ready = sc.post("vector",
                           lambda vv, ap_w=ap_w, k=k: vv.tensor_scalar_mul(
                               ap_w, ap_w, nw_sb[:, k:k + 1]),
                           waits=[(sW, w_base), (sA, a_base)] if k == 0 else (), inc=sV)
    sc.post("vector", lambda v: v.memset(scur[:, :], 0.0), inc=sV)
    zv0 = sc.post("vector", lambda v: v.memset(zeros_sb[:, :], 0.0), inc=sV)

    wq_sb = sb1("wq_sb", (128, 8192), BF16)
    wk_sb = sb1("wk_sb", (128, 8192), BF16)
    wv_sb = sb1("wv_sb", (128, 8192), BF16)
    wf1_sb = sb1("wf1_sb", (128, 512), BF16)
    wf2_sb = sb1("wf2_sb", (64, 1024), BF16)
    wog1_sb = sb1("wog1_sb", (128, 512), BF16)
    wgam_sb = sb1("wgam_sb", (128, 128), BF16)
    for nm, t in [("wq", wq_sb), ("wk", wk_sb), ("wv", wv_sb), ("wf1", wf1_sb),
                  ("wf2", wf2_sb), ("wog1", wog1_sb), ("wgam", wgam_sb)]:
        sc.dma("gpsimd", lambda g, t=t, nm=nm: g.dma_start(t[:, :], din[nm][:, :]), sW)
    pw_ready = sW.n

    xgb = sb1("xgb", (128, 8 * GT), BF16)
    xln = sb1("xln", (128, 8 * GT), BF16)
    sqb = xln
    q_g = sb1("q_g", (128, 8 * GT), BF16)
    k_g = sb1("k_g", (128, 8 * GT), BF16)
    v_g = sb1("v_g", (128, 8 * GT), BF16)
    ngk_g = sb1("ngk_g", (128, 8 * GT), F16)
    f1t = sb1("f1t", (64, GT), BF16)
    gs_g = sb1("gs_g", (16, GT), BF16)
    r_m2 = sb1("r_m2", (1, GT), F32)
    r_a = sb1("r_a", (1, GT), F32)
    r_b = sb1("r_b", (1, GT), F32)
    murt = sb1("murt", (1, 2 * GT), BF16)
    ebuf = sb1("ebuf", (128, GT), F32)

    ka = sb1("ka", (128, GT), BF16)
    khat = sb1("khat", (128, GT), BF16)
    AQs = [sb1(f"AQ{i}", (128, 2 * GT), BF16) for i in range(2)]
    nL = sb1("nL", (128, GT), F32)
    eDCs = [sb1(f"eDC{i}", (128, NCH), F32) for i in range(2)]
    nLCs = [sb1(f"nLC{i}", (128, NCH), F32) for i in range(2)]
    XU = [sb1(f"XU{i}", (128, NCH * 192), BF16) for i in range(2)]
    Tt = [sb1(f"Tt{i}", (128, GT), BF16) for i in range(2)]
    LK = sb1("LK", (128, 2 * GT), BF16)
    T0m = sb1("T0m", (128, GT), BF16)
    ktmp = sb1("ktmp", (128, GT), BF16)
    J_g = sb1("J_g", (128, GT), BF16)
    WTst = sb1("WTst", (128, NCH * 128), BF16)
    Ost = ktmp

    psA = ps1("psA", (128, 512), F32)
    psB = ps1("psB", (128, 512), F32)
    psW_ = ps1("psW", (128, 1024), F32)
    psT4 = ps1("psT4", (128, 1024), BF16)
    psX = ps1("psX", (128, 1024), F32)
    psU = ps1("psU", (128, 512), F32)

    ones1c = cst["c_ones1c"]; sel64 = cst["c_sel64"]; selH65 = cst["c_selH65"]
    ones65 = cst["c_ones65"]
    ones1 = cst["c_ones1"]; hA = cst["c_hA"]; hB = cst["c_hB"]
    selPair = cst["c_selPair"]
    maskU8 = cst["c_maskU8"]; maskI8 = cst["c_maskI8"]; maskL8 = cst["c_maskL8"]
    ident2 = cst["c_ident2"]; ident = cst["c_ident"]; id2b = cst["c_id2b"]

    def mm(psl, lhsT, rhs, start, stop, waits=(), inc=None, tp=None, sgc=False):
        return sc.post("tensor",
                       lambda t, psl=psl, lhsT=lhsT, rhs=rhs, start=start, stop=stop, tp=tp,
                       sgc=sgc: t.matmul(psl, lhsT, rhs, start=start, stop=stop,
                                         tile_position=tp, skip_group_check=sgc),
                       waits=waits, inc=inc)

    QUAD = ((0, 64, (0, 0)), (64, 128, (64, 64)))

    st8 = {"prev_group_v": 0, "stWT_dma": 0, "stO_dma": 0, "tap_dma": 0, "psT4_free": 0}

    def mixer_prep(g, f, prev):
        fsl = slice(f * GT, (f + 1) * GT)
        aq = AQs[f % 2]
        edc = eDCs[f % 2]
        nlc = nLCs[f % 2]
        P = {"aq": aq, "edc": edc}
        gb = mm(psW_[:, 0:GT], selPair[0:16, f * 128:(f + 1) * 128], gs_g[0:16, 0:GT],
                start=True, stop=True, waits=[(sV, prev["psw_free"])], inc=sT)
        kq = sc.post("scalar", lambda s, fsl=fsl: s.activation(sqb[:, 0:GT], k_g[:, fsl],
                                                               AF.Square), inc=sC)
        ssv = mm(psU[0:65, 0:GT], selH65[:, 0:65], sqb[:, 0:GT], start=True, stop=True,
                 waits=[(sC, kq), (sV, prev["psu_free"])], inc=sT)
        v6 = sc.post("vector", lambda v: v.tensor_scalar_max(r_a[0:1, 0:GT], psU[0:1, 0:GT],
                                                             1e-24), waits=[(sT, ssv)], inc=sV)
        v7 = sc.post("vector", lambda v: v.tensor_scalar_max(r_b[0:1, 0:GT], psU[64:65, 0:GT],
                                                             1e-24), inc=sV)
        P["v7"] = v7
        c2a = sc.post("scalar", lambda s: s.activation(r_a[0:1, 0:GT], r_a[0:1, 0:GT], AF.Ln),
                      waits=[(sV, v7)], inc=sC)
        sc.post("scalar", lambda s: s.activation(r_b[0:1, 0:GT], r_b[0:1, 0:GT], AF.Ln))
        sc.post("scalar", lambda s: s.activation(murt[0:1, 0:GT], r_a[0:1, 0:GT], AF.Exp,
                                                 scale=-0.5))
        c2d = sc.post("scalar", lambda s: s.activation(murt[0:1, GT:2 * GT], r_b[0:1, 0:GT],
                                                       AF.Exp, scale=-0.5), inc=sC)
        rb0 = mm(psW_[:, GT:2 * GT], hA[0:1, :], murt[0:1, 0:GT], start=True, stop=False,
                 waits=[(sC, c2d)])
        rb = mm(psW_[:, GT:2 * GT], hB[0:1, :], murt[0:1, GT:2 * GT], start=False, stop=True,
                inc=sT)
        v8 = sc.post("vector", lambda v, fsl=fsl: v.tensor_mul(k_g[:, fsl], k_g[:, fsl],
                                                               psW_[:, GT:2 * GT]),
                     waits=[(sT, rb)], inc=sV)
        v9 = sc.post("vector", lambda v, fsl=fsl: v.tensor_mul(ka[:, 0:GT], k_g[:, fsl],
                                                               psW_[:, 0:GT]), inc=sV)
        P["v9"] = v9
        scv = None
        for j in range(NCH):
            sl = slice(j * C, (j + 1) * C)
            scv = sc.post("vector", lambda v, sl=sl, f=f: v.tensor_tensor_scan(
                nL[:, sl], ngk_g[:, f * GT + sl.start: f * GT + sl.stop],
                zeros_sb[:, 0:C], 0.0, ALU.add, ALU.add),
                waits=[(sV, zv0)] if (g == 0 and f == 0 and j == 0) else (), inc=sV)
        vg = sc.post("vector", lambda v, nlc=nlc: v.tensor_copy(
            nlc[:, 0:NCH], _sv(nL, 1, NCH, C, C - 1)), waits=[(sV, scv)], inc=sV)
        ce = sc.post("scalar", lambda s, nlc=nlc, edc=edc: s.activation(
            edc[:, 0:NCH], nlc[:, 0:NCH], AF.Exp, scale=-1.0), waits=[(sV, vg)], inc=sC)
        P["ce"] = ce
        e1 = sc.post("scalar", lambda s: s.activation(ebuf[:, 0:GT], nL[:, 0:GT], AF.Exp),
                     waits=[(sV, scv)], inc=sC)
        vKH = sc.post("vector", lambda v, fsl=fsl: v.tensor_mul(khat[:, 0:GT], k_g[:, fsl],
                                                                ebuf[:, 0:GT]),
                      waits=[(sC, e1), (sV, v9)], inc=sV)
        e2 = sc.post("scalar", lambda s: s.activation(ebuf[:, 0:GT], nL[:, 0:GT], AF.Exp,
                                                      scale=-1.0), waits=[(sV, vKH)], inc=sC)
        vQH = sc.post("vector", lambda v, aq=aq, f=f: v.tensor_mul(
            _sv(aq, C, NCH, 2 * C, C), _sv(q_g, C, NCH, C, f * GT),
            _sv(ebuf, C, NCH, C)), waits=[(sC, e2)], inc=sV)
        vt = sc.post("vector", lambda v, fsl=fsl: v.tensor_add(nL[:, 0:GT], nL[:, 0:GT],
                                                               ngk_g[:, fsl]),
                     waits=[(sV, vQH)], inc=sV)
        e3 = sc.post("scalar", lambda s: s.activation(ebuf[:, 0:GT], nL[:, 0:GT], AF.Exp,
                                                      scale=-1.0), waits=[(sV, vt)], inc=sC)
        vAH = sc.post("vector", lambda v, aq=aq: v.tensor_mul(
            _sv(aq, C, NCH, 2 * C, 0), _sv(ka, C, NCH, C),
            _sv(ebuf, C, NCH, C)), waits=[(sC, e3)], inc=sV)
        P["vAH"] = vAH
        return P

    def mixer_core1(g, f, P):
        aq, edc = P["aq"], P["edc"]
        gl_done = g0_done = trv = None
        for j in range(NCH):
            for (r0, r1, tp) in QUAD:
                gl_done = mm(psW_[r0:r1, j * 128:(j + 1) * 128],
                             khat[r0:r1, j * C:(j + 1) * C],
                             aq[r0:r1, j * 128:(j + 1) * 128], True, True,
                             waits=[(sV, P["vAH"])] if (j == 0 and r0 == 0) else (),
                             inc=sT if (j == NCH - 1 and r0 == 64) else None, tp=tp)
        for j in range(NCH):
            for (r0, r1, tp) in QUAD:
                g0_done = mm(psU[r0:r1, j * C:(j + 1) * C],
                             aq[r0:r1, j * 128: j * 128 + C],
                             khat[r0:r1, j * C:(j + 1) * C], True, True,
                             inc=sT if (j == NCH - 1 and r0 == 64) else None, tp=tp)
        for j in range(NCH):
            for (r0, r1, tp) in QUAD:
                sc.post("tensor", lambda t, j=j, r0=r0, r1=r1, tp=tp, f=f: t.transpose(
                    psT4[r0:r1, j * 128: j * 128 + 64],
                    v_g[r0:r1, f * GT + j * C: f * GT + (j + 1) * C],
                    id2b[r0:r1, 0:64], tile_position=tp),
                    waits=[(sC, P["psT4_free"])] if (j == 0 and r0 == 0) else ())
                trv = sc.post("tensor", lambda t, j=j, r0=r0, r1=r1, tp=tp, aq=aq: t.transpose(
                    psT4[r0:r1, j * 128 + 64: j * 128 + 128],
                    aq[r0:r1, j * 128: j * 128 + C],
                    id2b[r0:r1, 0:64], tile_position=tp),
                    inc=sT if (j == NCH - 1 and r0 == 64) else None)
        evT0 = sc.post("vector", lambda v: v.tensor_mul(
            _sv(T0m, C, NCH, C), _sv(psW_, C, NCH, 128), _sv(maskU8, C, NCH, C)),
            waits=[(sT, gl_done)], inc=sV)
        evLam = sc.post("vector", lambda v: v.tensor_mul(
            _sv(LK, C, NCH, 2 * C), _sv(psW_, C, NCH, 128, C),
            _sv(maskI8, C, NCH, C)), inc=sV)
        evU0 = sc.post("vector", lambda v: v.tensor_mul(
            _sv(XU[0], C, NCH, 192, 128), _sv(psU, C, NCH, C),
            _sv(maskL8, C, NCH, C)), waits=[(sT, g0_done)], inc=sV)
        evX0 = sc.post("scalar", lambda sy: sy.copy(
            _sv(XU[0], 128, NCH, 192), _sv(psT4, 128, NCH, 128)),
            waits=[(sT, trv)], inc=sC)
        kv = None
        for j in range(NCH):
            kv = sc.post("vector", lambda v, j=j, edc=edc: v.tensor_scalar_mul(
                ktmp[:, j * C:(j + 1) * C], khat[:, j * C:(j + 1) * C], edc[:, j:j + 1]),
                waits=[(sC, P["ce"]), (sA, st8["stO_dma"])] if j == 0 else (),
                inc=sV if j == NCH - 1 else None)
        ktv = None
        for j in range(NCH):
            for (r0, r1, tp) in QUAD:
                ktv = sc.post("tensor", lambda t, j=j, r0=r0, r1=r1, tp=tp: t.transpose(
                    psT4[r0:r1, j * 64:(j + 1) * 64], ktmp[r0:r1, j * C:(j + 1) * C],
                    id2b[r0:r1, 0:64], tile_position=tp),
                    waits=[(sV, kv), (sC, evX0)] if (j == 0 and r0 == 0) else (),
                    inc=sT if (j == NCH - 1 and r0 == 64) else None)
        evKT = sc.post("scalar", lambda sy: sy.copy(
            _sv(LK, C, NCH, 2 * C, C), _sv(psT4, C, NCH, C)),
            waits=[(sT, ktv)], inc=sC)
        P.update(evT0=evT0, evLam=evLam, evU0=evU0, evX0=evX0, evKT=evKT)

    def mixer_core2(g, f, P, nxt):
        aq = P["aq"]
        edc = P["edc"]
        curT = T0m
        cur = 0
        last_solve_v = P["evU0"]
        last_solve_c = P["evKT"]
        for k in range(MIX_SOLVE_K):
            do_u, do_t = k < 4, k < 5
            if MIX_SOLVE_SUB == 'x':
                do_u = do_t = False
            elif MIX_SOLVE_SUB == 'xu':
                do_t = False
            nbatch = 1 + (1 if do_u else 0) + (1 if do_t else 0)
            last_mm = None

            def _is_last(b, j, r0):
                return b == nbatch - 1 and j == NCH - 1 and r0 == 64
            for j in range(NCH):
                for (r0, r1, tp) in QUAD:
                    last_mm = mm(psX[r0:r1, j * 128:(j + 1) * 128],
                                 curT[r0:r1, j * C:(j + 1) * C],
                                 XU[cur][r0:r1, j * 192: j * 192 + 128], True, True,
                                 waits=([(sV, last_solve_v)] +
                                        ([(sC, last_solve_c)] if last_solve_c else []))
                                 if (j == 0 and r0 == 0) else (),
                                 inc=sT if _is_last(0, j, r0) else None, tp=tp)
            if do_u:
                for j in range(NCH):
                    for (r0, r1, tp) in QUAD:
                        last_mm = mm(psU[r0:r1, j * C:(j + 1) * C],
                                     curT[r0:r1, j * C:(j + 1) * C],
                                     XU[cur][r0:r1, j * 192 + 128: j * 192 + 192],
                                     True, True, tp=tp,
                                     inc=sT if _is_last(1, j, r0) else None)
            if do_t:
                for j in range(NCH):
                    for (r0, r1, tp) in QUAD:
                        last_mm = mm(psB[r0:r1, j * C:(j + 1) * C],
                                     XU[cur][r0:r1, j * 192 + 128: j * 192 + 192],
                                     curT[r0:r1, j * C:(j + 1) * C], True, True, tp=tp,
                                     inc=sT if _is_last(nbatch - 1, j, r0) else None)
            nxt_b = 1 - cur
            ev1 = sc.post("vector", lambda v, cur=cur, nxt_b=nxt_b: v.tensor_add(
                _sv(XU[nxt_b], 128, NCH, 192), _sv(psX, 128, NCH, 128),
                _sv(XU[cur], 128, NCH, 192)), waits=[(sT, last_mm)], inc=sV)
            last_solve_v = ev1
            last_solve_c = 0
            if do_u:
                last_solve_c = sc.post("scalar", lambda sy, nxt_b=nxt_b: sy.copy(
                    _sv(XU[nxt_b], C, NCH, 192, 128), _sv(psU, C, NCH, C)),
                    waits=[(sT, last_mm)], inc=sC)
            if do_t:
                newT = Tt[k % 2]
                last_solve_c = sc.post("scalar", lambda sy, newT=newT: sy.copy(
                    newT[:, 0:GT], psB[:, 0:GT]), waits=[(sT, last_mm)] if not do_u else (),
                    inc=sC)
                curT = newT
            cur = nxt_b
        last_solve_c = max(last_solve_c, 1)

        def MV_ap(r0, r1, j):
            return XU[cur][r0:r1, j * 192: j * 192 + 64]

        def MA_ap(r0, r1, j):
            return XU[cur][r0:r1, j * 192 + 64: j * 192 + 128]

        wt = None
        wtw = [(sC, last_solve_c)]
        if nxt is not None:
            wtw.append((sV, nxt["v9"]))
        for j in range(NCH):
            for (r0, r1, tp) in QUAD:
                wt = mm(psW_[r0:r1, j * 128:(j + 1) * 128], MA_ap(r0, r1, j),
                        LK[r0:r1, j * 128:(j + 1) * 128], True, True,
                        waits=wtw if (j == 0 and r0 == 0) else (),
                        inc=sT if (j == NCH - 1 and r0 == 64) else None, tp=tp)
        jv = None
        jw = [(sV, last_solve_v)]
        if nxt is not None:
            jw.append((sV, nxt["v7"]))
        for j in range(NCH):
            for (r0, r1, tp) in QUAD:
                jv = mm(psU[r0:r1, j * C:(j + 1) * C],
                        LK[r0:r1, j * 128 + C: j * 128 + 128],
                        MV_ap(r0, r1, j), True, True,
                        waits=jw if (j == 0 and r0 == 0) else (),
                        inc=sT if (j == NCH - 1 and r0 == 64) else None, tp=tp)
        evW = sc.post("vector", lambda v, aq=aq: v.tensor_add(
            _sv(WTst, C, NCH, 128), _sv(psW_, C, NCH, 128),
            _sv(aq, C, NCH, 2 * C, C)),
            waits=[(sT, wt), (sA, st8["stWT_dma"])], inc=sV)
        evTc = None
        for j in range(NCH):
            evTc = sc.post("vector", lambda v, j=j, edc=edc: v.scalar_tensor_tensor(
                WTst[:, j * 128 + 64: j * 128 + 128], ident2[:, 0:64], edc[:, j:j + 1],
                psW_[:, j * 128 + 64: j * 128 + 128], ALU.mult, ALU.add), inc=sV)
        evJ = sc.post("vector", lambda v: v.tensor_copy(J_g[:, 0:GT], psU[:, 0:GT]),
                      waits=[(sT, jv)], inc=sV)
        st8["stWT_dma"] = sc.dma("sync", lambda s, g=g, f=f: s.dma_start(
            stWT_d[:, f, g * NCH:(g + 1) * NCH, :],
            WTst[:, :].rearrange("p (j m) -> p j m", j=NCH)), sA, waits=[(sV, evTc)])
        if "wtst" in taps:
            tap("wtst", (128, NCH * 128), BF16)
            st8["tap_dma"] = sc.dma("sync", lambda s: s.dma_start(
                tap_d["wtst"][:, :], WTst[:, :]), sA, waits=[(sV, evTc)])
        if "jg" in taps:
            tap("jg", (128, GT), F32)
            st8["tap_dma"] = sc.dma("sync", lambda s: s.dma_start(
                tap_d["jg"][:, :], J_g[:, :]), sA, waits=[(sV, evJ)])

        swv = evTc
        obl = None
        for j in range(NCH):
            for (r0, r1, tp) in QUAD:
                mm(psA[r0:r1, j * C:(j + 1) * C], MV_ap(r0, r1, j),
                   LK[r0:r1, j * 128: j * 128 + C], True, False,
                   waits=[(sV, swv)] if r0 == 0 else (), tp=tp, sgc=True)
            for (r0, r1, tp) in QUAD:
                obl = mm(psA[r0:r1, j * C:(j + 1) * C], scur[r0:r1, f * 64:(f + 1) * 64],
                         WTst[r0:r1, j * 128: j * 128 + 64], False, True,
                         inc=sT if r0 == 64 else None, tp=tp, sgc=True)
            smm = None
            for (r0, r1, tp) in QUAD:
                smm = mm(psX[r0:r1, 0:64], WTst[r0:r1, j * 128 + 64: j * 128 + 128],
                         scur[r0:r1, f * 64:(f + 1) * 64], True, True,
                         inc=sT if r0 == 64 else None, tp=tp)
            swv = sc.post("vector", lambda v, j=j, f=f: v.tensor_add(
                scur[:, f * 64:(f + 1) * 64], psX[:, 0:64], J_g[:, j * C:(j + 1) * C]),
                waits=[(sT, smm), (sV, evJ)], inc=sV)
        evO = sc.post("vector", lambda v: v.tensor_copy(Ost[:, 0:GT], psA[:, 0:GT]),
                      waits=[(sT, obl), (sV, swv), (sA, st8["stO_dma"])], inc=sV)
        st8["stO_dma"] = sc.dma("sync", lambda s, g=g, f=f: s.dma_start(
            stO_d[f, :, g * GT:(g + 1) * GT], Ost[:, 0:GT]), sA, waits=[(sV, evO)])
        if "op1" in taps:
            tap("op1", (NP, 128, TC), BF16)
            st8["stO_dma"] = sc.dma("sync", lambda s, g=g, f=f: s.dma_start(
                tap_d["op1"][f, :, g * GT:(g + 1) * GT], Ost[:, 0:GT]), sA)
        P["evW"] = evW
        P["evTc"] = evTc
        P["evJ"] = evJ

    def p1_group(g):
        g0 = g * GT
        src = x_d[:, :, g0:g0 + GT].rearrange("f p t -> p f t")
        ld = sc.dma("gpsimd", lambda gg: gg.dma_start(
            xgb[:, :].rearrange("p (f t) -> p f t", f=8), src),
            sW, waits=[(sV, st8["prev_group_v"]), (sA, st8["tap_dma"])])
        sq = sc.post("scalar", lambda s: s.activation(sqb[:, :], xgb[:, :], AF.Square),
                     waits=[(sW, ld)], inc=sC)
        stv = None
        for f8 in range(8):
            mm(psU[0:65, 0:GT], sel64[:, 0:65], sqb[:, f8 * GT:(f8 + 1) * GT],
               f8 == 0, False,
               waits=[(sC, sq), (sW, pw_ready), (sV, st8["prev_group_v"])] if f8 == 0 else ())
        for f8 in range(8):
            stv = mm(psU[0:65, 0:GT], ones65[:, 0:65], xgb[:, f8 * GT:(f8 + 1) * GT],
                     False, f8 == 7,
                     waits=(), inc=sT if f8 == 7 else None)
        v1 = sc.post("vector", lambda v: v.tensor_scalar_mul(
            r_a[0:1, 0:GT], psU[0:1, 0:GT], 1.0 / D), waits=[(sT, stv)], inc=sV)
        sc.post("vector", lambda v: v.tensor_copy(murt[0:1, 0:GT], r_a[0:1, 0:GT]))
        sc.post("vector", lambda v: v.tensor_scalar_mul(
            r_m2[0:1, 0:GT], psU[64:65, 0:GT], 1.0 / D))
        sc.post("vector", lambda v: v.tensor_mul(
            r_a[0:1, 0:GT], r_a[0:1, 0:GT], r_a[0:1, 0:GT]))
        v4 = sc.post("vector", lambda v: v.tensor_sub(
            r_m2[0:1, 0:GT], r_m2[0:1, 0:GT], r_a[0:1, 0:GT]), inc=sV)
        c1 = sc.post("scalar", lambda s: s.activation(
            r_b[0:1, 0:GT], r_m2[0:1, 0:GT], AF.Ln, bias=EPS), waits=[(sV, v4)], inc=sC)
        v5 = sc.post("scalar", lambda s: s.activation(
            murt[0:1, GT:2 * GT], r_b[0:1, 0:GT], AF.Exp, scale=-0.5), inc=sC)
        bc = None
        for hh in range(2):
            bc = mm(psW_[:, hh * GT:(hh + 1) * GT], ones1[0:1, :],
                    murt[0:1, hh * GT:(hh + 1) * GT], True, True,
                    waits=[(sC, v5), (sV, v1 + 1)] if hh == 0 else (),
                    inc=sT if hh == 1 else None)
        xv = None
        for f8 in range(8):
            sl = slice(f8 * GT, (f8 + 1) * GT)
            sc.post("vector", lambda v, sl=sl: v.tensor_sub(
                ebuf[:, 0:GT], xgb[:, sl], psW_[:, 0:GT]),
                waits=[(sT, bc)] if f8 == 0 else ())
            xv = sc.post("vector", lambda v, sl=sl: v.tensor_mul(
                xln[:, sl], ebuf[:, 0:GT], psW_[:, GT:2 * GT]), inc=sV)
        if tap("xln", (NG, 128, 8 * GT), BF16) is not None:
            st8["tap_dma"] = sc.dma("sync", lambda s, g=g: s.dma_start(
                tap_d["xln"][g, :, :], xln[:, :]), sA, waits=[(sV, xv)])

        ping = [psA, psB]
        prev_ev = {0: 0, 1: 0}
        nmm = 0
        for (wsb, dst, actf) in [(wq_sb, q_g, AF.Silu), (wk_sb, k_g, AF.Silu),
                                 (wv_sb, v_g, None)]:
            for cc in range(8):
                bank = ping[nmm % 2]
                w0 = ([(sC, prev_ev[nmm % 2])] if prev_ev[nmm % 2] else [])
                if nmm == 0:
                    w0 = w0 + [(sV, xv)]
                mv = None
                for kk in range(8):
                    mv = mm(bank[:, 0:GT],
                            wsb[:, cc * 1024 + kk * 128: cc * 1024 + (kk + 1) * 128],
                            xln[:, kk * GT:(kk + 1) * GT], kk == 0, kk == 7,
                            waits=w0 if kk == 0 else (), inc=sT if kk == 7 else None)
                if actf is not None:
                    ev = sc.post("scalar", lambda s, bank=bank, dst=dst, cc=cc, actf=actf:
                                 s.activation(dst[:, cc * GT:(cc + 1) * GT], bank[:, 0:GT], actf),
                                 waits=[(sT, mv)], inc=sC)
                else:
                    ev = sc.post("scalar", lambda s, bank=bank, dst=dst, cc=cc:
                                 s.copy(dst[:, cc * GT:(cc + 1) * GT], bank[:, 0:GT]),
                                 waits=[(sT, mv)], inc=sC)
                prev_ev[nmm % 2] = ev
                nmm += 1
        mv = None
        for kk in range(8):
            mv = mm(psA[0:64, 0:GT], wf1_sb[:, kk * 64:(kk + 1) * 64],
                    xln[:, kk * GT:(kk + 1) * GT], kk == 0, kk == 7,
                    waits=[(sC, prev_ev[0])] if kk == 0 else (), inc=sT if kk == 7 else None)
        evf1 = sc.post("scalar", lambda s: s.copy(f1t[0:64, 0:GT], psA[0:64, 0:GT]),
                       waits=[(sT, mv)], inc=sC)
        mv = None
        for kk in range(8):
            mv = mm(psB[0:64, 0:GT], wog1_sb[:, kk * 64:(kk + 1) * 64],
                    xln[:, kk * GT:(kk + 1) * GT], kk == 0, kk == 7,
                    waits=[(sC, prev_ev[1])] if kk == 0 else (), inc=sT if kk == 7 else None)
        evog = sc.post("scalar", lambda s, g=g: s.copy(
            gpre_sb[0:64, g * GT:(g + 1) * GT], psB[0:64, 0:GT]), waits=[(sT, mv)], inc=sC)
        mv = None
        for kk in range(8):
            mv = mm(psA[0:16, 0:GT], wgam_sb[:, kk * 16:(kk + 1) * 16],
                    xln[:, kk * GT:(kk + 1) * GT], kk == 0, kk == 7,
                    waits=[(sC, evf1)] if kk == 0 else (), inc=sT if kk == 7 else None)
        evgam = sc.post("scalar", lambda s: s.activation(gs_g[0:16, 0:GT], psA[0:16, 0:GT],
                                                         AF.Sigmoid), waits=[(sT, mv)], inc=sC)
        zv = evgam
        for cc in range(8):
            bank = ping[cc % 2]
            mv = mm(bank[:, 0:GT], wf2_sb[0:64, cc * 128:(cc + 1) * 128], f1t[0:64, 0:GT],
                    True, True, waits=[(sC, zv)], inc=sT)
            e1 = sc.post("scalar", lambda s, bank=bank: s.activation(
                ebuf[:, 0:GT], bank[:, 0:GT], AF.Exp, scale=-1.0), waits=[(sT, mv)], inc=sC)
            zv = sc.post("scalar", lambda s, cc=cc: s.activation(
                ngk_g[:, cc * GT:(cc + 1) * GT], ebuf[:, 0:GT], AF.Ln, bias=1.0), inc=sC)
        proj_done_c = sC.n

        if "q" in taps:
            for nm, t in [("q", q_g), ("k", k_g), ("v", v_g)]:
                tap(nm, (NG, 128, 8 * GT), BF16)
                st8["tap_dma"] = sc.dma("sync", lambda s, g=g, t=t, nm=nm: s.dma_start(
                    tap_d[nm][g, :, :], t[:, :]), sA, waits=[(sC, proj_done_c)])
        if "ngk" in taps:
            tap("ngk", (NG, 128, 8 * GT), F32)
            for cc in range(8):
                cv = sc.post("vector", lambda v, cc=cc: v.tensor_copy(
                    ebuf[:, 0:GT], ngk_g[:, cc * GT:(cc + 1) * GT]),
                    waits=[(sC, proj_done_c), (sA, st8["tap_dma"])], inc=sV)
                st8["tap_dma"] = sc.dma("sync", lambda s, g=g, cc=cc: s.dma_start(
                    tap_d["ngk"][g, :, cc * GT:(cc + 1) * GT], ebuf[:, 0:GT]),
                    sA, waits=[(sV, cv)])
        if "gam" in taps:
            tap("gam", (NG, 16, GT), BF16)
            st8["tap_dma"] = sc.dma("sync", lambda s, g=g: s.dma_start(
                tap_d["gam"][g, :, :], gs_g[:, :]), sA, waits=[(sC, proj_done_c)])
        if "gpre" in taps:
            tap("gpre", (NG, 64, GT), F32)
            st8["tap_dma"] = sc.dma("sync", lambda s, g=g: s.dma_start(
                tap_d["gpre"][g, :, :], gpre_sb[0:64, g * GT:(g + 1) * GT]),
                sA, waits=[(sC, proj_done_c)])

        if "mix" in phases:
            Pd = [None] * (MIX_PAIRS + 1)
            Pd[0] = mixer_prep(g, 0, {"psw_free": xv, "psu_free": xv, "psT4": 0})
            for f in range(MIX_PAIRS):
                Pd[f]["psT4_free"] = st8["psT4_free"]
                mixer_core1(g, f, Pd[f])
                st8["psT4_free"] = Pd[f]["evKT"]
                if f + 1 < MIX_PAIRS:
                    Pd[f + 1] = mixer_prep(g, f + 1, {"psw_free": Pd[f]["evLam"],
                                                      "psu_free": Pd[f]["evU0"]})
                mixer_core2(g, f, Pd[f], Pd[f + 1])
        st8["prev_group_v"] = sV.n

    if "p1" in phases:
        for g in range(MIX_GROUPS):
            sc.mark(f"p1g{g}")
            p1_group(g)

    p1stack.close()

    if "p2" in phases:
        p2stack = contextlib.ExitStack()

        def sb2(name, shape, dtype):
            return p2stack.enter_context(nc.sbuf_tensor(name, list(shape), dtype))

        def ps2(name, shape, dtype=F32):
            return p2stack.enter_context(nc.psum_tensor(name, list(shape), dtype))

        xg_sb = sb2("xg_sb", (128, 8 * GT), F32)
        sgath = sb2("sgath", (128, NCORE * NP * 64), F32)
        stil = sb2("stil", (128, NP * 64), BF16)
        wtbG = sb2("wtbG", (128, NP * NCH * 128), BF16)
        obG = sb2("obG", (128, NP * GT), BF16)
        ofin = sb2("ofin", (128, 8 * GT), F32)
        ogb = sb2("ogb", (128, 8 * GT), BF16)
        bbuf = sb2("bbuf", (128, 8 * GT), BF16)
        x1g = sb2("x1g", (128, 8 * GT), F32)
        gbuf = sb2("gbuf", (128, GT), F32)
        gpb = sb2("gpb", (64, GT), BF16)
        eb2 = sb2("eb2", (128, GT), F32)
        r2a = sb2("r2a", (1, GT), F32)
        r2b = sb2("r2b", (1, GT), F32)
        mur2 = sb2("mur2", (1, 2 * GT), BF16)
        psA2 = ps2("psA2", (128, 512), F32)
        psB2 = ps2("psB2", (128, 512), F32)
        psW2 = ps2("psW2", (128, 1024), F32)
        psU2 = ps2("psU2", (128, 512), F32)
        psO2 = ps2("psO2", (128, 512), F32)

        p1_end_v = sV.n
        sc.mark("coll")
        if "coll" in phases:
            cc1 = sc.dma("gpsimd", lambda g: g.dma_start(
                cc_in[:, :].rearrange("(f p) v -> p f v", p=128),
                scur[:, :].rearrange("p (f v) -> p f v", v=64)),
                sW, waits=[(sV, p1_end_v)])
            cg = sc.post("gpsimd", lambda g: g.collective_compute(
                "AllGather", ALU.bypass, replica_groups=[list(range(NCORE))],
                ins=[cc_in], outs=[cc_out]), waits=[(sW, cc1)], inc=sG)
            sc.dma("sync", lambda sy: sy.dma_start(
                sgath[:, :].rearrange("p (r f v) -> p r f v", f=NP, v=64),
                cc_out[:, :].rearrange("(r f p) v -> p r f v", f=NP, p=128)),
                sA, waits=[(sG, cg)])
            gath_a = sA.n
            stv2 = 0
            for f in range(NP):
                sc.post("vector", lambda v, f=f: v.tensor_scalar_mul(
                    eb2[:, 0:64], sgath[:, f * 64: f * 64 + 64], sel_sb[:, 0:1]),
                    waits=[(sA, gath_a)] if f == 0 else ())
                for r in range(1, NCORE):
                    sc.post("vector", lambda v, f=f, r=r: v.scalar_tensor_tensor(
                        eb2[:, 0:64], sgath[:, r * NP * 64 + f * 64: r * NP * 64 + f * 64 + 64],
                        sel_sb[:, r:r + 1], eb2[:, 0:64], ALU.mult, ALU.add))
                stv2 = sc.post("vector", lambda v, f=f: v.tensor_copy(
                    stil[:, f * 64:(f + 1) * 64], eb2[:, 0:64]), inc=sV)
        else:
            stv2 = sc.post("vector", lambda v: v.memset(stil[:, :], 0.0),
                           waits=[(sV, p1_end_v)], inc=sV)

        st2 = {"wtG_free": 0, "obG_free": 0, "of_ev": 0, "xg_dma": 0,
               "x1_dma": 0, "stil_v": stv2, "ep_c": 0, "ep_v": stv2, "ep_t": 0}
        for g in range(NG):
            sc.mark(f"p2g{g}")
            st2["xg_dma"] = sc.dma("sync", lambda sy, g=g: sy.dma_start(
                xg_sb[:, :].rearrange("p (f t) -> p f t", f=8),
                x_d[:, :, g * GT:(g + 1) * GT].rearrange("f p t -> p f t")),
                sA, waits=[(sV, st2["ep_v"]), (sT, st2["ep_t"])])
            wldG = sc.dma("sync", lambda sy, g=g: sy.dma_start(
                wtbG[:, :].rearrange("p (f j m) -> p f j m", f=NP, j=NCH),
                stWT_d[:, :, g * NCH:(g + 1) * NCH, :]),
                sA, waits=[(sT, st2["wtG_free"])])
            oldG = sc.dma("sync", lambda sy, g=g: sy.dma_start(
                obG[:, :].rearrange("p (f t) -> p f t", f=NP),
                stO_d[:, :, g * GT:(g + 1) * GT].rearrange("f p t -> p f t")),
                sA, waits=[(sV, st2["obG_free"])])
            sschain = st2["stil_v"]
            ocl = smm2 = None
            for j8 in range(NCH):
                for f in range(NP):
                    for (r0, r1, tp) in QUAD:
                        ocl = mm(psA2[r0:r1, f * 64:(f + 1) * 64],
                                 stil[r0:r1, f * 64:(f + 1) * 64],
                                 wtbG[r0:r1, (f * NCH + j8) * 128: (f * NCH + j8) * 128 + 64],
                                 True, True,
                                 waits=([(sA, wldG), (sV, sschain),
                                         (sV, st2["of_ev"])] if (f == 0 and r0 == 0)
                                        else ()), tp=tp)
                for f in range(NP):
                    for (r0, r1, tp) in QUAD:
                        smm2 = mm(psB2[r0:r1, f * 64:(f + 1) * 64],
                                  wtbG[r0:r1, (f * NCH + j8) * 128 + 64:
                                       (f * NCH + j8) * 128 + 128],
                                  stil[r0:r1, f * 64:(f + 1) * 64], True, True,
                                  inc=sT if (f == NP - 1 and r0 == 64) else None, tp=tp)
                ev = sc.post("vector", lambda v, j8=j8: v.tensor_add(
                    _sv(ofin, 64, NP, GT, j8 * 64), _sv(psA2, 64, NP, 64),
                    _sv(obG, 64, NP, GT, j8 * 64)),
                    waits=[(sT, smm2), (sA, oldG)], inc=sV)
                sschain = sc.post("vector", lambda v: v.tensor_copy(
                    stil[:, :], psB2[:, 0:NP * 64]), inc=sV)
            st2["stil_v"] = sschain
            st2["wtG_free"] = sT.n
            st2["obG_free"] = sschain
            st2["of_ev"] = sschain
            gc = sc.post("scalar", lambda sy, g=g: sy.copy(
                gpb[0:64, 0:GT], gpre_sb[0:64, g * GT:(g + 1) * GT]),
                waits=[(sV, st2["of_ev"])], inc=sC)
            gmul = st2["of_ev"]
            gm_t = 0
            for f8 in range(8):
                half = f8 % 2
                gmm = mm(psW2[:, half * GT:(half + 1) * GT],
                         wog2_sb[0:64, f8 * 128:(f8 + 1) * 128], gpb[0:64, 0:GT],
                         True, True, waits=[(sC, gc), (sV, gmul)] if f8 < 2 else
                         [(sV, gmul)], inc=sT)
                gsig = sc.post("scalar", lambda sy, half=half: sy.activation(
                    gbuf[:, 0:GT], psW2[:, half * GT:(half + 1) * GT], AF.Sigmoid),
                    waits=[(sT, gmm)], inc=sC)
                gmul = sc.post("vector", lambda v, f8=f8: v.tensor_mul(
                    ofin[:, f8 * GT:(f8 + 1) * GT], ofin[:, f8 * GT:(f8 + 1) * GT],
                    gbuf[:, 0:GT]), waits=[(sC, gsig)], inc=sV)
            if P2_STAGE == "gate":
                st2["ep_v"], st2["ep_t"] = sV.n, sT.n
                continue
            sq2 = sc.post("scalar", lambda sy: sy.activation(bbuf[:, :], ofin[:, :],
                                                             AF.Square),
                          waits=[(sV, gmul)], inc=sC)
            cp2 = sc.post("scalar", lambda sy: sy.copy(ogb[:, :], ofin[:, :]), inc=sC)
            stv = None
            for f8 in range(8):
                mm(psU2[0:65, 0:GT], sel64[:, 0:65], bbuf[:, f8 * GT:(f8 + 1) * GT],
                   f8 == 0, False, waits=[(sC, sq2), (sV, gmul)] if f8 == 0 else ())
            for f8 in range(8):
                stv = mm(psU2[0:65, 0:GT], ones65[:, 0:65], ogb[:, f8 * GT:(f8 + 1) * GT],
                         False, f8 == 7, waits=[(sC, cp2)] if f8 == 0 else (),
                         inc=sT if f8 == 7 else None)
            v1 = sc.post("vector", lambda v: v.tensor_scalar_mul(
                r2a[0:1, 0:GT], psU2[0:1, 0:GT], 1.0 / D), waits=[(sT, stv)], inc=sV)
            sc.post("vector", lambda v: v.tensor_copy(mur2[0:1, 0:GT], r2a[0:1, 0:GT]))
            sc.post("vector", lambda v: v.tensor_scalar_mul(
                r2b[0:1, 0:GT], psU2[64:65, 0:GT], 1.0 / D))
            sc.post("vector", lambda v: v.tensor_mul(
                r2a[0:1, 0:GT], r2a[0:1, 0:GT], r2a[0:1, 0:GT]))
            v4 = sc.post("vector", lambda v: v.tensor_sub(
                r2b[0:1, 0:GT], r2b[0:1, 0:GT], r2a[0:1, 0:GT]), inc=sV)
            c1 = sc.post("scalar", lambda sy: sy.activation(
                r2a[0:1, 0:GT], r2b[0:1, 0:GT], AF.Ln, bias=EPS), waits=[(sV, v4)], inc=sC)
            v5 = sc.post("scalar", lambda sy: sy.activation(
                mur2[0:1, GT:2 * GT], r2a[0:1, 0:GT], AF.Exp, scale=-0.5), inc=sC)
            bc = None
            for hh in range(2):
                bc = mm(psW2[:, hh * GT:(hh + 1) * GT], ones1[0:1, :],
                        mur2[0:1, hh * GT:(hh + 1) * GT], True, True,
                        waits=[(sC, v5), (sV, v4)] if hh == 0 else (),
                        inc=sT if hh == 1 else None)
            onv = None
            for f8 in range(8):
                sl = slice(f8 * GT, (f8 + 1) * GT)
                sc.post("vector", lambda v, sl=sl: v.tensor_sub(
                    eb2[:, 0:GT], ofin[:, sl], psW2[:, 0:GT]),
                    waits=[(sT, bc)] if f8 == 0 else ())
                onv = sc.post("vector", lambda v, sl=sl: v.tensor_mul(
                    ogb[:, sl], eb2[:, 0:GT], psW2[:, GT:2 * GT]), inc=sV)
            if P2_STAGE == "stats":
                st2["ep_v"], st2["ep_t"] = sV.n, sT.n
                continue
            prev_ev = 0
            x1v = None
            for cc in range(8):
                mv = None
                for kk in range(8):
                    mv = mm(psO2[:, 0:GT],
                            wo_sb[:, cc * 1024 + kk * 128: cc * 1024 + (kk + 1) * 128],
                            ogb[:, kk * GT:(kk + 1) * GT], kk == 0, kk == 7,
                            waits=([(sV, onv), (sA, st2["xg_dma"])] if cc == 0 else
                                   [(sV, prev_ev)]) if kk == 0 else (),
                            inc=sT if kk == 7 else None)
                x1v = sc.post("vector", lambda v, cc=cc: v.tensor_add(
                    x1g[:, cc * GT:(cc + 1) * GT], psO2[:, 0:GT],
                    xg_sb[:, cc * GT:(cc + 1) * GT]), waits=[(sT, mv)], inc=sV)
                prev_ev = x1v
            st2["x1_dma"] = sc.dma("sync", lambda sy, g=g: sy.dma_start(
                x1_d[:, :, g * GT:(g + 1) * GT].rearrange("f p t -> p f t"),
                x1g[:, :].rearrange("p (f t) -> p f t", f=8)),
                sA, waits=[(sV, x1v)])
            if "x1" in taps:
                tap("x1", (NG, 128, 8 * GT), F32)
                st8["tap_dma"] = sc.dma("sync", lambda sy, g=g: sy.dma_start(
                    tap_d["x1"][g, :, :], x1g[:, :]), sA, waits=[(sV, x1v)])
            if P2_STAGE == "wo":
                st2["ep_v"], st2["ep_t"] = sV.n, sT.n
                continue
            sq3 = sc.post("scalar", lambda sy: sy.activation(bbuf[:, :], x1g[:, :],
                                                             AF.Square),
                          waits=[(sV, x1v)], inc=sC)
            cp3 = sc.post("scalar", lambda sy: sy.copy(ogb[:, :], x1g[:, :]), inc=sC)
            stv = None
            for f8 in range(8):
                mm(psU2[0:65, 0:GT], sel64[:, 0:65], bbuf[:, f8 * GT:(f8 + 1) * GT],
                   f8 == 0, False, waits=[(sC, sq3)] if f8 == 0 else ())
            for f8 in range(8):
                stv = mm(psU2[0:65, 0:GT], ones65[:, 0:65], ogb[:, f8 * GT:(f8 + 1) * GT],
                         False, f8 == 7, waits=[(sC, cp3)] if f8 == 0 else (),
                         inc=sT if f8 == 7 else None)
            v1 = sc.post("vector", lambda v: v.tensor_scalar_mul(
                r2a[0:1, 0:GT], psU2[0:1, 0:GT], 1.0 / D), waits=[(sT, stv)], inc=sV)
            sc.post("vector", lambda v: v.tensor_copy(mur2[0:1, 0:GT], r2a[0:1, 0:GT]))
            sc.post("vector", lambda v: v.tensor_scalar_mul(
                r2b[0:1, 0:GT], psU2[64:65, 0:GT], 1.0 / D))
            sc.post("vector", lambda v: v.tensor_mul(
                r2a[0:1, 0:GT], r2a[0:1, 0:GT], r2a[0:1, 0:GT]))
            v4 = sc.post("vector", lambda v: v.tensor_sub(
                r2b[0:1, 0:GT], r2b[0:1, 0:GT], r2a[0:1, 0:GT]), inc=sV)
            c1 = sc.post("scalar", lambda sy: sy.activation(
                r2a[0:1, 0:GT], r2b[0:1, 0:GT], AF.Ln, bias=EPS), waits=[(sV, v4)], inc=sC)
            v5 = sc.post("scalar", lambda sy: sy.activation(
                mur2[0:1, GT:2 * GT], r2a[0:1, 0:GT], AF.Exp, scale=-0.5), inc=sC)
            bc = None
            for hh in range(2):
                bc = mm(psW2[:, hh * GT:(hh + 1) * GT], ones1[0:1, :],
                        mur2[0:1, hh * GT:(hh + 1) * GT], True, True,
                        waits=[(sC, v5), (sV, v4)] if hh == 0 else (),
                        inc=sT if hh == 1 else None)
            lnv = None
            for f8 in range(8):
                sl = slice(f8 * GT, (f8 + 1) * GT)
                sc.post("vector", lambda v, sl=sl: v.tensor_sub(
                    eb2[:, 0:GT], x1g[:, sl], psW2[:, 0:GT]),
                    waits=[(sT, bc)] if f8 == 0 else ())
                lnv = sc.post("vector", lambda v, f8=f8, g=g: v.tensor_mul(
                    x1ln_sb[:, f8 * TC + g * GT: f8 * TC + (g + 1) * GT],
                    eb2[:, 0:GT], psW2[:, GT:2 * GT]), inc=sV)
            st2["ep_v"] = lnv
            st2["ep_t"] = sT.n
        if "x1ln" in taps:
            tap("x1ln", (128, 8 * TC), BF16)
            st8["tap_dma"] = sc.dma("sync", lambda sy: sy.dma_start(
                tap_d["x1ln"][:, :], x1ln_sb[:, :]), sA, waits=[(sV, st2["ep_v"])])
        p2_end_v, p2_end_t, p2_end_c = sV.n, sT.n, sC.n
        p2stack.close()
        p12stack.close()

    if "p3" in phases:
        p3stack = contextlib.ExitStack()

        def sb3(name, shape, dtype):
            return p3stack.enter_context(nc.sbuf_tensor(name, list(shape), dtype))

        def ps3_(name, shape, dtype=F32):
            return p3stack.enter_context(nc.psum_tensor(name, list(shape), dtype))

        HT = TC // 2
        NS = HT // GT
        g1b = sb3("g1b", (128, 32 * HT), BF16)
        gHb = sb3("gHb", (128, 32 * HT), BF16)
        wsa = [sb3(f"wsa{i}", (128, 4096), BF16) for i in range(2)]
        wsb_ = [sb3(f"wsb{i}", (128, 2048), BF16) for i in range(2)]
        x1r = [sb3(f"x1r{i}", (128, GT), F32) for i in range(2)]
        outbd = [sb3(f"outb{i}", (128, GT), F32) for i in range(2)]
        sc1_sb = sb3("sc1_sb", (128, 32), F32)
        sc2_sb = sb3("sc2_sb", (128, 32), F32)
        sch1_sb = sb3("sch1_sb", (128, 32), F32)
        sch2_sb = sb3("sch2_sb", (128, 32), F32)
        sc3_sb = sb3("sc3_sb", (128, 8), F32)
        teb = sb3("teb", (128, GT), F32)
        pp = [ps3_(f"pp{i}", (128, 512), F32) for i in range(4)]

        p2v = p2_end_v if "p2" in phases else sV.n
        first = [(sV, p2v)]
        for nm, t in [("s1", sc1_sb), ("s2", sc2_sb), ("sh1", sch1_sb),
                      ("sh2", sch2_sb), ("s3", sc3_sb)]:
            sc.dma("sync", lambda sy, t=t, nm=nm: sy.dma_start(t[:, :], din[nm][:, :]),
                   sA, waits=first)
            first = []
        sc_ld = sA.n
        st3 = {"wd": [0, 0], "wd2": [0, 0], "pev": [0, 0, 0, 0], "pevC": [0, 0, 0, 0],
               "out_dma": 0, "rfree": [0, 0], "ob_dma": [0, 0],
               "bar_v": (p2_end_v if "p2" in phases else sV.n),
               "bar_c": (p2_end_c if "p2" in phases else sC.n)}

        def halves():
            return range(P3_HALVES)

        for h in halves():
            th = h * HT
            sc.mark(f"p3A{h}")
            for c in range(32):
                par = c % 2
                w1l = sc.dma("gpsimd", lambda g, c=c, par=par: g.dma_start(
                    wsb_[par][:, 0:1024], din["w1"][:, c * 1024:(c + 1) * 1024]),
                    sW, waits=[(sT, st3["wd"][par])] + ([(sV, p2v)] if (h == 0 and c < 2) else []))
                w2l = sc.dma("gpsimd", lambda g, c=c, par=par: g.dma_start(
                    wsb_[par][:, 1024:2048], din["w2"][:, c * 1024:(c + 1) * 1024]),
                    sW, waits=[])
                for sp in range(NS):
                    pa, pb = pp[2 * par], pp[2 * par + 1]
                    tsl = slice(th + sp * GT, th + (sp + 1) * GT)
                    mv1 = mv2 = None
                    for kk in range(8):
                        w0 = []
                        if kk == 0:
                            w0 = [(sW, w2l), (sC, st3["pevC"][2 * par]),
                                  (sV, st3["pev"][2 * par]), (sA, sc_ld),
                                  (sV, st3["bar_v"]), (sC, st3["bar_c"])]
                        mv1 = mm(pa[:, 0:GT], wsb_[par][:, kk * 128: kk * 128 + 128],
                                 x1ln_sb[:, kk * TC + tsl.start: kk * TC + tsl.stop],
                                 kk == 0, kk == 7, waits=w0,
                                 inc=sT if kk == 7 else None)
                    for kk in range(8):
                        mv2 = mm(pb[:, 0:GT], wsb_[par][:, 1024 + kk * 128: 1024 + kk * 128 + 128],
                                 x1ln_sb[:, kk * TC + tsl.start: kk * TC + tsl.stop],
                                 kk == 0, kk == 7,
                                 waits=[(sC, st3["pevC"][2 * par + 1]),
                                        (sV, st3["pev"][2 * par + 1])] if kk == 0 else (),
                                 inc=sT if kk == 7 else None)
                    e1 = sc.post("scalar", lambda sy, pa=pa, c=c: sy.activation(
                        teb[:, 0:GT], pa[:, 0:GT], AF.Silu, scale=sc1_sb[:, c:c + 1]),
                        waits=[(sT, mv1)], inc=sC)
                    st3["pevC"][2 * par] = e1
                    gv = sc.post("vector", lambda v, pb=pb, c=c, sp=sp, h=h: v.scalar_tensor_tensor(
                        g1b[:, c * HT + sp * GT: c * HT + (sp + 1) * GT],
                        pb[:, 0:GT], sc2_sb[:, c:c + 1], teb[:, 0:GT],
                        ALU.mult, ALU.mult), waits=[(sT, mv2), (sC, e1)], inc=sV)
                    st3["pev"][2 * par + 1] = gv
                st3["wd"][par] = sT.n
            st3["bar_v"], st3["bar_c"] = sV.n, sC.n
            for (stage, wnm, ssb, srcb, dstb) in [("B", "h1", sch1_sb, g1b, gHb),
                                                  ("C", "h2", sch2_sb, gHb, g1b)]:
                sc.mark(f"p3{stage}{h}")
                for c in range(32):
                    par = c % 2
                    wl = sc.dma("gpsimd", lambda g, c=c, par=par, wnm=wnm: g.dma_start(
                        wsa[par][:, :], din[wnm][:, c * 4096:(c + 1) * 4096]),
                        sW, waits=[(sT, st3["wd2"][par])])
                    pa, pb = pp[2 * par], pp[2 * par + 1]
                    mv = None
                    for kk in range(32):
                        w0 = ()
                        if kk == 0:
                            w0 = [(sW, wl), (sC, st3["pevC"][2 * par]),
                                  (sC, st3["pevC"][2 * par + 1]),
                                  (sV, st3["bar_v"]), (sC, st3["bar_c"])]
                        for sp in range(NS):
                            mv = mm(pa[:, 0:GT] if sp == 0 else pb[:, 0:GT],
                                    wsa[par][:, kk * 128:(kk + 1) * 128],
                                    srcb[:, kk * HT + sp * GT: kk * HT + (sp + 1) * GT],
                                    kk == 0, kk == 31, waits=w0 if sp == 0 else (),
                                    inc=sT if (kk == 31 and sp == NS - 1) else None)
                            w0 = ()
                    for sp in range(NS):
                        px = pa if sp == 0 else pb
                        ge = sc.post("scalar", lambda sy, px=px, c=c, sp=sp, ssb=ssb,
                                     dstb=dstb: sy.activation(
                            dstb[:, c * HT + sp * GT: c * HT + (sp + 1) * GT],
                            px[:, 0:GT], AF.Gelu, scale=ssb[:, c:c + 1]),
                            waits=[(sT, mv)] if sp == 0 else (), inc=sC)
                        st3["pevC"][2 * par + sp] = ge
                    st3["wd2"][par] = sT.n
                st3["bar_v"], st3["bar_c"] = sV.n, sC.n
            sc.mark(f"p3D{h}")
            for c in range(8):
                par = c % 2
                w3l = sc.dma("gpsimd", lambda g, c=c, par=par: g.dma_start(
                    wsa[par][:, :], din["w3"][:, c * 4096:(c + 1) * 4096]),
                    sW, waits=[(sT, st3["wd2"][par])])
                x1ls = []
                for sp in range(NS):
                    rp = sp
                    x1ls.append(sc.dma("sync", lambda sy, c=c, sp=sp, th=th, rp=rp:
                                       sy.dma_start(
                        x1r[rp][:, :], x1_d[c, :, th + sp * GT: th + (sp + 1) * GT]),
                        sA, waits=[(sV, st3["rfree"][rp])]))
                pa, pb = pp[2 * par], pp[2 * par + 1]
                mv = None
                for kk in range(32):
                    w0 = ()
                    if kk == 0:
                        w0 = [(sW, w3l), (sV, st3["pev"][2 * par]),
                              (sV, st3["pev"][2 * par + 1]),
                              (sV, st3["bar_v"]), (sC, st3["bar_c"])]
                    for sp in range(NS):
                        mv = mm(pa[:, 0:GT] if sp == 0 else pb[:, 0:GT],
                                wsa[par][:, kk * 128:(kk + 1) * 128],
                                g1b[:, kk * HT + sp * GT: kk * HT + (sp + 1) * GT],
                                kk == 0, kk == 31, waits=w0 if sp == 0 else (),
                                inc=sT if (kk == 31 and sp == NS - 1) else None)
                        w0 = ()
                for sp in range(NS):
                    px = pa if sp == 0 else pb
                    rp = sp
                    op = sp
                    ov = sc.post("vector", lambda v, px=px, c=c, rp=rp, op=op:
                                 v.scalar_tensor_tensor(
                        outbd[op][:, 0:GT], px[:, 0:GT], sc3_sb[:, c:c + 1],
                        x1r[rp][:, :], ALU.mult, ALU.add),
                        waits=[(sT, mv), (sA, x1ls[sp]), (sS, st3["ob_dma"][op])], inc=sV)
                    st3["pev"][2 * par + sp] = ov
                    st3["rfree"][rp] = ov
                    st3["ob_dma"][op] = sc.dma("scalar", lambda sy, c=c, sp=sp, th=th,
                                               op=op: sy.dma_start(
                        out_d[c, :, th + sp * GT: th + (sp + 1) * GT], outbd[op][:, 0:GT]),
                        sS, waits=[(sV, ov)])
                st3["wd2"][par] = sT.n
            st3["wd2"][0] = st3["wd2"][1] = sT.n
            st3["bar_v"], st3["bar_c"] = sV.n, sC.n
        sc.mark(None)
        p3stack.close()
    else:
        pass


    if "send" in taps:
        tap("send", (128, NP * 64), F32)
        sbuf_send = sb("sbuf_send", (128, NP * 64), F32)
        cvv = sc.post("vector", lambda v: v.tensor_copy(sbuf_send[:, :], scur[:, :]),
                      waits=[(sV, st8["prev_group_v"])], inc=sV)
        sc.dma("sync", lambda s: s.dma_start(tap_d["send"][:, :], sbuf_send[:, :]),
               sA, waits=[(sV, cvv)])

    sc.post("sync", None, waits=[(sA, sA.n), (sW, sW.n), (sS, sS.n)])

    with nc.Block() as block:
        for s in sc.sems.values():
            s.h = stack.enter_context(nc.semaphore(s.name))

        @block.sync
        def _(e):
            sc.replay("sync", e, nc)

        @block.tensor
        def _(e):
            sc.replay("tensor", e, nc)

        @block.vector
        def _(e):
            sc.replay("vector", e, nc)

        @block.scalar
        def _(e):
            sc.replay("scalar", e, nc)

        @block.gpsimd
        def _(e):
            sc.replay("gpsimd", e, nc)

    stack.close()
    return nc, tap_d



def kernel(**inputs):
    nc, tap_d = build()
    in_maps = [_prep_inputs(i, inputs) for i in range(NCORE)]
    res = run_bass_kernel_spmd(nc, in_maps, core_ids=list(range(NCORE)))
    out = np.empty((B, T, D), np.float32)
    for i in range(NCORE):
        b, half = i // 2, i % 2
        o = res.results[i]["out"].reshape(8 * 128, TC)
        out[b, half * TC:(half + 1) * TC, :] = o.T
    return out


def run_taps(inputs, taps, phases=None, trace=False):
    nc, tap_d = build(taps=set(taps), phases=set(phases or PHASES))
    in_maps = [_prep_inputs(i, inputs) for i in range(NCORE)]
    res = run_bass_kernel_spmd(nc, in_maps, core_ids=list(range(NCORE)), trace=trace)
    return res


# version 09
# speedup vs baseline: 1.0190x; 1.0036x over previous
import contextlib
import numpy as np
import ml_dtypes
import concourse.bass as bass
import concourse.mybir as mybir
from concourse.ap import AP
from concourse.bass_utils import run_bass_kernel_spmd

F32 = mybir.dt.float32
F16 = mybir.dt.float16
BF16 = mybir.dt.bfloat16
AF = mybir.ActivationFunctionType
ALU = mybir.AluOpType

B, T, D = 4, 4096, 1024
H, HD = 16, 64
DH = 4096
NCORE = 8
TC = 2048
NG = 4
GT = 512
C = 64
NCH = GT // C
NP = 8
EPS = 1e-5

TAPS = set()
PHASES = {"p1", "mix", "coll", "p2", "p3"}
MIX_GROUPS = NG
MIX_PAIRS = NP
MIX_STAGE = "all"
MIX_SOLVE_K = 6
MIX_SOLVE_SUB = "full"
DBG_J = 0
P3_HALVES = 2
P2_STAGE = "all"
P2_SWEEP_MM = 3



def _lhsT_blocks(w_in_out, kchunks, cchunks):
    a = w_in_out.reshape(kchunks, 128, cchunks, 128).transpose(1, 2, 0, 3)
    return np.ascontiguousarray(a.reshape(128, cchunks * kchunks * 128))


def _consts():
    c = {}
    c["c_ones1c"] = np.ones((128, 1), np.float32)
    s64 = np.zeros((128, 65), np.float32); s64[:, 64] = 1.0
    c["c_sel64"] = s64
    o65 = np.zeros((128, 65), np.float32); o65[:, 0] = 1.0
    c["c_ones65"] = o65
    sh = np.zeros((128, 65), np.float32)
    sh[:64, 0] = 1.0; sh[64:, 64] = 1.0
    c["c_selH65"] = sh
    c["c_ones1"] = np.ones((1, 128), np.float32)
    hA = np.zeros((1, 128), np.float32); hA[0, :64] = 1.0
    hB = np.zeros((1, 128), np.float32); hB[0, 64:] = 1.0
    c["c_hA"], c["c_hB"] = hA, hB
    sp = np.zeros((16, NP * 128), np.float32)
    for f in range(NP):
        sp[2 * f, f * 128: f * 128 + 64] = -1.0
        sp[2 * f + 1, f * 128 + 64: f * 128 + 128] = -1.0
    c["c_selPair"] = sp
    s = np.arange(64)
    mU = (s[:, None] < s[None, :]).astype(np.float32)
    mI = (s[:, None] <= s[None, :]).astype(np.float32)
    mL = (s[None, :] < s[:, None]).astype(np.float32)
    c["c_maskU8"] = np.tile(np.tile(mU, (2, 1)), (1, NCH))
    c["c_maskI8"] = np.tile(np.tile(mI, (2, 1)), (1, NCH))
    c["c_maskL8"] = np.tile(np.tile(mL, (2, 1)), (1, NCH))
    c["c_ident2"] = np.tile(np.eye(64, dtype=np.float32), (2, 1))
    c["c_ident"] = np.eye(128, dtype=np.float32)
    c["c_eps"] = np.full((128, 1), EPS, np.float32)
    c["c_id2b"] = np.tile(np.eye(64, dtype=np.float32), (2, 1))
    return c


def _prep_inputs(core, inp):
    b, half = core // 2, core % 2
    t0 = half * TC
    d = {}
    xs = np.asarray(inp["L0_attn_x"])[b, t0:t0 + TC, :]
    d["xfm"] = np.ascontiguousarray(xs.T.reshape(8, 128, TC))
    d["wq"] = _lhsT_blocks(np.asarray(inp["L0_attn_Wq"]), 8, 8)
    d["wk"] = _lhsT_blocks(np.asarray(inp["L0_attn_Wk"]), 8, 8)
    d["wv"] = _lhsT_blocks(np.asarray(inp["L0_attn_Wv"]), 8, 8)
    d["wo"] = _lhsT_blocks(np.asarray(inp["L0_attn_Wo"]), 8, 8)
    d["wf1"] = np.ascontiguousarray(np.asarray(inp["L0_attn_Wf1"]).reshape(8, 128, 64)
                                    .transpose(1, 0, 2).reshape(128, 512))
    d["wf2"] = np.ascontiguousarray(np.asarray(inp["L0_attn_Wf2"]))
    d["wog1"] = np.ascontiguousarray(np.asarray(inp["L0_attn_Wog1"]).reshape(8, 128, 64)
                                     .transpose(1, 0, 2).reshape(128, 512))
    d["wog2"] = np.ascontiguousarray(np.asarray(inp["L0_attn_Wog2"]))
    d["wgam"] = np.ascontiguousarray(np.asarray(inp["L0_attn_Wgamma"]).reshape(8, 128, 16)
                                     .transpose(1, 0, 2).reshape(128, 128))
    d["nw"] = np.ascontiguousarray(np.asarray(inp["L0_attn_norm_w"]).reshape(8, 128).T)
    d["w1"] = _lhsT_blocks(np.asarray(inp["L0_ffn_w1_w"]).T, 8, 32)
    d["w2"] = _lhsT_blocks(np.asarray(inp["L0_ffn_w2_w"]).T, 8, 32)
    d["h1"] = _lhsT_blocks(np.asarray(inp["L0_ffn_h1_w"]).T, 32, 32)
    d["h2"] = _lhsT_blocks(np.asarray(inp["L0_ffn_h2_w"]).T, 32, 32)
    d["w3"] = _lhsT_blocks(np.asarray(inp["L0_ffn_w3_w"]).T, 32, 8)
    d["s1"] = np.ascontiguousarray(np.asarray(inp["L0_ffn_w1_s"]).reshape(32, 128).T)
    d["s2"] = np.ascontiguousarray(np.asarray(inp["L0_ffn_w2_s"]).reshape(32, 128).T)
    d["sh1"] = np.ascontiguousarray(np.asarray(inp["L0_ffn_h1_s"]).reshape(32, 128).T)
    d["sh2"] = np.ascontiguousarray(np.asarray(inp["L0_ffn_h2_s"]).reshape(32, 128).T)
    d["s3"] = np.ascontiguousarray(np.asarray(inp["L0_ffn_w3_s"]).reshape(8, 128).T)
    sel = np.zeros((128, NCORE), np.float32)
    if half == 1:
        sel[:, core - 1] = 1.0
    d["sel"] = sel
    d.update(_consts())
    return d



ENGINES = ("tensor", "vector", "scalar", "gpsimd", "sync")


class _Sem:
    __slots__ = ("name", "n", "h")

    def __init__(self, name):
        self.name, self.n, self.h = name, 0, None


class Sched:
    def __init__(self):
        self.streams = {e: [] for e in ENGINES}
        self.sems = {}

    def sem(self, name):
        if name not in self.sems:
            self.sems[name] = _Sem(name)
        return self.sems[name]

    def dma(self, eng, fn, qsem, waits=()):
        w = tuple(waits) + ((qsem, qsem.n),)
        return self.post(eng, fn, waits=w, inc=qsem, amt=16)

    def post(self, eng, fn, waits=(), inc=None, amt=1):
        val = None
        if inc is not None:
            inc.n += amt
            val = inc.n
        self.streams[eng].append((tuple(waits), fn, inc, amt))
        return val

    def mark(self, name):
        for e in ENGINES:
            self.streams[e].append(("MARK", name, None, None))

    def replay(self, eng_name, eng, nc=None):
        import contextlib as _ctx
        cur = None
        for entry in self.streams[eng_name]:
            if entry[0] == "MARK":
                if cur is not None and nc is not None:
                    nc.leave_named_scope(cur[0], cur[1], False)
                    cur = None
                if nc is not None and entry[1] is not None:
                    sid, _ = nc.enter_named_scope(entry[1], False)
                    cur = (entry[1], sid)
                continue
            waits, fn, inc, amt = entry
            for (s, v) in waits:
                if v > 0:
                    eng.wait_ge(s.h, v)
            inst = fn(eng) if fn is not None else None
            if inc is not None:
                if inst is None:
                    inst = eng.nop()
                inst.then_inc(inc.h, amt)
        if cur is not None and nc is not None:
            nc.leave_named_scope(cur[0], cur[1], False)


def _sv(tile, width, count, stride, offset=0):
    base = tile[:, 0:1]
    return AP(base.tensor, base.offset + offset,
              [list(base.ap[0]), [stride, count], [1, width]])



def build(taps=TAPS, phases=PHASES):
    nc = bass.Bass("TRN2", target_bir_lowering=False)
    sc = Sched()
    sA = sc.sem("sA")
    sW = sc.sem("sW")
    sT = sc.sem("sT")
    sV = sc.sem("sV")
    sC = sc.sem("sC")
    sG = sc.sem("sG")
    sS = sc.sem("sS")

    din = {}

    def dram_in(name, shape):
        din[name] = nc.dram_tensor(name, list(shape), F32, kind="ExternalInput").ap()
        return din[name]

    x_d = dram_in("xfm", (8, 128, TC))
    for nm, shp in [("wq", (128, 8192)), ("wk", (128, 8192)), ("wv", (128, 8192)),
                    ("wo", (128, 8192)), ("wf1", (128, 512)), ("wf2", (64, 1024)),
                    ("wog1", (128, 512)), ("wog2", (64, 1024)), ("wgam", (128, 128)),
                    ("nw", (128, 8)),
                    ("w1", (128, 32768)), ("w2", (128, 32768)),
                    ("h1", (128, 131072)), ("h2", (128, 131072)), ("w3", (128, 32768)),
                    ("s1", (128, 32)), ("s2", (128, 32)), ("sh1", (128, 32)),
                    ("sh2", (128, 32)), ("s3", (128, 8)), ("sel", (128, 8)),
                    ("c_ones1c", (128, 1)), ("c_sel64", (128, 65)), ("c_ones65", (128, 65)), ("c_selH65", (128, 65)),
                    ("c_ones1", (1, 128)), ("c_hA", (1, 128)), ("c_hB", (1, 128)),
                    ("c_selPair", (16, NP * 128)),
                    ("c_maskU8", (128, 512)), ("c_maskI8", (128, 512)),
                    ("c_maskL8", (128, 512)),
                    ("c_ident2", (128, 64)), ("c_ident", (128, 128)), ("c_eps", (128, 1)), ("c_id2b", (128, 64))]:
        dram_in(nm, shp)

    out_d = nc.dram_tensor("out", [8, 128, TC], F32, kind="ExternalOutput").ap()
    tap_d = {}

    def tap(name, shape, dtype=F32):
        if name in taps and name not in tap_d:
            tap_d[name] = nc.dram_tensor("tap_" + name, list(shape), dtype,
                                         kind="ExternalOutput").ap()
        return tap_d.get(name)

    x1_d = nc.dram_tensor("x1buf", [8, 128, TC], F32).ap()
    stWT_d = nc.dram_tensor("stWT", [128, NP, TC // C, 128], BF16).ap()
    stO_d = nc.dram_tensor("stO", [NP, 128, TC], BF16).ap()
    cc_in = nc.dram_tensor("cc_in", [NP * 128, 64], F32).ap()
    cc_out = nc.dram_tensor("cc_out", [NCORE * NP * 128, 64], F32).ap()

    stack = contextlib.ExitStack()
    p12stack = contextlib.ExitStack()
    p1stack = contextlib.ExitStack()

    def sb(name, shape, dtype):
        return stack.enter_context(nc.sbuf_tensor(name, list(shape), dtype))

    def sb1(name, shape, dtype):
        return p1stack.enter_context(nc.sbuf_tensor(name, list(shape), dtype))

    def ps(name, shape, dtype=F32):
        return stack.enter_context(nc.psum_tensor(name, list(shape), dtype))

    def ps1(name, shape, dtype=F32):
        return p1stack.enter_context(nc.psum_tensor(name, list(shape), dtype))

    def sb12(name, shape, dtype):
        return p12stack.enter_context(nc.sbuf_tensor(name, list(shape), dtype))

    def ps12(name, shape, dtype=F32):
        return p12stack.enter_context(nc.psum_tensor(name, list(shape), dtype))

    cst = {}
    for nm, shp, dt in [("c_ones1c", (128, 1), BF16), ("c_sel64", (128, 65), BF16),
                        ("c_ones65", (128, 65), BF16),
                        ("c_selH65", (128, 65), BF16),
                        ("c_ones1", (1, 128), BF16), ("c_hA", (1, 128), BF16),
                        ("c_hB", (1, 128), BF16), ("c_selPair", (16, NP * 128), BF16),
                        ("c_maskU8", (128, 512), F32), ("c_maskI8", (128, 512), F32),
                        ("c_maskL8", (128, 512), F32), ("c_ident2", (128, 64), F32),
                        ("c_ident", (128, 128), BF16), ("c_eps", (128, 1), F32),
                        ("c_id2b", (128, 64), BF16)]:
        cst[nm] = sb("sb_" + nm, shp, dt)
    scur = sb("scur", (128, NP * 64), BF16)
    zeros_sb = sb("zeros_sb", (128, C), F32)
    wsb_ = [sb(f"wsb{i}", (128, 2048), BF16) for i in range(2)]
    sel_sb = sb("sel_sb", (128, 8), F32)
    x1ln_sb = sb("x1ln_sb", (128, 8 * TC), BF16)
    wo_sb = sb12("wo_sb", (128, 8192), BF16)
    wog2_sb = sb12("wog2_sb", (64, 1024), BF16)
    nw_sb = sb12("nw_sb", (128, 8), F32)
    gpre_sb = sb12("gpre_sb", (64, TC), F32)

    for nm in cst:
        sc.dma("gpsimd", lambda g, t=cst[nm], nm=nm: g.dma_start(t[:, :], din[nm][:, :]), sW)
    sc.dma("gpsimd", lambda g: g.dma_start(wo_sb[:, :], din["wo"][:, :]), sW)
    sc.dma("gpsimd", lambda g: g.dma_start(wog2_sb[:, :], din["wog2"][:, :]), sW)
    sc.dma("sync", lambda s: s.dma_start(nw_sb[:, :], din["nw"][:, :]), sA)
    sc.dma("sync", lambda s: s.dma_start(sel_sb[:, :], din["sel"][:, :]), sA)
    nc.const_aps.register(F32, EPS, cst['c_eps'][:, 0:1])
    w_base, a_base = sW.n, sA.n
    wo_ready = 0
    for k in range(8):
        ap_w = wo_sb[:, :].rearrange("p (c k m) -> p c k m", c=8, k=8)[:, :, k, :]
        wo_ready = sc.post("vector",
                           lambda vv, ap_w=ap_w, k=k: vv.tensor_scalar_mul(
                               ap_w, ap_w, nw_sb[:, k:k + 1]),
                           waits=[(sW, w_base), (sA, a_base)] if k == 0 else (), inc=sV)
    sc.post("vector", lambda v: v.memset(scur[:, :], 0.0), inc=sV)
    zv0 = sc.post("vector", lambda v: v.memset(zeros_sb[:, :], 0.0), inc=sV)

    wq_sb = sb1("wq_sb", (128, 8192), BF16)
    wk_sb = sb1("wk_sb", (128, 8192), BF16)
    wv_sb = sb1("wv_sb", (128, 8192), BF16)
    wf1_sb = sb1("wf1_sb", (128, 512), BF16)
    wf2_sb = sb1("wf2_sb", (64, 1024), BF16)
    wog1_sb = sb1("wog1_sb", (128, 512), BF16)
    wgam_sb = sb1("wgam_sb", (128, 128), BF16)
    for nm, t in [("wq", wq_sb), ("wk", wk_sb), ("wv", wv_sb), ("wf1", wf1_sb),
                  ("wf2", wf2_sb), ("wog1", wog1_sb), ("wgam", wgam_sb)]:
        sc.dma("gpsimd", lambda g, t=t, nm=nm: g.dma_start(t[:, :], din[nm][:, :]), sW)
    pw_ready = sW.n

    xgb = sb1("xgb", (128, 8 * GT), BF16)
    xln = sb1("xln", (128, 8 * GT), BF16)
    sqb = xln
    q_g = xgb
    k_g = sb1("k_g", (128, 8 * GT), BF16)
    v_g = sb1("v_g", (128, 8 * GT), BF16)
    ngk_g = sb1("ngk_g", (128, 8 * GT), F16)
    f1t = sb1("f1t", (64, GT), BF16)
    gs_g = sb1("gs_g", (16, GT), BF16)
    r_m2 = sb1("r_m2", (1, GT), F32)
    r_a = sb1("r_a", (1, GT), F32)
    r_b = sb1("r_b", (1, GT), F32)
    murt = sb1("murt", (1, 2 * GT), BF16)
    ebuf = sb1("ebuf", (128, GT), F32)

    ka = sb1("ka", (128, GT), BF16)
    khat = sb1("khat", (128, GT), BF16)
    AQs = [sb1(f"AQ{i}", (128, 2 * GT), BF16) for i in range(2)]
    nL = sb1("nL", (128, GT), F32)
    eDCs = [sb1(f"eDC{i}", (128, NCH), F32) for i in range(2)]
    nLCs = [sb1(f"nLC{i}", (128, NCH), F32) for i in range(2)]
    XU = [sb1(f"XU{i}", (128, NCH * 192), BF16) for i in range(2)]
    Tt = [sb1(f"Tt{i}", (128, GT), BF16) for i in range(2)]
    LK = sb1("LK", (128, 2 * GT), BF16)
    T0m = sb1("T0m", (128, GT), BF16)
    ktmp = sb1("ktmp", (128, GT), BF16)
    J_g = sb1("J_g", (128, GT), BF16)
    WTst = sb1("WTst", (128, NCH * 128), BF16)
    Ost = ktmp

    psA = ps1("psA", (128, 512), F32)
    psB = ps1("psB", (128, 512), F32)
    psW_ = ps1("psW", (128, 1024), F32)
    psT4 = ps1("psT4", (128, 1024), BF16)
    psX = ps1("psX", (128, 1024), F32)
    psU = ps1("psU", (128, 512), F32)

    ones1c = cst["c_ones1c"]; sel64 = cst["c_sel64"]; selH65 = cst["c_selH65"]
    ones65 = cst["c_ones65"]
    ones1 = cst["c_ones1"]; hA = cst["c_hA"]; hB = cst["c_hB"]
    selPair = cst["c_selPair"]
    maskU8 = cst["c_maskU8"]; maskI8 = cst["c_maskI8"]; maskL8 = cst["c_maskL8"]
    ident2 = cst["c_ident2"]; ident = cst["c_ident"]; id2b = cst["c_id2b"]

    def mm(psl, lhsT, rhs, start, stop, waits=(), inc=None, tp=None, sgc=False):
        return sc.post("tensor",
                       lambda t, psl=psl, lhsT=lhsT, rhs=rhs, start=start, stop=stop, tp=tp,
                       sgc=sgc: t.matmul(psl, lhsT, rhs, start=start, stop=stop,
                                         tile_position=tp, skip_group_check=sgc),
                       waits=waits, inc=inc)

    QUAD = ((0, 64, (0, 0)), (64, 128, (64, 64)))

    st8 = {"prev_group_v": 0, "stWT_dma": 0, "stO_dma": 0, "tap_dma": 0, "psT4_free": 0}

    def mixer_prep(g, f, prev):
        fsl = slice(f * GT, (f + 1) * GT)
        aq = AQs[f % 2]
        edc = eDCs[f % 2]
        nlc = nLCs[f % 2]
        P = {"aq": aq, "edc": edc}
        gb = mm(psW_[:, 0:GT], selPair[0:16, f * 128:(f + 1) * 128], gs_g[0:16, 0:GT],
                start=True, stop=True, waits=[(sV, prev["psw_free"])], inc=sT)
        kq = sc.post("scalar", lambda s, fsl=fsl: s.activation(sqb[:, 0:GT], k_g[:, fsl],
                                                               AF.Square), inc=sC)
        ssv = mm(psU[0:65, 0:GT], selH65[:, 0:65], sqb[:, 0:GT], start=True, stop=True,
                 waits=[(sC, kq), (sV, prev["psu_free"])], inc=sT)
        v6 = sc.post("vector", lambda v: v.tensor_scalar_max(r_a[0:1, 0:GT], psU[0:1, 0:GT],
                                                             1e-24), waits=[(sT, ssv)], inc=sV)
        v7 = sc.post("vector", lambda v: v.tensor_scalar_max(r_b[0:1, 0:GT], psU[64:65, 0:GT],
                                                             1e-24), inc=sV)
        P["v7"] = v7
        c2a = sc.post("scalar", lambda s: s.activation(r_a[0:1, 0:GT], r_a[0:1, 0:GT], AF.Ln),
                      waits=[(sV, v7)], inc=sC)
        sc.post("scalar", lambda s: s.activation(r_b[0:1, 0:GT], r_b[0:1, 0:GT], AF.Ln))
        sc.post("scalar", lambda s: s.activation(murt[0:1, 0:GT], r_a[0:1, 0:GT], AF.Exp,
                                                 scale=-0.5))
        c2d = sc.post("scalar", lambda s: s.activation(murt[0:1, GT:2 * GT], r_b[0:1, 0:GT],
                                                       AF.Exp, scale=-0.5), inc=sC)
        rb0 = mm(psW_[:, GT:2 * GT], hA[0:1, :], murt[0:1, 0:GT], start=True, stop=False,
                 waits=[(sC, c2d)])
        rb = mm(psW_[:, GT:2 * GT], hB[0:1, :], murt[0:1, GT:2 * GT], start=False, stop=True,
                inc=sT)
        v8 = sc.post("vector", lambda v, fsl=fsl: v.tensor_mul(k_g[:, fsl], k_g[:, fsl],
                                                               psW_[:, GT:2 * GT]),
                     waits=[(sT, rb)], inc=sV)
        v9 = sc.post("vector", lambda v, fsl=fsl: v.tensor_mul(ka[:, 0:GT], k_g[:, fsl],
                                                               psW_[:, 0:GT]), inc=sV)
        P["v9"] = v9
        scv = None
        for j in range(NCH):
            sl = slice(j * C, (j + 1) * C)
            scv = sc.post("vector", lambda v, sl=sl, f=f: v.tensor_tensor_scan(
                nL[:, sl], ngk_g[:, f * GT + sl.start: f * GT + sl.stop],
                zeros_sb[:, 0:C], 0.0, ALU.add, ALU.add),
                waits=[(sV, zv0)] if (g == 0 and f == 0 and j == 0) else (), inc=sV)
        vg = sc.post("vector", lambda v, nlc=nlc: v.tensor_copy(
            nlc[:, 0:NCH], _sv(nL, 1, NCH, C, C - 1)), waits=[(sV, scv)], inc=sV)
        ce = sc.post("scalar", lambda s, nlc=nlc, edc=edc: s.activation(
            edc[:, 0:NCH], nlc[:, 0:NCH], AF.Exp, scale=-1.0), waits=[(sV, vg)], inc=sC)
        P["ce"] = ce
        e1 = sc.post("scalar", lambda s: s.activation(ebuf[:, 0:GT], nL[:, 0:GT], AF.Exp),
                     waits=[(sV, scv)], inc=sC)
        vKH = sc.post("vector", lambda v, fsl=fsl: v.tensor_mul(khat[:, 0:GT], k_g[:, fsl],
                                                                ebuf[:, 0:GT]),
                      waits=[(sC, e1), (sV, v9)], inc=sV)
        e2 = sc.post("scalar", lambda s: s.activation(ebuf[:, 0:GT], nL[:, 0:GT], AF.Exp,
                                                      scale=-1.0), waits=[(sV, vKH)], inc=sC)
        vQH = sc.post("vector", lambda v, aq=aq, f=f: v.tensor_mul(
            _sv(aq, C, NCH, 2 * C, C), _sv(q_g, C, NCH, C, f * GT),
            _sv(ebuf, C, NCH, C)), waits=[(sC, e2)], inc=sV)
        vt = sc.post("vector", lambda v, fsl=fsl: v.tensor_add(nL[:, 0:GT], nL[:, 0:GT],
                                                               ngk_g[:, fsl]),
                     waits=[(sV, vQH)], inc=sV)
        e3 = sc.post("scalar", lambda s: s.activation(ebuf[:, 0:GT], nL[:, 0:GT], AF.Exp,
                                                      scale=-1.0), waits=[(sV, vt)], inc=sC)
        vAH = sc.post("vector", lambda v, aq=aq: v.tensor_mul(
            _sv(aq, C, NCH, 2 * C, 0), _sv(ka, C, NCH, C),
            _sv(ebuf, C, NCH, C)), waits=[(sC, e3)], inc=sV)
        P["vAH"] = vAH
        return P

    def mixer_core1(g, f, P):
        aq, edc = P["aq"], P["edc"]
        gl_done = g0_done = trv = None
        for j in range(NCH):
            for (r0, r1, tp) in QUAD:
                gl_done = mm(psW_[r0:r1, j * 128:(j + 1) * 128],
                             khat[r0:r1, j * C:(j + 1) * C],
                             aq[r0:r1, j * 128:(j + 1) * 128], True, True,
                             waits=[(sV, P["vAH"])] if (j == 0 and r0 == 0) else (),
                             inc=sT if (j == NCH - 1 and r0 == 64) else None, tp=tp)
        for j in range(NCH):
            for (r0, r1, tp) in QUAD:
                g0_done = mm(psU[r0:r1, j * C:(j + 1) * C],
                             aq[r0:r1, j * 128: j * 128 + C],
                             khat[r0:r1, j * C:(j + 1) * C], True, True,
                             inc=sT if (j == NCH - 1 and r0 == 64) else None, tp=tp)
        for j in range(NCH):
            for (r0, r1, tp) in QUAD:
                sc.post("tensor", lambda t, j=j, r0=r0, r1=r1, tp=tp, f=f: t.transpose(
                    psT4[r0:r1, j * 128: j * 128 + 64],
                    v_g[r0:r1, f * GT + j * C: f * GT + (j + 1) * C],
                    id2b[r0:r1, 0:64], tile_position=tp),
                    waits=[(sC, P["psT4_free"])] if (j == 0 and r0 == 0) else ())
                trv = sc.post("tensor", lambda t, j=j, r0=r0, r1=r1, tp=tp, aq=aq: t.transpose(
                    psT4[r0:r1, j * 128 + 64: j * 128 + 128],
                    aq[r0:r1, j * 128: j * 128 + C],
                    id2b[r0:r1, 0:64], tile_position=tp),
                    inc=sT if (j == NCH - 1 and r0 == 64) else None)
        evT0 = sc.post("vector", lambda v: v.tensor_mul(
            _sv(T0m, C, NCH, C), _sv(psW_, C, NCH, 128), _sv(maskU8, C, NCH, C)),
            waits=[(sT, gl_done)], inc=sV)
        evLam = sc.post("vector", lambda v: v.tensor_mul(
            _sv(LK, C, NCH, 2 * C), _sv(psW_, C, NCH, 128, C),
            _sv(maskI8, C, NCH, C)), inc=sV)
        evU0 = sc.post("vector", lambda v: v.tensor_mul(
            _sv(XU[0], C, NCH, 192, 128), _sv(psU, C, NCH, C),
            _sv(maskL8, C, NCH, C)), waits=[(sT, g0_done)], inc=sV)
        evX0 = sc.post("scalar", lambda sy: sy.copy(
            _sv(XU[0], 128, NCH, 192), _sv(psT4, 128, NCH, 128)),
            waits=[(sT, trv)], inc=sC)
        kv = None
        for j in range(NCH):
            kv = sc.post("vector", lambda v, j=j, edc=edc: v.tensor_scalar_mul(
                ktmp[:, j * C:(j + 1) * C], khat[:, j * C:(j + 1) * C], edc[:, j:j + 1]),
                waits=[(sC, P["ce"]), (sA, st8["stO_dma"])] if j == 0 else (),
                inc=sV if j == NCH - 1 else None)
        ktv = None
        for j in range(NCH):
            for (r0, r1, tp) in QUAD:
                ktv = sc.post("tensor", lambda t, j=j, r0=r0, r1=r1, tp=tp: t.transpose(
                    psT4[r0:r1, j * 64:(j + 1) * 64], ktmp[r0:r1, j * C:(j + 1) * C],
                    id2b[r0:r1, 0:64], tile_position=tp),
                    waits=[(sV, kv), (sC, evX0)] if (j == 0 and r0 == 0) else (),
                    inc=sT if (j == NCH - 1 and r0 == 64) else None)
        evKT = sc.post("scalar", lambda sy: sy.copy(
            _sv(LK, C, NCH, 2 * C, C), _sv(psT4, C, NCH, C)),
            waits=[(sT, ktv)], inc=sC)
        P.update(evT0=evT0, evLam=evLam, evU0=evU0, evX0=evX0, evKT=evKT)

    def mixer_core2(g, f, P, nxt):
        aq = P["aq"]
        edc = P["edc"]
        curT = T0m
        cur = 0
        last_solve_v = P["evU0"]
        last_solve_c = P["evKT"]
        for k in range(MIX_SOLVE_K):
            do_u, do_t = k < 4, k < 5
            if MIX_SOLVE_SUB == 'x':
                do_u = do_t = False
            elif MIX_SOLVE_SUB == 'xu':
                do_t = False
            nbatch = 1 + (1 if do_u else 0) + (1 if do_t else 0)
            last_mm = None

            def _is_last(b, j, r0):
                return b == nbatch - 1 and j == NCH - 1 and r0 == 64
            for j in range(NCH):
                for (r0, r1, tp) in QUAD:
                    last_mm = mm(psX[r0:r1, j * 128:(j + 1) * 128],
                                 curT[r0:r1, j * C:(j + 1) * C],
                                 XU[cur][r0:r1, j * 192: j * 192 + 128], True, True,
                                 waits=([(sV, last_solve_v)] +
                                        ([(sC, last_solve_c)] if last_solve_c else []))
                                 if (j == 0 and r0 == 0) else (),
                                 inc=sT if _is_last(0, j, r0) else None, tp=tp)
            if do_u:
                for j in range(NCH):
                    for (r0, r1, tp) in QUAD:
                        last_mm = mm(psU[r0:r1, j * C:(j + 1) * C],
                                     curT[r0:r1, j * C:(j + 1) * C],
                                     XU[cur][r0:r1, j * 192 + 128: j * 192 + 192],
                                     True, True, tp=tp,
                                     inc=sT if _is_last(1, j, r0) else None)
            if do_t:
                for j in range(NCH):
                    for (r0, r1, tp) in QUAD:
                        last_mm = mm(psB[r0:r1, j * C:(j + 1) * C],
                                     XU[cur][r0:r1, j * 192 + 128: j * 192 + 192],
                                     curT[r0:r1, j * C:(j + 1) * C], True, True, tp=tp,
                                     inc=sT if _is_last(nbatch - 1, j, r0) else None)
            nxt_b = 1 - cur
            ev1 = sc.post("vector", lambda v, cur=cur, nxt_b=nxt_b: v.tensor_add(
                _sv(XU[nxt_b], 128, NCH, 192), _sv(psX, 128, NCH, 128),
                _sv(XU[cur], 128, NCH, 192)), waits=[(sT, last_mm)], inc=sV)
            last_solve_v = ev1
            last_solve_c = 0
            if do_u:
                last_solve_c = sc.post("scalar", lambda sy, nxt_b=nxt_b: sy.copy(
                    _sv(XU[nxt_b], C, NCH, 192, 128), _sv(psU, C, NCH, C)),
                    waits=[(sT, last_mm)], inc=sC)
            if do_t:
                newT = Tt[k % 2]
                last_solve_c = sc.post("scalar", lambda sy, newT=newT: sy.copy(
                    newT[:, 0:GT], psB[:, 0:GT]), waits=[(sT, last_mm)] if not do_u else (),
                    inc=sC)
                curT = newT
            cur = nxt_b
        last_solve_c = max(last_solve_c, 1)

        def MV_ap(r0, r1, j):
            return XU[cur][r0:r1, j * 192: j * 192 + 64]

        def MA_ap(r0, r1, j):
            return XU[cur][r0:r1, j * 192 + 64: j * 192 + 128]

        wt = None
        wtw = [(sC, last_solve_c)]
        if nxt is not None:
            wtw.append((sV, nxt["v9"]))
        for j in range(NCH):
            for (r0, r1, tp) in QUAD:
                wt = mm(psW_[r0:r1, j * 128:(j + 1) * 128], MA_ap(r0, r1, j),
                        LK[r0:r1, j * 128:(j + 1) * 128], True, True,
                        waits=wtw if (j == 0 and r0 == 0) else (),
                        inc=sT if (j == NCH - 1 and r0 == 64) else None, tp=tp)
        jv = None
        jw = [(sV, last_solve_v)]
        if nxt is not None:
            jw.append((sV, nxt["v7"]))
        for j in range(NCH):
            for (r0, r1, tp) in QUAD:
                jv = mm(psU[r0:r1, j * C:(j + 1) * C],
                        LK[r0:r1, j * 128 + C: j * 128 + 128],
                        MV_ap(r0, r1, j), True, True,
                        waits=jw if (j == 0 and r0 == 0) else (),
                        inc=sT if (j == NCH - 1 and r0 == 64) else None, tp=tp)
        evW = sc.post("vector", lambda v, aq=aq: v.tensor_add(
            _sv(WTst, C, NCH, 128), _sv(psW_, C, NCH, 128),
            _sv(aq, C, NCH, 2 * C, C)),
            waits=[(sT, wt), (sA, st8["stWT_dma"])], inc=sV)
        evTc = None
        for j in range(NCH):
            evTc = sc.post("vector", lambda v, j=j, edc=edc: v.scalar_tensor_tensor(
                WTst[:, j * 128 + 64: j * 128 + 128], ident2[:, 0:64], edc[:, j:j + 1],
                psW_[:, j * 128 + 64: j * 128 + 128], ALU.mult, ALU.add), inc=sV)
        evJ = sc.post("vector", lambda v: v.tensor_copy(J_g[:, 0:GT], psU[:, 0:GT]),
                      waits=[(sT, jv)], inc=sV)
        st8["stWT_dma"] = sc.dma("sync", lambda s, g=g, f=f: s.dma_start(
            stWT_d[:, f, g * NCH:(g + 1) * NCH, :],
            WTst[:, :].rearrange("p (j m) -> p j m", j=NCH)), sA, waits=[(sV, evTc)])
        if "wtst" in taps:
            tap("wtst", (128, NCH * 128), BF16)
            st8["tap_dma"] = sc.dma("sync", lambda s: s.dma_start(
                tap_d["wtst"][:, :], WTst[:, :]), sA, waits=[(sV, evTc)])
        if "jg" in taps:
            tap("jg", (128, GT), F32)
            st8["tap_dma"] = sc.dma("sync", lambda s: s.dma_start(
                tap_d["jg"][:, :], J_g[:, :]), sA, waits=[(sV, evJ)])

        swv = evTc
        obl = None
        for j in range(NCH):
            for (r0, r1, tp) in QUAD:
                mm(psA[r0:r1, j * C:(j + 1) * C], MV_ap(r0, r1, j),
                   LK[r0:r1, j * 128: j * 128 + C], True, False,
                   waits=[(sV, swv)] if r0 == 0 else (), tp=tp, sgc=True)
            for (r0, r1, tp) in QUAD:
                obl = mm(psA[r0:r1, j * C:(j + 1) * C], scur[r0:r1, f * 64:(f + 1) * 64],
                         WTst[r0:r1, j * 128: j * 128 + 64], False, True,
                         inc=sT if r0 == 64 else None, tp=tp, sgc=True)
            smm = None
            for (r0, r1, tp) in QUAD:
                smm = mm(psX[r0:r1, 0:64], WTst[r0:r1, j * 128 + 64: j * 128 + 128],
                         scur[r0:r1, f * 64:(f + 1) * 64], True, True,
                         inc=sT if r0 == 64 else None, tp=tp)
            swv = sc.post("vector", lambda v, j=j, f=f: v.tensor_add(
                scur[:, f * 64:(f + 1) * 64], psX[:, 0:64], J_g[:, j * C:(j + 1) * C]),
                waits=[(sT, smm), (sV, evJ)], inc=sV)
        evO = sc.post("vector", lambda v: v.tensor_copy(Ost[:, 0:GT], psA[:, 0:GT]),
                      waits=[(sT, obl), (sV, swv), (sA, st8["stO_dma"])], inc=sV)
        st8["stO_dma"] = sc.dma("sync", lambda s, g=g, f=f: s.dma_start(
            stO_d[f, :, g * GT:(g + 1) * GT], Ost[:, 0:GT]), sA, waits=[(sV, evO)])
        if "op1" in taps:
            tap("op1", (NP, 128, TC), BF16)
            st8["stO_dma"] = sc.dma("sync", lambda s, g=g, f=f: s.dma_start(
                tap_d["op1"][f, :, g * GT:(g + 1) * GT], Ost[:, 0:GT]), sA)
        P["evW"] = evW
        P["evTc"] = evTc
        P["evJ"] = evJ

    def p1_group(g):
        g0 = g * GT
        src = x_d[:, :, g0:g0 + GT].rearrange("f p t -> p f t")
        ld = sc.dma("gpsimd", lambda gg: gg.dma_start(
            xgb[:, :].rearrange("p (f t) -> p f t", f=8), src),
            sW, waits=[(sV, st8["prev_group_v"]), (sA, st8["tap_dma"])])
        sq = sc.post("scalar", lambda s: s.activation(sqb[:, :], xgb[:, :], AF.Square),
                     waits=[(sW, ld)], inc=sC)
        stv = None
        for f8 in range(8):
            mm(psU[0:65, 0:GT], sel64[:, 0:65], sqb[:, f8 * GT:(f8 + 1) * GT],
               f8 == 0, False,
               waits=[(sC, sq), (sW, pw_ready), (sV, st8["prev_group_v"])] if f8 == 0 else ())
        for f8 in range(8):
            stv = mm(psU[0:65, 0:GT], ones65[:, 0:65], xgb[:, f8 * GT:(f8 + 1) * GT],
                     False, f8 == 7,
                     waits=(), inc=sT if f8 == 7 else None)
        v1 = sc.post("vector", lambda v: v.tensor_scalar_mul(
            r_a[0:1, 0:GT], psU[0:1, 0:GT], 1.0 / D), waits=[(sT, stv)], inc=sV)
        sc.post("vector", lambda v: v.tensor_copy(murt[0:1, 0:GT], r_a[0:1, 0:GT]))
        sc.post("vector", lambda v: v.tensor_scalar_mul(
            r_m2[0:1, 0:GT], psU[64:65, 0:GT], 1.0 / D))
        sc.post("vector", lambda v: v.tensor_mul(
            r_a[0:1, 0:GT], r_a[0:1, 0:GT], r_a[0:1, 0:GT]))
        v4 = sc.post("vector", lambda v: v.tensor_sub(
            r_m2[0:1, 0:GT], r_m2[0:1, 0:GT], r_a[0:1, 0:GT]), inc=sV)
        c1 = sc.post("scalar", lambda s: s.activation(
            r_b[0:1, 0:GT], r_m2[0:1, 0:GT], AF.Ln, bias=EPS), waits=[(sV, v4)], inc=sC)
        v5 = sc.post("scalar", lambda s: s.activation(
            murt[0:1, GT:2 * GT], r_b[0:1, 0:GT], AF.Exp, scale=-0.5), inc=sC)
        bc = None
        for hh in range(2):
            bc = mm(psW_[:, hh * GT:(hh + 1) * GT], ones1[0:1, :],
                    murt[0:1, hh * GT:(hh + 1) * GT], True, True,
                    waits=[(sC, v5), (sV, v1 + 1)] if hh == 0 else (),
                    inc=sT if hh == 1 else None)
        xv = None
        for f8 in range(8):
            sl = slice(f8 * GT, (f8 + 1) * GT)
            sc.post("vector", lambda v, sl=sl: v.tensor_sub(
                ebuf[:, 0:GT], xgb[:, sl], psW_[:, 0:GT]),
                waits=[(sT, bc)] if f8 == 0 else ())
            xv = sc.post("vector", lambda v, sl=sl: v.tensor_mul(
                xln[:, sl], ebuf[:, 0:GT], psW_[:, GT:2 * GT]), inc=sV)
        if tap("xln", (NG, 128, 8 * GT), BF16) is not None:
            st8["tap_dma"] = sc.dma("sync", lambda s, g=g: s.dma_start(
                tap_d["xln"][g, :, :], xln[:, :]), sA, waits=[(sV, xv)])

        ping = [psA, psB]
        prev_ev = {0: 0, 1: 0}
        nmm = 0
        for (wsb, dst, actf) in [(wq_sb, q_g, AF.Silu), (wk_sb, k_g, AF.Silu),
                                 (wv_sb, v_g, None)]:
            for cc in range(8):
                bank = ping[nmm % 2]
                w0 = ([(sC, prev_ev[nmm % 2])] if prev_ev[nmm % 2] else [])
                if nmm == 0:
                    w0 = w0 + [(sV, xv)]
                mv = None
                for kk in range(8):
                    mv = mm(bank[:, 0:GT],
                            wsb[:, cc * 1024 + kk * 128: cc * 1024 + (kk + 1) * 128],
                            xln[:, kk * GT:(kk + 1) * GT], kk == 0, kk == 7,
                            waits=w0 if kk == 0 else (), inc=sT if kk == 7 else None)
                if actf is not None:
                    ev = sc.post("scalar", lambda s, bank=bank, dst=dst, cc=cc, actf=actf:
                                 s.activation(dst[:, cc * GT:(cc + 1) * GT], bank[:, 0:GT], actf),
                                 waits=[(sT, mv)], inc=sC)
                else:
                    ev = sc.post("scalar", lambda s, bank=bank, dst=dst, cc=cc:
                                 s.copy(dst[:, cc * GT:(cc + 1) * GT], bank[:, 0:GT]),
                                 waits=[(sT, mv)], inc=sC)
                prev_ev[nmm % 2] = ev
                nmm += 1
        mv = None
        for kk in range(8):
            mv = mm(psA[0:64, 0:GT], wf1_sb[:, kk * 64:(kk + 1) * 64],
                    xln[:, kk * GT:(kk + 1) * GT], kk == 0, kk == 7,
                    waits=[(sC, prev_ev[0])] if kk == 0 else (), inc=sT if kk == 7 else None)
        evf1 = sc.post("scalar", lambda s: s.copy(f1t[0:64, 0:GT], psA[0:64, 0:GT]),
                       waits=[(sT, mv)], inc=sC)
        mv = None
        for kk in range(8):
            mv = mm(psB[0:64, 0:GT], wog1_sb[:, kk * 64:(kk + 1) * 64],
                    xln[:, kk * GT:(kk + 1) * GT], kk == 0, kk == 7,
                    waits=[(sC, prev_ev[1])] if kk == 0 else (), inc=sT if kk == 7 else None)
        evog = sc.post("scalar", lambda s, g=g: s.copy(
            gpre_sb[0:64, g * GT:(g + 1) * GT], psB[0:64, 0:GT]), waits=[(sT, mv)], inc=sC)
        mv = None
        for kk in range(8):
            mv = mm(psA[0:16, 0:GT], wgam_sb[:, kk * 16:(kk + 1) * 16],
                    xln[:, kk * GT:(kk + 1) * GT], kk == 0, kk == 7,
                    waits=[(sC, evf1)] if kk == 0 else (), inc=sT if kk == 7 else None)
        evgam = sc.post("scalar", lambda s: s.activation(gs_g[0:16, 0:GT], psA[0:16, 0:GT],
                                                         AF.Sigmoid), waits=[(sT, mv)], inc=sC)
        zv = evgam
        for cc in range(8):
            bank = ping[cc % 2]
            mv = mm(bank[:, 0:GT], wf2_sb[0:64, cc * 128:(cc + 1) * 128], f1t[0:64, 0:GT],
                    True, True, waits=[(sC, zv)], inc=sT)
            e1 = sc.post("scalar", lambda s, bank=bank: s.activation(
                ebuf[:, 0:GT], bank[:, 0:GT], AF.Exp, scale=-1.0), waits=[(sT, mv)], inc=sC)
            zv = sc.post("scalar", lambda s, cc=cc: s.activation(
                ngk_g[:, cc * GT:(cc + 1) * GT], ebuf[:, 0:GT], AF.Ln, bias=1.0), inc=sC)
        proj_done_c = sC.n

        if "q" in taps:
            for nm, t in [("q", q_g), ("k", k_g), ("v", v_g)]:
                tap(nm, (NG, 128, 8 * GT), BF16)
                st8["tap_dma"] = sc.dma("sync", lambda s, g=g, t=t, nm=nm: s.dma_start(
                    tap_d[nm][g, :, :], t[:, :]), sA, waits=[(sC, proj_done_c)])
        if "ngk" in taps:
            tap("ngk", (NG, 128, 8 * GT), F32)
            for cc in range(8):
                cv = sc.post("vector", lambda v, cc=cc: v.tensor_copy(
                    ebuf[:, 0:GT], ngk_g[:, cc * GT:(cc + 1) * GT]),
                    waits=[(sC, proj_done_c), (sA, st8["tap_dma"])], inc=sV)
                st8["tap_dma"] = sc.dma("sync", lambda s, g=g, cc=cc: s.dma_start(
                    tap_d["ngk"][g, :, cc * GT:(cc + 1) * GT], ebuf[:, 0:GT]),
                    sA, waits=[(sV, cv)])
        if "gam" in taps:
            tap("gam", (NG, 16, GT), BF16)
            st8["tap_dma"] = sc.dma("sync", lambda s, g=g: s.dma_start(
                tap_d["gam"][g, :, :], gs_g[:, :]), sA, waits=[(sC, proj_done_c)])
        if "gpre" in taps:
            tap("gpre", (NG, 64, GT), F32)
            st8["tap_dma"] = sc.dma("sync", lambda s, g=g: s.dma_start(
                tap_d["gpre"][g, :, :], gpre_sb[0:64, g * GT:(g + 1) * GT]),
                sA, waits=[(sC, proj_done_c)])

        if "mix" in phases:
            Pd = [None] * (MIX_PAIRS + 1)
            Pd[0] = mixer_prep(g, 0, {"psw_free": xv, "psu_free": xv, "psT4": 0})
            for f in range(MIX_PAIRS):
                Pd[f]["psT4_free"] = st8["psT4_free"]
                mixer_core1(g, f, Pd[f])
                st8["psT4_free"] = Pd[f]["evKT"]
                if f + 1 < MIX_PAIRS:
                    Pd[f + 1] = mixer_prep(g, f + 1, {"psw_free": Pd[f]["evLam"],
                                                      "psu_free": Pd[f]["evU0"]})
                mixer_core2(g, f, Pd[f], Pd[f + 1])
        st8["prev_group_v"] = sV.n

    if "p1" in phases:
        for g in range(MIX_GROUPS):
            sc.mark(f"p1g{g}")
            p1_group(g)

    p1stack.close()

    if "p2" in phases:
        p2stack = contextlib.ExitStack()

        def sb2(name, shape, dtype):
            return p2stack.enter_context(nc.sbuf_tensor(name, list(shape), dtype))

        def ps2(name, shape, dtype=F32):
            return p2stack.enter_context(nc.psum_tensor(name, list(shape), dtype))

        xg_sb = sb2("xg_sb", (128, 8 * GT), F32)
        sgath = sb2("sgath", (128, NCORE * NP * 64), F32)
        stil = sb2("stil", (128, NP * 64), BF16)
        wtbG = sb2("wtbG", (128, NP * NCH * 128), BF16)
        obG = sb2("obG", (128, NP * GT), BF16)
        ofin = sb2("ofin", (128, 8 * GT), F32)
        ogb = sb2("ogb", (128, 8 * GT), BF16)
        bbuf = sb2("bbuf", (128, 8 * GT), BF16)
        x1g = sb2("x1g", (128, 8 * GT), F32)
        gbuf = sb2("gbuf", (128, GT), F32)
        gpb = sb2("gpb", (64, GT), BF16)
        eb2 = sb2("eb2", (128, GT), F32)
        r2a = sb2("r2a", (1, GT), F32)
        r2b = sb2("r2b", (1, GT), F32)
        mur2 = sb2("mur2", (1, 2 * GT), BF16)
        psA2 = ps2("psA2", (128, 512), F32)
        psW2 = ps2("psW2", (128, 1024), F32)
        psU2 = ps2("psU2", (128, 512), F32)
        ppG = [ps2(f"pp{i}", (128, 512), F32) for i in range(4)]
        psB2 = psW2[:, GT:2 * GT]
        psO2 = psU2

        p1_end_v = sV.n
        sc.mark("coll")
        if "coll" in phases:
            cc1 = sc.dma("gpsimd", lambda g: g.dma_start(
                cc_in[:, :].rearrange("(f p) v -> p f v", p=128),
                scur[:, :].rearrange("p (f v) -> p f v", v=64)),
                sW, waits=[(sV, p1_end_v)])
            cg = sc.post("gpsimd", lambda g: g.collective_compute(
                "AllGather", ALU.bypass, replica_groups=[list(range(NCORE))],
                ins=[cc_in], outs=[cc_out]), waits=[(sW, cc1)], inc=sG)
            sc.dma("sync", lambda sy: sy.dma_start(
                sgath[:, :].rearrange("p (r f v) -> p r f v", f=NP, v=64),
                cc_out[:, :].rearrange("(r f p) v -> p r f v", f=NP, p=128)),
                sA, waits=[(sG, cg)])
            gath_a = sA.n
            stv2 = 0
            for f in range(NP):
                sc.post("vector", lambda v, f=f: v.tensor_scalar_mul(
                    eb2[:, 0:64], sgath[:, f * 64: f * 64 + 64], sel_sb[:, 0:1]),
                    waits=[(sA, gath_a)] if f == 0 else ())
                for r in range(1, NCORE):
                    sc.post("vector", lambda v, f=f, r=r: v.scalar_tensor_tensor(
                        eb2[:, 0:64], sgath[:, r * NP * 64 + f * 64: r * NP * 64 + f * 64 + 64],
                        sel_sb[:, r:r + 1], eb2[:, 0:64], ALU.mult, ALU.add))
                stv2 = sc.post("vector", lambda v, f=f: v.tensor_copy(
                    stil[:, f * 64:(f + 1) * 64], eb2[:, 0:64]), inc=sV)
        else:
            stv2 = sc.post("vector", lambda v: v.memset(stil[:, :], 0.0),
                           waits=[(sV, p1_end_v)], inc=sV)

        st2 = {"wtG_free": 0, "obG_free": 0, "of_ev": 0, "xg_dma": 0,
               "x1_dma": 0, "stil_v": stv2, "ep_c": 0, "ep_v": stv2, "ep_t": 0,
               "lnv_g": [0, 0, 0, 0]}
        for g in range(NG):
            sc.mark(f"p2g{g}")
            st2["xg_dma"] = sc.dma("sync", lambda sy, g=g: sy.dma_start(
                xg_sb[:, :].rearrange("p (f t) -> p f t", f=8),
                x_d[:, :, g * GT:(g + 1) * GT].rearrange("f p t -> p f t")),
                sA, waits=[(sV, st2["ep_v"]), (sT, st2["ep_t"])])
            wldG = sc.dma("sync", lambda sy, g=g: sy.dma_start(
                wtbG[:, :].rearrange("p (f j m) -> p f j m", f=NP, j=NCH),
                stWT_d[:, :, g * NCH:(g + 1) * NCH, :]),
                sA, waits=[(sT, st2["wtG_free"])])
            oldG = sc.dma("sync", lambda sy, g=g: sy.dma_start(
                obG[:, :].rearrange("p (f t) -> p f t", f=NP),
                stO_d[:, :, g * GT:(g + 1) * GT].rearrange("f p t -> p f t")),
                sA, waits=[(sV, st2["obG_free"])])
            sschain = st2["stil_v"]
            ocl = smm2 = None
            for j8 in range(NCH):
                for f in range(NP):
                    for (r0, r1, tp) in QUAD:
                        ocl = mm(psA2[r0:r1, f * 64:(f + 1) * 64],
                                 stil[r0:r1, f * 64:(f + 1) * 64],
                                 wtbG[r0:r1, (f * NCH + j8) * 128: (f * NCH + j8) * 128 + 64],
                                 True, True,
                                 waits=([(sA, wldG), (sV, sschain),
                                         (sV, st2["of_ev"])] if (f == 0 and r0 == 0)
                                        else ()), tp=tp)
                for f in range(NP):
                    for (r0, r1, tp) in QUAD:
                        smm2 = mm(psW2[r0:r1, GT + f * 64: GT + (f + 1) * 64],
                                  wtbG[r0:r1, (f * NCH + j8) * 128 + 64:
                                       (f * NCH + j8) * 128 + 128],
                                  stil[r0:r1, f * 64:(f + 1) * 64], True, True,
                                  inc=sT if (f == NP - 1 and r0 == 64) else None, tp=tp)
                ev = sc.post("vector", lambda v, j8=j8: v.tensor_add(
                    _sv(ofin, 64, NP, GT, j8 * 64), _sv(psA2, 64, NP, 64),
                    _sv(obG, 64, NP, GT, j8 * 64)),
                    waits=[(sT, smm2), (sA, oldG)], inc=sV)
                sschain = sc.post("vector", lambda v: v.tensor_copy(
                    stil[:, :], psW2[:, GT:GT + NP * 64]), inc=sV)
            st2["stil_v"] = sschain
            st2["wtG_free"] = sT.n
            st2["obG_free"] = sschain
            st2["of_ev"] = sschain
            gc = sc.post("scalar", lambda sy, g=g: sy.copy(
                gpb[0:64, 0:GT], gpre_sb[0:64, g * GT:(g + 1) * GT]),
                waits=[(sV, st2["of_ev"])], inc=sC)
            gmul = st2["of_ev"]
            gm_t = 0
            for f8 in range(8):
                half = f8 % 2
                gmm = mm(psW2[:, half * GT:(half + 1) * GT],
                         wog2_sb[0:64, f8 * 128:(f8 + 1) * 128], gpb[0:64, 0:GT],
                         True, True, waits=[(sC, gc), (sV, gmul)] if f8 < 2 else
                         [(sV, gmul)], inc=sT)
                gsig = sc.post("scalar", lambda sy, half=half: sy.activation(
                    gbuf[:, 0:GT], psW2[:, half * GT:(half + 1) * GT], AF.Sigmoid),
                    waits=[(sT, gmm)], inc=sC)
                gmul = sc.post("vector", lambda v, f8=f8: v.tensor_mul(
                    ofin[:, f8 * GT:(f8 + 1) * GT], ofin[:, f8 * GT:(f8 + 1) * GT],
                    gbuf[:, 0:GT]), waits=[(sC, gsig)], inc=sV)
            if P2_STAGE == "gate":
                st2["ep_v"], st2["ep_t"] = sV.n, sT.n
                continue
            sq2 = sc.post("scalar", lambda sy: sy.activation(bbuf[:, :], ofin[:, :],
                                                             AF.Square),
                          waits=[(sV, gmul)], inc=sC)
            cp2 = sc.post("scalar", lambda sy: sy.copy(ogb[:, :], ofin[:, :]), inc=sC)
            stv = None
            for f8 in range(8):
                mm(psU2[0:65, 0:GT], sel64[:, 0:65], bbuf[:, f8 * GT:(f8 + 1) * GT],
                   f8 == 0, False, waits=[(sC, sq2), (sV, gmul)] if f8 == 0 else ())
            for f8 in range(8):
                stv = mm(psU2[0:65, 0:GT], ones65[:, 0:65], ogb[:, f8 * GT:(f8 + 1) * GT],
                         False, f8 == 7, waits=[(sC, cp2)] if f8 == 0 else (),
                         inc=sT if f8 == 7 else None)
            v1 = sc.post("vector", lambda v: v.tensor_scalar_mul(
                r2a[0:1, 0:GT], psU2[0:1, 0:GT], 1.0 / D), waits=[(sT, stv)], inc=sV)
            sc.post("vector", lambda v: v.tensor_copy(mur2[0:1, 0:GT], r2a[0:1, 0:GT]))
            sc.post("vector", lambda v: v.tensor_scalar_mul(
                r2b[0:1, 0:GT], psU2[64:65, 0:GT], 1.0 / D))
            sc.post("vector", lambda v: v.tensor_mul(
                r2a[0:1, 0:GT], r2a[0:1, 0:GT], r2a[0:1, 0:GT]))
            v4 = sc.post("vector", lambda v: v.tensor_sub(
                r2b[0:1, 0:GT], r2b[0:1, 0:GT], r2a[0:1, 0:GT]), inc=sV)
            c1 = sc.post("scalar", lambda sy: sy.activation(
                r2a[0:1, 0:GT], r2b[0:1, 0:GT], AF.Ln, bias=EPS), waits=[(sV, v4)], inc=sC)
            v5 = sc.post("scalar", lambda sy: sy.activation(
                mur2[0:1, GT:2 * GT], r2a[0:1, 0:GT], AF.Exp, scale=-0.5), inc=sC)
            bc = None
            for hh in range(2):
                bc = mm(psW2[:, hh * GT:(hh + 1) * GT], ones1[0:1, :],
                        mur2[0:1, hh * GT:(hh + 1) * GT], True, True,
                        waits=[(sC, v5), (sV, v4)] if hh == 0 else (),
                        inc=sT if hh == 1 else None)
            onv = None
            for f8 in range(8):
                sl = slice(f8 * GT, (f8 + 1) * GT)
                sc.post("vector", lambda v, sl=sl: v.tensor_sub(
                    eb2[:, 0:GT], ofin[:, sl], psW2[:, 0:GT]),
                    waits=[(sT, bc)] if f8 == 0 else ())
                onv = sc.post("vector", lambda v, sl=sl: v.tensor_mul(
                    ogb[:, sl], eb2[:, 0:GT], psW2[:, GT:2 * GT]), inc=sV)
            if P2_STAGE == "stats":
                st2["ep_v"], st2["ep_t"] = sV.n, sT.n
                continue
            prev_ev = 0
            x1v = None
            for cc in range(8):
                mv = None
                for kk in range(8):
                    mv = mm(psU2[:, 0:GT],
                            wo_sb[:, cc * 1024 + kk * 128: cc * 1024 + (kk + 1) * 128],
                            ogb[:, kk * GT:(kk + 1) * GT], kk == 0, kk == 7,
                            waits=([(sV, onv), (sA, st2["xg_dma"])] if cc == 0 else
                                   [(sV, prev_ev)]) if kk == 0 else (),
                            inc=sT if kk == 7 else None)
                x1v = sc.post("vector", lambda v, cc=cc: v.tensor_add(
                    x1g[:, cc * GT:(cc + 1) * GT], psU2[:, 0:GT],
                    xg_sb[:, cc * GT:(cc + 1) * GT]), waits=[(sT, mv)], inc=sV)
                prev_ev = x1v
            st2["x1_dma"] = sc.dma("sync", lambda sy, g=g: sy.dma_start(
                x1_d[:, :, g * GT:(g + 1) * GT].rearrange("f p t -> p f t"),
                x1g[:, :].rearrange("p (f t) -> p f t", f=8)),
                sA, waits=[(sV, x1v)])
            if "x1" in taps:
                tap("x1", (NG, 128, 8 * GT), F32)
                st8["tap_dma"] = sc.dma("sync", lambda sy, g=g: sy.dma_start(
                    tap_d["x1"][g, :, :], x1g[:, :]), sA, waits=[(sV, x1v)])
            if P2_STAGE == "wo":
                st2["ep_v"], st2["ep_t"] = sV.n, sT.n
                continue
            sq3 = sc.post("scalar", lambda sy: sy.activation(bbuf[:, :], x1g[:, :],
                                                             AF.Square),
                          waits=[(sV, x1v)], inc=sC)
            cp3 = sc.post("scalar", lambda sy: sy.copy(ogb[:, :], x1g[:, :]), inc=sC)
            stv = None
            for f8 in range(8):
                mm(psU2[0:65, 0:GT], sel64[:, 0:65], bbuf[:, f8 * GT:(f8 + 1) * GT],
                   f8 == 0, False, waits=[(sC, sq3)] if f8 == 0 else ())
            for f8 in range(8):
                stv = mm(psU2[0:65, 0:GT], ones65[:, 0:65], ogb[:, f8 * GT:(f8 + 1) * GT],
                         False, f8 == 7, waits=[(sC, cp3)] if f8 == 0 else (),
                         inc=sT if f8 == 7 else None)
            v1 = sc.post("vector", lambda v: v.tensor_scalar_mul(
                r2a[0:1, 0:GT], psU2[0:1, 0:GT], 1.0 / D), waits=[(sT, stv)], inc=sV)
            sc.post("vector", lambda v: v.tensor_copy(mur2[0:1, 0:GT], r2a[0:1, 0:GT]))
            sc.post("vector", lambda v: v.tensor_scalar_mul(
                r2b[0:1, 0:GT], psU2[64:65, 0:GT], 1.0 / D))
            sc.post("vector", lambda v: v.tensor_mul(
                r2a[0:1, 0:GT], r2a[0:1, 0:GT], r2a[0:1, 0:GT]))
            v4 = sc.post("vector", lambda v: v.tensor_sub(
                r2b[0:1, 0:GT], r2b[0:1, 0:GT], r2a[0:1, 0:GT]), inc=sV)
            c1 = sc.post("scalar", lambda sy: sy.activation(
                r2a[0:1, 0:GT], r2b[0:1, 0:GT], AF.Ln, bias=EPS), waits=[(sV, v4)], inc=sC)
            v5 = sc.post("scalar", lambda sy: sy.activation(
                mur2[0:1, GT:2 * GT], r2a[0:1, 0:GT], AF.Exp, scale=-0.5), inc=sC)
            bc = None
            for hh in range(2):
                bc = mm(psW2[:, hh * GT:(hh + 1) * GT], ones1[0:1, :],
                        mur2[0:1, hh * GT:(hh + 1) * GT], True, True,
                        waits=[(sC, v5), (sV, v4)] if hh == 0 else (),
                        inc=sT if hh == 1 else None)
            lnv = None
            for f8 in range(8):
                sl = slice(f8 * GT, (f8 + 1) * GT)
                sc.post("vector", lambda v, sl=sl: v.tensor_sub(
                    eb2[:, 0:GT], x1g[:, sl], psW2[:, 0:GT]),
                    waits=[(sT, bc)] if f8 == 0 else ())
                lnv = sc.post("vector", lambda v, f8=f8, g=g: v.tensor_mul(
                    x1ln_sb[:, f8 * TC + g * GT: f8 * TC + (g + 1) * GT],
                    eb2[:, 0:GT], psW2[:, GT:2 * GT]), inc=sV)
            st2["ep_v"] = lnv
            st2["ep_t"] = sT.n
            st2["lnv_g"][g] = lnv
        if "x1ln" in taps:
            tap("x1ln", (128, 8 * TC), BF16)
            st8["tap_dma"] = sc.dma("sync", lambda sy: sy.dma_start(
                tap_d["x1ln"][:, :], x1ln_sb[:, :]), sA, waits=[(sV, st2["ep_v"])])
        p2_end_v, p2_end_t, p2_end_c = sV.n, sT.n, sC.n
        p2stack.close()
        p12stack.close()

    if "p3" in phases:
        p3stack = contextlib.ExitStack()

        def sb3(name, shape, dtype):
            return p3stack.enter_context(nc.sbuf_tensor(name, list(shape), dtype))

        def ps3_(name, shape, dtype=F32):
            return p3stack.enter_context(nc.psum_tensor(name, list(shape), dtype))

        HT = TC // 2
        NS = HT // GT
        g1b = sb3("g1b", (128, 32 * HT), BF16)
        gHb = sb3("gHb", (128, 32 * HT), BF16)
        wsa = [sb3(f"wsa{i}", (128, 4096), BF16) for i in range(2)]
        x1r = [sb3(f"x1r{i}", (128, GT), F32) for i in range(2)]
        outbd = [sb3(f"outb{i}", (128, GT), F32) for i in range(2)]
        sc1_sb = sb3("sc1_sb", (128, 32), F32)
        sc2_sb = sb3("sc2_sb", (128, 32), F32)
        sch1_sb = sb3("sch1_sb", (128, 32), F32)
        sch2_sb = sb3("sch2_sb", (128, 32), F32)
        sc3_sb = sb3("sc3_sb", (128, 8), F32)
        teb = sb3("teb", (128, GT), F32)
        pp = ppG

        p2v = p2_end_v if "p2" in phases else sV.n
        first = [(sV, p2v)]
        for nm, t in [("s1", sc1_sb), ("s2", sc2_sb), ("sh1", sch1_sb),
                      ("sh2", sch2_sb), ("s3", sc3_sb)]:
            sc.dma("sync", lambda sy, t=t, nm=nm: sy.dma_start(t[:, :], din[nm][:, :]),
                   sA, waits=first)
            first = []
        sc_ld = sA.n
        st3 = {"wd": [0, 0], "wd2": [0, 0], "pev": [0, 0, 0, 0], "pevC": [0, 0, 0, 0],
               "out_dma": 0, "rfree": [0, 0], "ob_dma": [0, 0],
               "bar_v": (p2_end_v if "p2" in phases else sV.n),
               "bar_c": (p2_end_c if "p2" in phases else sC.n)}

        def halves():
            return range(P3_HALVES)

        for h in halves():
            th = h * HT
            if "p2" in phases:
                st3["half_v"] = max(st2["lnv_g"][2 * h], st2["lnv_g"][2 * h + 1])
            else:
                st3["half_v"] = st3["bar_v"]
            sc.mark(f"p3A{h}")
            for c in range(32):
                par = c % 2
                w1l = sc.dma("gpsimd", lambda g, c=c, par=par: g.dma_start(
                    wsb_[par][:, 0:1024], din["w1"][:, c * 1024:(c + 1) * 1024]),
                    sW, waits=[(sT, st3["wd"][par])] + ([(sV, p2v)] if (h == 0 and c < 2) else []))
                w2l = sc.dma("gpsimd", lambda g, c=c, par=par: g.dma_start(
                    wsb_[par][:, 1024:2048], din["w2"][:, c * 1024:(c + 1) * 1024]),
                    sW, waits=[])
                for sp in range(NS):
                    pa, pb = pp[2 * par], pp[2 * par + 1]
                    tsl = slice(th + sp * GT, th + (sp + 1) * GT)
                    mv1 = mv2 = None
                    for kk in range(8):
                        w0 = []
                        if kk == 0:
                            w0 = [(sW, w2l), (sC, st3["pevC"][2 * par]),
                                  (sV, st3["pev"][2 * par]), (sA, sc_ld),
                                  (sV, st3["half_v"])]
                        mv1 = mm(pa[:, 0:GT], wsb_[par][:, kk * 128: kk * 128 + 128],
                                 x1ln_sb[:, kk * TC + tsl.start: kk * TC + tsl.stop],
                                 kk == 0, kk == 7, waits=w0,
                                 inc=sT if kk == 7 else None)
                    for kk in range(8):
                        mv2 = mm(pb[:, 0:GT], wsb_[par][:, 1024 + kk * 128: 1024 + kk * 128 + 128],
                                 x1ln_sb[:, kk * TC + tsl.start: kk * TC + tsl.stop],
                                 kk == 0, kk == 7,
                                 waits=[(sC, st3["pevC"][2 * par + 1]),
                                        (sV, st3["pev"][2 * par + 1])] if kk == 0 else (),
                                 inc=sT if kk == 7 else None)
                    e1 = sc.post("scalar", lambda sy, pa=pa, c=c: sy.activation(
                        teb[:, 0:GT], pa[:, 0:GT], AF.Silu, scale=sc1_sb[:, c:c + 1]),
                        waits=[(sT, mv1)], inc=sC)
                    st3["pevC"][2 * par] = e1
                    gv = sc.post("vector", lambda v, pb=pb, c=c, sp=sp, h=h: v.scalar_tensor_tensor(
                        g1b[:, c * HT + sp * GT: c * HT + (sp + 1) * GT],
                        pb[:, 0:GT], sc2_sb[:, c:c + 1], teb[:, 0:GT],
                        ALU.mult, ALU.mult), waits=[(sT, mv2), (sC, e1)], inc=sV)
                    st3["pev"][2 * par + 1] = gv
                st3["wd"][par] = sT.n
            st3["bar_v"], st3["bar_c"] = sV.n, sC.n
            for (stage, wnm, ssb, srcb, dstb) in [("B", "h1", sch1_sb, g1b, gHb),
                                                  ("C", "h2", sch2_sb, gHb, g1b)]:
                sc.mark(f"p3{stage}{h}")
                for c in range(32):
                    par = c % 2
                    wl = sc.dma("gpsimd", lambda g, c=c, par=par, wnm=wnm: g.dma_start(
                        wsa[par][:, :], din[wnm][:, c * 4096:(c + 1) * 4096]),
                        sW, waits=[(sT, st3["wd2"][par])])
                    pa, pb = pp[2 * par], pp[2 * par + 1]
                    mv = None
                    for kk in range(32):
                        w0 = ()
                        if kk == 0:
                            w0 = [(sW, wl), (sC, st3["pevC"][2 * par]),
                                  (sC, st3["pevC"][2 * par + 1]),
                                  (sV, st3["bar_v"]), (sC, st3["bar_c"])]
                        for sp in range(NS):
                            mv = mm(pa[:, 0:GT] if sp == 0 else pb[:, 0:GT],
                                    wsa[par][:, kk * 128:(kk + 1) * 128],
                                    srcb[:, kk * HT + sp * GT: kk * HT + (sp + 1) * GT],
                                    kk == 0, kk == 31, waits=w0 if sp == 0 else (),
                                    inc=sT if (kk == 31 and sp == NS - 1) else None)
                            w0 = ()
                    for sp in range(NS):
                        px = pa if sp == 0 else pb
                        ge = sc.post("scalar", lambda sy, px=px, c=c, sp=sp, ssb=ssb,
                                     dstb=dstb: sy.activation(
                            dstb[:, c * HT + sp * GT: c * HT + (sp + 1) * GT],
                            px[:, 0:GT], AF.Gelu, scale=ssb[:, c:c + 1]),
                            waits=[(sT, mv)] if sp == 0 else (), inc=sC)
                        st3["pevC"][2 * par + sp] = ge
                    st3["wd2"][par] = sT.n
                st3["bar_v"], st3["bar_c"] = sV.n, sC.n
            sc.mark(f"p3D{h}")
            for c in range(8):
                par = c % 2
                w3l = sc.dma("gpsimd", lambda g, c=c, par=par: g.dma_start(
                    wsa[par][:, :], din["w3"][:, c * 4096:(c + 1) * 4096]),
                    sW, waits=[(sT, st3["wd2"][par])])
                x1ls = []
                for sp in range(NS):
                    rp = sp
                    x1ls.append(sc.dma("sync", lambda sy, c=c, sp=sp, th=th, rp=rp:
                                       sy.dma_start(
                        x1r[rp][:, :], x1_d[c, :, th + sp * GT: th + (sp + 1) * GT]),
                        sA, waits=[(sV, st3["rfree"][rp])]))
                pa, pb = pp[2 * par], pp[2 * par + 1]
                mv = None
                for kk in range(32):
                    w0 = ()
                    if kk == 0:
                        w0 = [(sW, w3l), (sV, st3["pev"][2 * par]),
                              (sV, st3["pev"][2 * par + 1]),
                              (sV, st3["bar_v"]), (sC, st3["bar_c"])]
                    for sp in range(NS):
                        mv = mm(pa[:, 0:GT] if sp == 0 else pb[:, 0:GT],
                                wsa[par][:, kk * 128:(kk + 1) * 128],
                                g1b[:, kk * HT + sp * GT: kk * HT + (sp + 1) * GT],
                                kk == 0, kk == 31, waits=w0 if sp == 0 else (),
                                inc=sT if (kk == 31 and sp == NS - 1) else None)
                        w0 = ()
                for sp in range(NS):
                    px = pa if sp == 0 else pb
                    rp = sp
                    op = sp
                    ov = sc.post("vector", lambda v, px=px, c=c, rp=rp, op=op:
                                 v.scalar_tensor_tensor(
                        outbd[op][:, 0:GT], px[:, 0:GT], sc3_sb[:, c:c + 1],
                        x1r[rp][:, :], ALU.mult, ALU.add),
                        waits=[(sT, mv), (sA, x1ls[sp]), (sS, st3["ob_dma"][op])], inc=sV)
                    st3["pev"][2 * par + sp] = ov
                    st3["rfree"][rp] = ov
                    st3["ob_dma"][op] = sc.dma("scalar", lambda sy, c=c, sp=sp, th=th,
                                               op=op: sy.dma_start(
                        out_d[c, :, th + sp * GT: th + (sp + 1) * GT], outbd[op][:, 0:GT]),
                        sS, waits=[(sV, ov)])
                st3["wd2"][par] = sT.n
            st3["wd2"][0] = st3["wd2"][1] = sT.n
            st3["bar_v"], st3["bar_c"] = sV.n, sC.n
        sc.mark(None)
        p3stack.close()
    else:
        pass


    if "send" in taps:
        tap("send", (128, NP * 64), F32)
        sbuf_send = sb("sbuf_send", (128, NP * 64), F32)
        cvv = sc.post("vector", lambda v: v.tensor_copy(sbuf_send[:, :], scur[:, :]),
                      waits=[(sV, st8["prev_group_v"])], inc=sV)
        sc.dma("sync", lambda s: s.dma_start(tap_d["send"][:, :], sbuf_send[:, :]),
               sA, waits=[(sV, cvv)])

    sc.post("sync", None, waits=[(sA, sA.n), (sW, sW.n), (sS, sS.n)])

    with nc.Block() as block:
        for s in sc.sems.values():
            s.h = stack.enter_context(nc.semaphore(s.name))

        @block.sync
        def _(e):
            sc.replay("sync", e, nc)

        @block.tensor
        def _(e):
            sc.replay("tensor", e, nc)

        @block.vector
        def _(e):
            sc.replay("vector", e, nc)

        @block.scalar
        def _(e):
            sc.replay("scalar", e, nc)

        @block.gpsimd
        def _(e):
            sc.replay("gpsimd", e, nc)

    stack.close()
    return nc, tap_d



def kernel(**inputs):
    nc, tap_d = build()
    in_maps = [_prep_inputs(i, inputs) for i in range(NCORE)]
    res = run_bass_kernel_spmd(nc, in_maps, core_ids=list(range(NCORE)))
    out = np.empty((B, T, D), np.float32)
    for i in range(NCORE):
        b, half = i // 2, i % 2
        o = res.results[i]["out"].reshape(8 * 128, TC)
        out[b, half * TC:(half + 1) * TC, :] = o.T
    return out


def run_taps(inputs, taps, phases=None, trace=False):
    nc, tap_d = build(taps=set(taps), phases=set(phases or PHASES))
    in_maps = [_prep_inputs(i, inputs) for i in range(NCORE)]
    res = run_bass_kernel_spmd(nc, in_maps, core_ids=list(range(NCORE)), trace=trace)
    return res


# version 10
# speedup vs baseline: 1.0332x; 1.0139x over previous
import contextlib
import numpy as np
import ml_dtypes
import concourse.bass as bass
import concourse.mybir as mybir
from concourse.ap import AP
from concourse.bass_utils import run_bass_kernel_spmd

F32 = mybir.dt.float32
F16 = mybir.dt.float16
BF16 = mybir.dt.bfloat16
AF = mybir.ActivationFunctionType
ALU = mybir.AluOpType

B, T, D = 4, 4096, 1024
H, HD = 16, 64
DH = 4096
NCORE = 8
TC = 2048
NG = 4
GT = 512
C = 64
NCH = GT // C
NP = 8
EPS = 1e-5

TAPS = set()
PHASES = {"p1", "mix", "coll", "p2", "p3"}
MIX_GROUPS = NG
MIX_PAIRS = NP
MIX_STAGE = "all"
MIX_SOLVE_K = 6
MIX_SOLVE_SUB = "full"
DBG_J = 0
P3_HALVES = 2
P2_STAGE = "all"
P2_SWEEP_MM = 3



def _lhsT_blocks(w_in_out, kchunks, cchunks):
    a = w_in_out.reshape(kchunks, 128, cchunks, 128).transpose(1, 2, 0, 3)
    return np.ascontiguousarray(a.reshape(128, cchunks * kchunks * 128))


def _consts():
    c = {}
    c["c_ones1c"] = np.ones((128, 1), np.float32)
    s64 = np.zeros((128, 65), np.float32); s64[:, 64] = 1.0
    c["c_sel64"] = s64
    o65 = np.zeros((128, 65), np.float32); o65[:, 0] = 1.0
    c["c_ones65"] = o65
    sh = np.zeros((128, 65), np.float32)
    sh[:64, 0] = 1.0; sh[64:, 64] = 1.0
    c["c_selH65"] = sh
    c["c_ones1"] = np.ones((1, 128), np.float32)
    hA = np.zeros((1, 128), np.float32); hA[0, :64] = 1.0
    hB = np.zeros((1, 128), np.float32); hB[0, 64:] = 1.0
    c["c_hA"], c["c_hB"] = hA, hB
    sp = np.zeros((16, NP * 128), np.float32)
    for f in range(NP):
        sp[2 * f, f * 128: f * 128 + 64] = -1.0
        sp[2 * f + 1, f * 128 + 64: f * 128 + 128] = -1.0
    c["c_selPair"] = sp
    s = np.arange(64)
    mU = (s[:, None] < s[None, :]).astype(np.float32)
    mI = (s[:, None] <= s[None, :]).astype(np.float32)
    mL = (s[None, :] < s[:, None]).astype(np.float32)
    c["c_maskU8"] = np.tile(np.tile(mU, (2, 1)), (1, NCH))
    c["c_maskI8"] = np.tile(np.tile(mI, (2, 1)), (1, NCH))
    c["c_maskL8"] = np.tile(np.tile(mL, (2, 1)), (1, NCH))
    c["c_ident2"] = np.tile(np.eye(64, dtype=np.float32), (2, 1))
    c["c_ident"] = np.eye(128, dtype=np.float32)
    c["c_eps"] = np.full((128, 1), EPS, np.float32)
    c["c_id2b"] = np.tile(np.eye(64, dtype=np.float32), (2, 1))
    return c


def _prep_inputs(core, inp):
    b, half = core // 2, core % 2
    t0 = half * TC
    d = {}
    xs = np.asarray(inp["L0_attn_x"])[b, t0:t0 + TC, :]
    d["xfm"] = np.ascontiguousarray(xs.T.reshape(8, 128, TC))
    d["wq"] = _lhsT_blocks(np.asarray(inp["L0_attn_Wq"]), 8, 8)
    d["wk"] = _lhsT_blocks(np.asarray(inp["L0_attn_Wk"]), 8, 8)
    d["wv"] = _lhsT_blocks(np.asarray(inp["L0_attn_Wv"]), 8, 8)
    d["wo"] = _lhsT_blocks(np.asarray(inp["L0_attn_Wo"]), 8, 8)
    d["wf1"] = np.ascontiguousarray(np.asarray(inp["L0_attn_Wf1"]).reshape(8, 128, 64)
                                    .transpose(1, 0, 2).reshape(128, 512))
    d["wf2"] = np.ascontiguousarray(np.asarray(inp["L0_attn_Wf2"]))
    d["wog1"] = np.ascontiguousarray(np.asarray(inp["L0_attn_Wog1"]).reshape(8, 128, 64)
                                     .transpose(1, 0, 2).reshape(128, 512))
    d["wog2"] = np.ascontiguousarray(np.asarray(inp["L0_attn_Wog2"]))
    d["wgam"] = np.ascontiguousarray(np.asarray(inp["L0_attn_Wgamma"]).reshape(8, 128, 16)
                                     .transpose(1, 0, 2).reshape(128, 128))
    d["nw"] = np.ascontiguousarray(np.asarray(inp["L0_attn_norm_w"]).reshape(8, 128).T)
    d["w1"] = _lhsT_blocks(np.asarray(inp["L0_ffn_w1_w"]).T, 8, 32)
    d["w2"] = _lhsT_blocks(np.asarray(inp["L0_ffn_w2_w"]).T, 8, 32)
    d["h1"] = _lhsT_blocks(np.asarray(inp["L0_ffn_h1_w"]).T, 32, 32)
    d["h2"] = _lhsT_blocks(np.asarray(inp["L0_ffn_h2_w"]).T, 32, 32)
    d["w3"] = _lhsT_blocks(np.asarray(inp["L0_ffn_w3_w"]).T, 32, 8)
    d["s1"] = np.ascontiguousarray(np.asarray(inp["L0_ffn_w1_s"]).reshape(32, 128).T)
    d["s2"] = np.ascontiguousarray(np.asarray(inp["L0_ffn_w2_s"]).reshape(32, 128).T)
    d["sh1"] = np.ascontiguousarray(np.asarray(inp["L0_ffn_h1_s"]).reshape(32, 128).T)
    d["sh2"] = np.ascontiguousarray(np.asarray(inp["L0_ffn_h2_s"]).reshape(32, 128).T)
    d["s3"] = np.ascontiguousarray(np.asarray(inp["L0_ffn_w3_s"]).reshape(8, 128).T)
    sel = np.zeros((128, NCORE), np.float32)
    if half == 1:
        sel[:, core - 1] = 1.0
    d["sel"] = sel
    d.update(_consts())
    return d



ENGINES = ("tensor", "vector", "scalar", "gpsimd", "sync")


class _Sem:
    __slots__ = ("name", "n", "h")

    def __init__(self, name):
        self.name, self.n, self.h = name, 0, None


class Sched:
    def __init__(self):
        self.streams = {e: [] for e in ENGINES}
        self.sems = {}

    def sem(self, name):
        if name not in self.sems:
            self.sems[name] = _Sem(name)
        return self.sems[name]

    def dma(self, eng, fn, qsem, waits=()):
        w = tuple(waits) + ((qsem, qsem.n),)
        return self.post(eng, fn, waits=w, inc=qsem, amt=16)

    def post(self, eng, fn, waits=(), inc=None, amt=1):
        val = None
        if inc is not None:
            inc.n += amt
            val = inc.n
        self.streams[eng].append((tuple(waits), fn, inc, amt))
        return val

    def mark(self, name):
        for e in ENGINES:
            self.streams[e].append(("MARK", name, None, None))

    def replay(self, eng_name, eng, nc=None):
        import contextlib as _ctx
        cur = None
        for entry in self.streams[eng_name]:
            if entry[0] == "MARK":
                if cur is not None and nc is not None:
                    nc.leave_named_scope(cur[0], cur[1], False)
                    cur = None
                if nc is not None and entry[1] is not None:
                    sid, _ = nc.enter_named_scope(entry[1], False)
                    cur = (entry[1], sid)
                continue
            waits, fn, inc, amt = entry
            for (s, v) in waits:
                if v > 0:
                    eng.wait_ge(s.h, v)
            inst = fn(eng) if fn is not None else None
            if inc is not None:
                if inst is None:
                    inst = eng.nop()
                inst.then_inc(inc.h, amt)
        if cur is not None and nc is not None:
            nc.leave_named_scope(cur[0], cur[1], False)


def _sv(tile, width, count, stride, offset=0):
    base = tile[:, 0:1]
    return AP(base.tensor, base.offset + offset,
              [list(base.ap[0]), [stride, count], [1, width]])



def build(taps=TAPS, phases=PHASES):
    nc = bass.Bass("TRN2", target_bir_lowering=False)
    sc = Sched()
    sA = sc.sem("sA")
    sW = sc.sem("sW")
    sT = sc.sem("sT")
    sV = sc.sem("sV")
    sC = sc.sem("sC")
    sG = sc.sem("sG")
    sS = sc.sem("sS")

    din = {}

    def dram_in(name, shape):
        din[name] = nc.dram_tensor(name, list(shape), F32, kind="ExternalInput").ap()
        return din[name]

    x_d = dram_in("xfm", (8, 128, TC))
    for nm, shp in [("wq", (128, 8192)), ("wk", (128, 8192)), ("wv", (128, 8192)),
                    ("wo", (128, 8192)), ("wf1", (128, 512)), ("wf2", (64, 1024)),
                    ("wog1", (128, 512)), ("wog2", (64, 1024)), ("wgam", (128, 128)),
                    ("nw", (128, 8)),
                    ("w1", (128, 32768)), ("w2", (128, 32768)),
                    ("h1", (128, 131072)), ("h2", (128, 131072)), ("w3", (128, 32768)),
                    ("s1", (128, 32)), ("s2", (128, 32)), ("sh1", (128, 32)),
                    ("sh2", (128, 32)), ("s3", (128, 8)), ("sel", (128, 8)),
                    ("c_ones1c", (128, 1)), ("c_sel64", (128, 65)), ("c_ones65", (128, 65)), ("c_selH65", (128, 65)),
                    ("c_ones1", (1, 128)), ("c_hA", (1, 128)), ("c_hB", (1, 128)),
                    ("c_selPair", (16, NP * 128)),
                    ("c_maskU8", (128, 512)), ("c_maskI8", (128, 512)),
                    ("c_maskL8", (128, 512)),
                    ("c_ident2", (128, 64)), ("c_ident", (128, 128)), ("c_eps", (128, 1)), ("c_id2b", (128, 64))]:
        dram_in(nm, shp)

    out_d = nc.dram_tensor("out", [8, 128, TC], F32, kind="ExternalOutput").ap()
    tap_d = {}

    def tap(name, shape, dtype=F32):
        if name in taps and name not in tap_d:
            tap_d[name] = nc.dram_tensor("tap_" + name, list(shape), dtype,
                                         kind="ExternalOutput").ap()
        return tap_d.get(name)

    x1_d = nc.dram_tensor("x1buf", [8, 128, TC], F32).ap()
    stWT_d = nc.dram_tensor("stWT", [128, NP, TC // C, 128], BF16).ap()
    stO_d = nc.dram_tensor("stO", [NP, 128, TC], BF16).ap()
    cc_in = nc.dram_tensor("cc_in", [NP * 128, 64], F32).ap()
    cc_out = nc.dram_tensor("cc_out", [NCORE * NP * 128, 64], F32).ap()

    stack = contextlib.ExitStack()
    p12stack = contextlib.ExitStack()
    p1stack = contextlib.ExitStack()

    def sb(name, shape, dtype):
        return stack.enter_context(nc.sbuf_tensor(name, list(shape), dtype))

    def sb1(name, shape, dtype):
        return p1stack.enter_context(nc.sbuf_tensor(name, list(shape), dtype))

    def ps(name, shape, dtype=F32):
        return stack.enter_context(nc.psum_tensor(name, list(shape), dtype))

    def ps1(name, shape, dtype=F32):
        return p1stack.enter_context(nc.psum_tensor(name, list(shape), dtype))

    def sb12(name, shape, dtype):
        return p12stack.enter_context(nc.sbuf_tensor(name, list(shape), dtype))

    def ps12(name, shape, dtype=F32):
        return p12stack.enter_context(nc.psum_tensor(name, list(shape), dtype))

    cst = {}
    for nm, shp, dt in [("c_ones1c", (128, 1), BF16), ("c_sel64", (128, 65), BF16),
                        ("c_ones65", (128, 65), BF16),
                        ("c_selH65", (128, 65), BF16),
                        ("c_ones1", (1, 128), BF16), ("c_hA", (1, 128), BF16),
                        ("c_hB", (1, 128), BF16), ("c_selPair", (16, NP * 128), BF16),
                        ("c_maskU8", (128, 512), F32), ("c_maskI8", (128, 512), F32),
                        ("c_maskL8", (128, 512), F32), ("c_ident2", (128, 64), F32),
                        ("c_ident", (128, 128), BF16), ("c_eps", (128, 1), F32),
                        ("c_id2b", (128, 64), BF16)]:
        cst[nm] = sb("sb_" + nm, shp, dt)
    scur = sb("scur", (128, NP * 64), BF16)
    zeros_sb = sb("zeros_sb", (128, C), F32)
    wsb_ = [sb(f"wsb{i}", (128, 2048), BF16) for i in range(2)]
    sel_sb = sb("sel_sb", (128, 8), F32)
    x1ln_sb = sb("x1ln_sb", (128, 8 * TC), BF16)
    wo_sb = sb12("wo_sb", (128, 8192), BF16)
    wog2_sb = sb12("wog2_sb", (64, 1024), BF16)
    nw_sb = sb12("nw_sb", (128, 8), F32)
    gpre_sb = sb12("gpre_sb", (64, TC), F32)

    for nm in cst:
        sc.dma("gpsimd", lambda g, t=cst[nm], nm=nm: g.dma_start(t[:, :], din[nm][:, :]), sW)
    sc.dma("gpsimd", lambda g: g.dma_start(wo_sb[:, :], din["wo"][:, :]), sW)
    sc.dma("gpsimd", lambda g: g.dma_start(wog2_sb[:, :], din["wog2"][:, :]), sW)
    sc.dma("sync", lambda s: s.dma_start(nw_sb[:, :], din["nw"][:, :]), sA)
    sc.dma("sync", lambda s: s.dma_start(sel_sb[:, :], din["sel"][:, :]), sA)
    nc.const_aps.register(F32, EPS, cst['c_eps'][:, 0:1])
    w_base, a_base = sW.n, sA.n
    wo_ready = 0
    for k in range(8):
        ap_w = wo_sb[:, :].rearrange("p (c k m) -> p c k m", c=8, k=8)[:, :, k, :]
        wo_ready = sc.post("vector",
                           lambda vv, ap_w=ap_w, k=k: vv.tensor_scalar_mul(
                               ap_w, ap_w, nw_sb[:, k:k + 1]),
                           waits=[(sW, w_base), (sA, a_base)] if k == 0 else (), inc=sV)
    sc.post("vector", lambda v: v.memset(scur[:, :], 0.0), inc=sV)
    zv0 = sc.post("vector", lambda v: v.memset(zeros_sb[:, :], 0.0), inc=sV)

    wq_sb = sb1("wq_sb", (128, 8192), BF16)
    wk_sb = sb1("wk_sb", (128, 8192), BF16)
    wv_sb = sb1("wv_sb", (128, 8192), BF16)
    wf1_sb = sb1("wf1_sb", (128, 512), BF16)
    wf2_sb = sb1("wf2_sb", (64, 1024), BF16)
    wog1_sb = sb1("wog1_sb", (128, 512), BF16)
    wgam_sb = sb1("wgam_sb", (128, 128), BF16)
    for nm, t in [("wq", wq_sb), ("wk", wk_sb), ("wv", wv_sb), ("wf1", wf1_sb),
                  ("wf2", wf2_sb), ("wog1", wog1_sb), ("wgam", wgam_sb)]:
        sc.dma("gpsimd", lambda g, t=t, nm=nm: g.dma_start(t[:, :], din[nm][:, :]), sW)
    pw_ready = sW.n

    xgb = sb1("xgb", (128, 8 * GT), BF16)
    xln = sb1("xln", (128, 8 * GT), BF16)
    sqb = xln
    q_g = xgb
    k_g = sb1("k_g", (128, 8 * GT), BF16)
    v_g = sb1("v_g", (128, 8 * GT), BF16)
    ngk_g = sb1("ngk_g", (128, 8 * GT), F16)
    f1t = sb1("f1t", (64, GT), BF16)
    gs_g = sb1("gs_g", (16, GT), BF16)
    r_m2 = sb1("r_m2", (1, GT), F32)
    r_a = sb1("r_a", (1, GT), F32)
    r_b = sb1("r_b", (1, GT), F32)
    murt = sb1("murt", (1, 2 * GT), BF16)
    ebuf = sb1("ebuf", (128, GT), F32)

    ka = sb1("ka", (128, GT), BF16)
    khat = sb1("khat", (128, GT), BF16)
    AQs = [sb1(f"AQ{i}", (128, 2 * GT), BF16) for i in range(2)]
    nL = sb1("nL", (128, GT), F32)
    eDCs = [sb1(f"eDC{i}", (128, NCH), F32) for i in range(2)]
    nLCs = [sb1(f"nLC{i}", (128, NCH), F32) for i in range(2)]
    XU = [sb1(f"XU{i}", (128, NCH * 192), BF16) for i in range(2)]
    Tt = [sb1(f"Tt{i}", (128, GT), BF16) for i in range(2)]
    LK = sb1("LK", (128, 2 * GT), BF16)
    T0m = sb1("T0m", (128, GT), BF16)
    ktmp = sb1("ktmp", (128, GT), BF16)
    J_g = sb1("J_g", (128, GT), BF16)
    WTst = sb1("WTst", (128, NCH * 128), BF16)
    Ost = ktmp

    psA = ps1("psA", (128, 512), F32)
    psB = ps1("psB", (128, 512), F32)
    psW_ = ps1("psW", (128, 1024), F32)
    psT4 = ps1("psT4", (128, 1024), BF16)
    psX = ps1("psX", (128, 1024), F32)
    psU = ps1("psU", (128, 512), F32)

    ones1c = cst["c_ones1c"]; sel64 = cst["c_sel64"]; selH65 = cst["c_selH65"]
    ones65 = cst["c_ones65"]
    ones1 = cst["c_ones1"]; hA = cst["c_hA"]; hB = cst["c_hB"]
    selPair = cst["c_selPair"]
    maskU8 = cst["c_maskU8"]; maskI8 = cst["c_maskI8"]; maskL8 = cst["c_maskL8"]
    ident2 = cst["c_ident2"]; ident = cst["c_ident"]; id2b = cst["c_id2b"]

    def mm(psl, lhsT, rhs, start, stop, waits=(), inc=None, tp=None, sgc=False):
        return sc.post("tensor",
                       lambda t, psl=psl, lhsT=lhsT, rhs=rhs, start=start, stop=stop, tp=tp,
                       sgc=sgc: t.matmul(psl, lhsT, rhs, start=start, stop=stop,
                                         tile_position=tp, skip_group_check=sgc),
                       waits=waits, inc=inc)

    QUAD = ((0, 64, (0, 0)), (64, 128, (64, 64)))

    st8 = {"prev_group_v": 0, "stWT_dma": 0, "stO_dma": 0, "tap_dma": 0, "psT4_free": 0}

    def mixer_prep(g, f, prev):
        fsl = slice(f * GT, (f + 1) * GT)
        aq = AQs[f % 2]
        edc = eDCs[f % 2]
        nlc = nLCs[f % 2]
        P = {"aq": aq, "edc": edc}
        gb = mm(psW_[:, 0:GT], selPair[0:16, f * 128:(f + 1) * 128], gs_g[0:16, 0:GT],
                start=True, stop=True, waits=[(sV, prev["psw_free"])], inc=sT)
        kq = sc.post("scalar", lambda s, fsl=fsl: s.activation(sqb[:, 0:GT], k_g[:, fsl],
                                                               AF.Square), inc=sC)
        ssv = mm(psU[0:65, 0:GT], selH65[:, 0:65], sqb[:, 0:GT], start=True, stop=True,
                 waits=[(sC, kq), (sV, prev["psu_free"])], inc=sT)
        v6 = sc.post("vector", lambda v: v.tensor_scalar_max(r_a[0:1, 0:GT], psU[0:1, 0:GT],
                                                             1e-24), waits=[(sT, ssv)], inc=sV)
        v7 = sc.post("vector", lambda v: v.tensor_scalar_max(r_b[0:1, 0:GT], psU[64:65, 0:GT],
                                                             1e-24), inc=sV)
        P["v7"] = v7
        c2a = sc.post("scalar", lambda s: s.activation(r_a[0:1, 0:GT], r_a[0:1, 0:GT], AF.Ln),
                      waits=[(sV, v7)], inc=sC)
        sc.post("scalar", lambda s: s.activation(r_b[0:1, 0:GT], r_b[0:1, 0:GT], AF.Ln))
        sc.post("scalar", lambda s: s.activation(murt[0:1, 0:GT], r_a[0:1, 0:GT], AF.Exp,
                                                 scale=-0.5))
        c2d = sc.post("scalar", lambda s: s.activation(murt[0:1, GT:2 * GT], r_b[0:1, 0:GT],
                                                       AF.Exp, scale=-0.5), inc=sC)
        rb0 = mm(psW_[:, GT:2 * GT], hA[0:1, :], murt[0:1, 0:GT], start=True, stop=False,
                 waits=[(sC, c2d)])
        rb = mm(psW_[:, GT:2 * GT], hB[0:1, :], murt[0:1, GT:2 * GT], start=False, stop=True,
                inc=sT)
        v8 = sc.post("vector", lambda v, fsl=fsl: v.tensor_mul(k_g[:, fsl], k_g[:, fsl],
                                                               psW_[:, GT:2 * GT]),
                     waits=[(sT, rb)], inc=sV)
        v9 = sc.post("vector", lambda v, fsl=fsl: v.tensor_mul(ka[:, 0:GT], k_g[:, fsl],
                                                               psW_[:, 0:GT]), inc=sV)
        P["v9"] = v9
        scv = None
        for j in range(NCH):
            sl = slice(j * C, (j + 1) * C)
            scv = sc.post("vector", lambda v, sl=sl, f=f: v.tensor_tensor_scan(
                nL[:, sl], ngk_g[:, f * GT + sl.start: f * GT + sl.stop],
                zeros_sb[:, 0:C], 0.0, ALU.add, ALU.add),
                waits=[(sV, zv0)] if (g == 0 and f == 0 and j == 0) else (), inc=sV)
        vg = sc.post("vector", lambda v, nlc=nlc: v.tensor_copy(
            nlc[:, 0:NCH], _sv(nL, 1, NCH, C, C - 1)), waits=[(sV, scv)], inc=sV)
        ce = sc.post("scalar", lambda s, nlc=nlc, edc=edc: s.activation(
            edc[:, 0:NCH], nlc[:, 0:NCH], AF.Exp, scale=-1.0), waits=[(sV, vg)], inc=sC)
        P["ce"] = ce
        e1 = sc.post("scalar", lambda s: s.activation(ebuf[:, 0:GT], nL[:, 0:GT], AF.Exp),
                     waits=[(sV, scv)], inc=sC)
        vKH = sc.post("vector", lambda v, fsl=fsl: v.tensor_mul(khat[:, 0:GT], k_g[:, fsl],
                                                                ebuf[:, 0:GT]),
                      waits=[(sC, e1), (sV, v9)], inc=sV)
        e2 = sc.post("scalar", lambda s: s.activation(ebuf[:, 0:GT], nL[:, 0:GT], AF.Exp,
                                                      scale=-1.0), waits=[(sV, vKH)], inc=sC)
        vQH = sc.post("vector", lambda v, aq=aq, f=f: v.tensor_mul(
            _sv(aq, C, NCH, 2 * C, C), _sv(q_g, C, NCH, C, f * GT),
            _sv(ebuf, C, NCH, C)), waits=[(sC, e2)], inc=sV)
        vt = sc.post("vector", lambda v, fsl=fsl: v.tensor_add(nL[:, 0:GT], nL[:, 0:GT],
                                                               ngk_g[:, fsl]),
                     waits=[(sV, vQH)], inc=sV)
        e3 = sc.post("scalar", lambda s: s.activation(ebuf[:, 0:GT], nL[:, 0:GT], AF.Exp,
                                                      scale=-1.0), waits=[(sV, vt)], inc=sC)
        vAH = sc.post("vector", lambda v, aq=aq: v.tensor_mul(
            _sv(aq, C, NCH, 2 * C, 0), _sv(ka, C, NCH, C),
            _sv(ebuf, C, NCH, C)), waits=[(sC, e3)], inc=sV)
        P["vAH"] = vAH
        return P

    def mixer_core1(g, f, P):
        aq, edc = P["aq"], P["edc"]
        gl_done = g0_done = trv = None
        for j in range(NCH):
            for (r0, r1, tp) in QUAD:
                gl_done = mm(psW_[r0:r1, j * 128:(j + 1) * 128],
                             khat[r0:r1, j * C:(j + 1) * C],
                             aq[r0:r1, j * 128:(j + 1) * 128], True, True,
                             waits=[(sV, P["vAH"])] if (j == 0 and r0 == 0) else (),
                             inc=sT if (j == NCH - 1 and r0 == 64) else None, tp=tp)
        for j in range(NCH):
            for (r0, r1, tp) in QUAD:
                g0_done = mm(psU[r0:r1, j * C:(j + 1) * C],
                             aq[r0:r1, j * 128: j * 128 + C],
                             khat[r0:r1, j * C:(j + 1) * C], True, True,
                             inc=sT if (j == NCH - 1 and r0 == 64) else None, tp=tp)
        for j in range(NCH):
            for (r0, r1, tp) in QUAD:
                sc.post("tensor", lambda t, j=j, r0=r0, r1=r1, tp=tp, f=f: t.transpose(
                    psT4[r0:r1, j * 128: j * 128 + 64],
                    v_g[r0:r1, f * GT + j * C: f * GT + (j + 1) * C],
                    id2b[r0:r1, 0:64], tile_position=tp),
                    waits=[(sC, P["psT4_free"])] if (j == 0 and r0 == 0) else ())
                trv = sc.post("tensor", lambda t, j=j, r0=r0, r1=r1, tp=tp, aq=aq: t.transpose(
                    psT4[r0:r1, j * 128 + 64: j * 128 + 128],
                    aq[r0:r1, j * 128: j * 128 + C],
                    id2b[r0:r1, 0:64], tile_position=tp),
                    inc=sT if (j == NCH - 1 and r0 == 64) else None)
        evT0 = sc.post("vector", lambda v: v.tensor_mul(
            _sv(T0m, C, NCH, C), _sv(psW_, C, NCH, 128), _sv(maskU8, C, NCH, C)),
            waits=[(sT, gl_done)], inc=sV)
        evLam = sc.post("vector", lambda v: v.tensor_mul(
            _sv(LK, C, NCH, 2 * C), _sv(psW_, C, NCH, 128, C),
            _sv(maskI8, C, NCH, C)), inc=sV)
        evU0 = sc.post("vector", lambda v: v.tensor_mul(
            _sv(XU[0], C, NCH, 192, 128), _sv(psU, C, NCH, C),
            _sv(maskL8, C, NCH, C)), waits=[(sT, g0_done)], inc=sV)
        evX0 = sc.post("scalar", lambda sy: sy.copy(
            _sv(XU[0], 128, NCH, 192), _sv(psT4, 128, NCH, 128)),
            waits=[(sT, trv)], inc=sC)
        kv = None
        for j in range(NCH):
            kv = sc.post("vector", lambda v, j=j, edc=edc: v.tensor_scalar_mul(
                ktmp[:, j * C:(j + 1) * C], khat[:, j * C:(j + 1) * C], edc[:, j:j + 1]),
                waits=[(sC, P["ce"]), (sA, st8["stO_dma"])] if j == 0 else (),
                inc=sV if j == NCH - 1 else None)
        ktv = None
        for j in range(NCH):
            for (r0, r1, tp) in QUAD:
                ktv = sc.post("tensor", lambda t, j=j, r0=r0, r1=r1, tp=tp: t.transpose(
                    psT4[r0:r1, j * 64:(j + 1) * 64], ktmp[r0:r1, j * C:(j + 1) * C],
                    id2b[r0:r1, 0:64], tile_position=tp),
                    waits=[(sV, kv), (sC, evX0)] if (j == 0 and r0 == 0) else (),
                    inc=sT if (j == NCH - 1 and r0 == 64) else None)
        evKT = sc.post("scalar", lambda sy: sy.copy(
            _sv(LK, C, NCH, 2 * C, C), _sv(psT4, C, NCH, C)),
            waits=[(sT, ktv)], inc=sC)
        P.update(evT0=evT0, evLam=evLam, evU0=evU0, evX0=evX0, evKT=evKT)

    def mixer_core2(g, f, P, nxt):
        aq = P["aq"]
        edc = P["edc"]
        curT = T0m
        cur = 0
        last_solve_v = P["evU0"]
        last_solve_c = P["evKT"]
        for k in range(MIX_SOLVE_K):
            do_u, do_t = k < 4, k < 5
            if MIX_SOLVE_SUB == 'x':
                do_u = do_t = False
            elif MIX_SOLVE_SUB == 'xu':
                do_t = False
            nbatch = 1 + (1 if do_u else 0) + (1 if do_t else 0)
            last_mm = None

            def _is_last(b, j, r0):
                return b == nbatch - 1 and j == NCH - 1 and r0 == 64
            for j in range(NCH):
                for (r0, r1, tp) in QUAD:
                    last_mm = mm(psX[r0:r1, j * 128:(j + 1) * 128],
                                 curT[r0:r1, j * C:(j + 1) * C],
                                 XU[cur][r0:r1, j * 192: j * 192 + 128], True, True,
                                 waits=([(sV, last_solve_v)] +
                                        ([(sC, last_solve_c)] if last_solve_c else []))
                                 if (j == 0 and r0 == 0) else (),
                                 inc=sT if _is_last(0, j, r0) else None, tp=tp)
            if do_u:
                for j in range(NCH):
                    for (r0, r1, tp) in QUAD:
                        last_mm = mm(psU[r0:r1, j * C:(j + 1) * C],
                                     curT[r0:r1, j * C:(j + 1) * C],
                                     XU[cur][r0:r1, j * 192 + 128: j * 192 + 192],
                                     True, True, tp=tp,
                                     inc=sT if _is_last(1, j, r0) else None)
            if do_t:
                for j in range(NCH):
                    for (r0, r1, tp) in QUAD:
                        last_mm = mm(psB[r0:r1, j * C:(j + 1) * C],
                                     XU[cur][r0:r1, j * 192 + 128: j * 192 + 192],
                                     curT[r0:r1, j * C:(j + 1) * C], True, True, tp=tp,
                                     inc=sT if _is_last(nbatch - 1, j, r0) else None)
            nxt_b = 1 - cur
            ev1 = sc.post("vector", lambda v, cur=cur, nxt_b=nxt_b: v.tensor_add(
                _sv(XU[nxt_b], 128, NCH, 192), _sv(psX, 128, NCH, 128),
                _sv(XU[cur], 128, NCH, 192)), waits=[(sT, last_mm)], inc=sV)
            last_solve_v = ev1
            last_solve_c = 0
            if do_u:
                last_solve_c = sc.post("scalar", lambda sy, nxt_b=nxt_b: sy.copy(
                    _sv(XU[nxt_b], C, NCH, 192, 128), _sv(psU, C, NCH, C)),
                    waits=[(sT, last_mm)], inc=sC)
            if do_t:
                newT = Tt[k % 2]
                last_solve_c = sc.post("scalar", lambda sy, newT=newT: sy.copy(
                    newT[:, 0:GT], psB[:, 0:GT]), waits=[(sT, last_mm)] if not do_u else (),
                    inc=sC)
                curT = newT
            cur = nxt_b
        last_solve_c = max(last_solve_c, 1)

        def MV_ap(r0, r1, j):
            return XU[cur][r0:r1, j * 192: j * 192 + 64]

        def MA_ap(r0, r1, j):
            return XU[cur][r0:r1, j * 192 + 64: j * 192 + 128]

        wt = None
        wtw = [(sC, last_solve_c)]
        if nxt is not None:
            wtw.append((sV, nxt["v9"]))
        for j in range(NCH):
            for (r0, r1, tp) in QUAD:
                wt = mm(psW_[r0:r1, j * 128:(j + 1) * 128], MA_ap(r0, r1, j),
                        LK[r0:r1, j * 128:(j + 1) * 128], True, True,
                        waits=wtw if (j == 0 and r0 == 0) else (),
                        inc=sT if (j == NCH - 1 and r0 == 64) else None, tp=tp)
        jv = None
        jw = [(sV, last_solve_v)]
        if nxt is not None:
            jw.append((sV, nxt["v7"]))
        for j in range(NCH):
            for (r0, r1, tp) in QUAD:
                jv = mm(psU[r0:r1, j * C:(j + 1) * C],
                        LK[r0:r1, j * 128 + C: j * 128 + 128],
                        MV_ap(r0, r1, j), True, True,
                        waits=jw if (j == 0 and r0 == 0) else (),
                        inc=sT if (j == NCH - 1 and r0 == 64) else None, tp=tp)
        evW = sc.post("vector", lambda v, aq=aq: v.tensor_add(
            _sv(WTst, C, NCH, 128), _sv(psW_, C, NCH, 128),
            _sv(aq, C, NCH, 2 * C, C)),
            waits=[(sT, wt), (sA, st8["stWT_dma"])], inc=sV)
        evTc = None
        for j in range(NCH):
            evTc = sc.post("vector", lambda v, j=j, edc=edc: v.scalar_tensor_tensor(
                WTst[:, j * 128 + 64: j * 128 + 128], ident2[:, 0:64], edc[:, j:j + 1],
                psW_[:, j * 128 + 64: j * 128 + 128], ALU.mult, ALU.add), inc=sV)
        evJ = sc.post("vector", lambda v: v.tensor_copy(J_g[:, 0:GT], psU[:, 0:GT]),
                      waits=[(sT, jv)], inc=sV)
        st8["stWT_dma"] = sc.dma("sync", lambda s, g=g, f=f: s.dma_start(
            stWT_d[:, f, g * NCH:(g + 1) * NCH, :],
            WTst[:, :].rearrange("p (j m) -> p j m", j=NCH)), sA, waits=[(sV, evTc)])
        if "wtst" in taps:
            tap("wtst", (128, NCH * 128), BF16)
            st8["tap_dma"] = sc.dma("sync", lambda s: s.dma_start(
                tap_d["wtst"][:, :], WTst[:, :]), sA, waits=[(sV, evTc)])
        if "jg" in taps:
            tap("jg", (128, GT), F32)
            st8["tap_dma"] = sc.dma("sync", lambda s: s.dma_start(
                tap_d["jg"][:, :], J_g[:, :]), sA, waits=[(sV, evJ)])

        swv = evTc
        obl = None
        for j in range(NCH):
            for (r0, r1, tp) in QUAD:
                mm(psA[r0:r1, j * C:(j + 1) * C], MV_ap(r0, r1, j),
                   LK[r0:r1, j * 128: j * 128 + C], True, False,
                   waits=[(sV, swv)] if r0 == 0 else (), tp=tp, sgc=True)
            for (r0, r1, tp) in QUAD:
                obl = mm(psA[r0:r1, j * C:(j + 1) * C], scur[r0:r1, f * 64:(f + 1) * 64],
                         WTst[r0:r1, j * 128: j * 128 + 64], False, True,
                         inc=sT if r0 == 64 else None, tp=tp, sgc=True)
            smm = None
            for (r0, r1, tp) in QUAD:
                smm = mm(psX[r0:r1, 0:64], WTst[r0:r1, j * 128 + 64: j * 128 + 128],
                         scur[r0:r1, f * 64:(f + 1) * 64], True, True,
                         inc=sT if r0 == 64 else None, tp=tp)
            swv = sc.post("vector", lambda v, j=j, f=f: v.tensor_add(
                scur[:, f * 64:(f + 1) * 64], psX[:, 0:64], J_g[:, j * C:(j + 1) * C]),
                waits=[(sT, smm), (sV, evJ)], inc=sV)
        evO = sc.post("vector", lambda v: v.tensor_copy(Ost[:, 0:GT], psA[:, 0:GT]),
                      waits=[(sT, obl), (sV, swv), (sA, st8["stO_dma"])], inc=sV)
        st8["stO_dma"] = sc.dma("sync", lambda s, g=g, f=f: s.dma_start(
            stO_d[f, :, g * GT:(g + 1) * GT], Ost[:, 0:GT]), sA, waits=[(sV, evO)])
        if "op1" in taps:
            tap("op1", (NP, 128, TC), BF16)
            st8["stO_dma"] = sc.dma("sync", lambda s, g=g, f=f: s.dma_start(
                tap_d["op1"][f, :, g * GT:(g + 1) * GT], Ost[:, 0:GT]), sA)
        P["evW"] = evW
        P["evTc"] = evTc
        P["evJ"] = evJ

    def p1_group(g):
        g0 = g * GT
        src = x_d[:, :, g0:g0 + GT].rearrange("f p t -> p f t")
        ld = sc.dma("gpsimd", lambda gg: gg.dma_start(
            xgb[:, :].rearrange("p (f t) -> p f t", f=8), src),
            sW, waits=[(sV, st8["prev_group_v"]), (sA, st8["tap_dma"])])
        sq = sc.post("scalar", lambda s: s.activation(sqb[:, :], xgb[:, :], AF.Square),
                     waits=[(sW, ld)], inc=sC)
        stv = None
        for f8 in range(8):
            mm(psU[0:65, 0:GT], sel64[:, 0:65], sqb[:, f8 * GT:(f8 + 1) * GT],
               f8 == 0, False,
               waits=[(sC, sq), (sW, pw_ready), (sV, st8["prev_group_v"])] if f8 == 0 else ())
        for f8 in range(8):
            stv = mm(psU[0:65, 0:GT], ones65[:, 0:65], xgb[:, f8 * GT:(f8 + 1) * GT],
                     False, f8 == 7,
                     waits=(), inc=sT if f8 == 7 else None)
        v1 = sc.post("vector", lambda v: v.tensor_scalar_mul(
            r_a[0:1, 0:GT], psU[0:1, 0:GT], 1.0 / D), waits=[(sT, stv)], inc=sV)
        sc.post("vector", lambda v: v.tensor_copy(murt[0:1, 0:GT], r_a[0:1, 0:GT]))
        sc.post("vector", lambda v: v.tensor_scalar_mul(
            r_m2[0:1, 0:GT], psU[64:65, 0:GT], 1.0 / D))
        sc.post("vector", lambda v: v.tensor_mul(
            r_a[0:1, 0:GT], r_a[0:1, 0:GT], r_a[0:1, 0:GT]))
        v4 = sc.post("vector", lambda v: v.tensor_sub(
            r_m2[0:1, 0:GT], r_m2[0:1, 0:GT], r_a[0:1, 0:GT]), inc=sV)
        c1 = sc.post("scalar", lambda s: s.activation(
            r_b[0:1, 0:GT], r_m2[0:1, 0:GT], AF.Ln, bias=EPS), waits=[(sV, v4)], inc=sC)
        v5 = sc.post("scalar", lambda s: s.activation(
            murt[0:1, GT:2 * GT], r_b[0:1, 0:GT], AF.Exp, scale=-0.5), inc=sC)
        bc = None
        for hh in range(2):
            bc = mm(psW_[:, hh * GT:(hh + 1) * GT], ones1[0:1, :],
                    murt[0:1, hh * GT:(hh + 1) * GT], True, True,
                    waits=[(sC, v5), (sV, v1 + 1)] if hh == 0 else (),
                    inc=sT if hh == 1 else None)
        xv = None
        for f8 in range(8):
            sl = slice(f8 * GT, (f8 + 1) * GT)
            sc.post("vector", lambda v, sl=sl: v.tensor_sub(
                ebuf[:, 0:GT], xgb[:, sl], psW_[:, 0:GT]),
                waits=[(sT, bc)] if f8 == 0 else ())
            xv = sc.post("vector", lambda v, sl=sl: v.tensor_mul(
                xln[:, sl], ebuf[:, 0:GT], psW_[:, GT:2 * GT]), inc=sV)
        if tap("xln", (NG, 128, 8 * GT), BF16) is not None:
            st8["tap_dma"] = sc.dma("sync", lambda s, g=g: s.dma_start(
                tap_d["xln"][g, :, :], xln[:, :]), sA, waits=[(sV, xv)])

        ping = [psA, psB]
        prev_ev = {0: 0, 1: 0}
        nmm = 0
        for (wsb, dst, actf) in [(wq_sb, q_g, AF.Silu), (wk_sb, k_g, AF.Silu),
                                 (wv_sb, v_g, None)]:
            for cc in range(8):
                bank = ping[nmm % 2]
                w0 = ([(sC, prev_ev[nmm % 2])] if prev_ev[nmm % 2] else [])
                if nmm == 0:
                    w0 = w0 + [(sV, xv)]
                mv = None
                for kk in range(8):
                    mv = mm(bank[:, 0:GT],
                            wsb[:, cc * 1024 + kk * 128: cc * 1024 + (kk + 1) * 128],
                            xln[:, kk * GT:(kk + 1) * GT], kk == 0, kk == 7,
                            waits=w0 if kk == 0 else (), inc=sT if kk == 7 else None)
                if actf is not None:
                    ev = sc.post("scalar", lambda s, bank=bank, dst=dst, cc=cc, actf=actf:
                                 s.activation(dst[:, cc * GT:(cc + 1) * GT], bank[:, 0:GT], actf),
                                 waits=[(sT, mv)], inc=sC)
                else:
                    ev = sc.post("scalar", lambda s, bank=bank, dst=dst, cc=cc:
                                 s.copy(dst[:, cc * GT:(cc + 1) * GT], bank[:, 0:GT]),
                                 waits=[(sT, mv)], inc=sC)
                prev_ev[nmm % 2] = ev
                nmm += 1
        mv = None
        for kk in range(8):
            mv = mm(psA[0:64, 0:GT], wf1_sb[:, kk * 64:(kk + 1) * 64],
                    xln[:, kk * GT:(kk + 1) * GT], kk == 0, kk == 7,
                    waits=[(sC, prev_ev[0])] if kk == 0 else (), inc=sT if kk == 7 else None)
        evf1 = sc.post("scalar", lambda s: s.copy(f1t[0:64, 0:GT], psA[0:64, 0:GT]),
                       waits=[(sT, mv)], inc=sC)
        mv = None
        for kk in range(8):
            mv = mm(psB[0:64, 0:GT], wog1_sb[:, kk * 64:(kk + 1) * 64],
                    xln[:, kk * GT:(kk + 1) * GT], kk == 0, kk == 7,
                    waits=[(sC, prev_ev[1])] if kk == 0 else (), inc=sT if kk == 7 else None)
        evog = sc.post("scalar", lambda s, g=g: s.copy(
            gpre_sb[0:64, g * GT:(g + 1) * GT], psB[0:64, 0:GT]), waits=[(sT, mv)], inc=sC)
        mv = None
        for kk in range(8):
            mv = mm(psA[0:16, 0:GT], wgam_sb[:, kk * 16:(kk + 1) * 16],
                    xln[:, kk * GT:(kk + 1) * GT], kk == 0, kk == 7,
                    waits=[(sC, evf1)] if kk == 0 else (), inc=sT if kk == 7 else None)
        evgam = sc.post("scalar", lambda s: s.activation(gs_g[0:16, 0:GT], psA[0:16, 0:GT],
                                                         AF.Sigmoid), waits=[(sT, mv)], inc=sC)
        zv = evgam
        for cc in range(8):
            bank = ping[cc % 2]
            mv = mm(bank[:, 0:GT], wf2_sb[0:64, cc * 128:(cc + 1) * 128], f1t[0:64, 0:GT],
                    True, True, waits=[(sC, zv)], inc=sT)
            e1 = sc.post("scalar", lambda s, bank=bank: s.activation(
                ebuf[:, 0:GT], bank[:, 0:GT], AF.Exp, scale=-1.0), waits=[(sT, mv)], inc=sC)
            zv = sc.post("scalar", lambda s, cc=cc: s.activation(
                ngk_g[:, cc * GT:(cc + 1) * GT], ebuf[:, 0:GT], AF.Ln, bias=1.0), inc=sC)
        proj_done_c = sC.n

        if "q" in taps:
            for nm, t in [("q", q_g), ("k", k_g), ("v", v_g)]:
                tap(nm, (NG, 128, 8 * GT), BF16)
                st8["tap_dma"] = sc.dma("sync", lambda s, g=g, t=t, nm=nm: s.dma_start(
                    tap_d[nm][g, :, :], t[:, :]), sA, waits=[(sC, proj_done_c)])
        if "ngk" in taps:
            tap("ngk", (NG, 128, 8 * GT), F32)
            for cc in range(8):
                cv = sc.post("vector", lambda v, cc=cc: v.tensor_copy(
                    ebuf[:, 0:GT], ngk_g[:, cc * GT:(cc + 1) * GT]),
                    waits=[(sC, proj_done_c), (sA, st8["tap_dma"])], inc=sV)
                st8["tap_dma"] = sc.dma("sync", lambda s, g=g, cc=cc: s.dma_start(
                    tap_d["ngk"][g, :, cc * GT:(cc + 1) * GT], ebuf[:, 0:GT]),
                    sA, waits=[(sV, cv)])
        if "gam" in taps:
            tap("gam", (NG, 16, GT), BF16)
            st8["tap_dma"] = sc.dma("sync", lambda s, g=g: s.dma_start(
                tap_d["gam"][g, :, :], gs_g[:, :]), sA, waits=[(sC, proj_done_c)])
        if "gpre" in taps:
            tap("gpre", (NG, 64, GT), F32)
            st8["tap_dma"] = sc.dma("sync", lambda s, g=g: s.dma_start(
                tap_d["gpre"][g, :, :], gpre_sb[0:64, g * GT:(g + 1) * GT]),
                sA, waits=[(sC, proj_done_c)])

        if "mix" in phases:
            Pd = [None] * (MIX_PAIRS + 1)
            Pd[0] = mixer_prep(g, 0, {"psw_free": xv, "psu_free": xv, "psT4": 0})
            for f in range(MIX_PAIRS):
                Pd[f]["psT4_free"] = st8["psT4_free"]
                mixer_core1(g, f, Pd[f])
                st8["psT4_free"] = Pd[f]["evKT"]
                if f + 1 < MIX_PAIRS:
                    Pd[f + 1] = mixer_prep(g, f + 1, {"psw_free": Pd[f]["evLam"],
                                                      "psu_free": Pd[f]["evU0"]})
                mixer_core2(g, f, Pd[f], Pd[f + 1])
        st8["prev_group_v"] = sV.n

    if "p1" in phases:
        for g in range(MIX_GROUPS):
            sc.mark(f"p1g{g}")
            p1_group(g)

    p1stack.close()

    if "p2" in phases:
        p2stack = contextlib.ExitStack()

        def sb2(name, shape, dtype):
            return p2stack.enter_context(nc.sbuf_tensor(name, list(shape), dtype))

        def ps2(name, shape, dtype=F32):
            return p2stack.enter_context(nc.psum_tensor(name, list(shape), dtype))

        xg_sb = sb2("xg_sb", (128, 8 * GT), F32)
        sgath = sb2("sgath", (128, NCORE * NP * 64), F32)
        stil = sb2("stil", (128, NP * 64), BF16)
        wtbG = sb2("wtbG", (128, NP * NCH * 128), BF16)
        obG = sb2("obG", (128, NP * GT), BF16)
        ofin = sb2("ofin", (128, 8 * GT), F32)
        ogb = sb2("ogb", (128, 8 * GT), BF16)
        bbuf = sb2("bbuf", (128, 8 * GT), BF16)
        x1g = sb2("x1g", (128, 8 * GT), F32)
        gbuf = sb2("gbuf", (128, GT), F32)
        gpb = sb2("gpb", (64, GT), BF16)
        eb2 = sb2("eb2", (128, GT), F32)
        r2a = sb2("r2a", (1, GT), F32)
        r2b = sb2("r2b", (1, GT), F32)
        mur2 = sb2("mur2", (1, 2 * GT), BF16)
        psA2 = ps2("psA2", (128, 512), F32)
        psB2t = ps2("psB2", (128, 512), F32)
        psW2 = ps2("psW2", (128, 1024), F32)
        psU2 = ps2("psU2", (128, 512), F32)
        psO2 = ps2("psO2", (128, 512), F32)

        p1_end_v = sV.n
        sc.mark("coll")
        if "coll" in phases:
            cc1 = sc.dma("gpsimd", lambda g: g.dma_start(
                cc_in[:, :].rearrange("(f p) v -> p f v", p=128),
                scur[:, :].rearrange("p (f v) -> p f v", v=64)),
                sW, waits=[(sV, p1_end_v)])
            cg = sc.post("gpsimd", lambda g: g.collective_compute(
                "AllGather", ALU.bypass, replica_groups=[list(range(NCORE))],
                ins=[cc_in], outs=[cc_out]), waits=[(sW, cc1)], inc=sG)
            sc.dma("sync", lambda sy: sy.dma_start(
                sgath[:, :].rearrange("p (r f v) -> p r f v", f=NP, v=64),
                cc_out[:, :].rearrange("(r f p) v -> p r f v", f=NP, p=128)),
                sA, waits=[(sG, cg)])
            gath_a = sA.n
            stv2 = 0
            for f in range(NP):
                sc.post("vector", lambda v, f=f: v.tensor_scalar_mul(
                    eb2[:, 0:64], sgath[:, f * 64: f * 64 + 64], sel_sb[:, 0:1]),
                    waits=[(sA, gath_a)] if f == 0 else ())
                for r in range(1, NCORE):
                    sc.post("vector", lambda v, f=f, r=r: v.scalar_tensor_tensor(
                        eb2[:, 0:64], sgath[:, r * NP * 64 + f * 64: r * NP * 64 + f * 64 + 64],
                        sel_sb[:, r:r + 1], eb2[:, 0:64], ALU.mult, ALU.add))
                stv2 = sc.post("vector", lambda v, f=f: v.tensor_copy(
                    stil[:, f * 64:(f + 1) * 64], eb2[:, 0:64]), inc=sV)
        else:
            stv2 = sc.post("vector", lambda v: v.memset(stil[:, :], 0.0),
                           waits=[(sV, p1_end_v)], inc=sV)

        st2 = {"wtG_free": 0, "obG_free": 0, "of_ev": 0, "xg_dma": 0,
               "x1_dma": 0, "stil_v": stv2, "ep_c": 0, "ep_v": stv2, "ep_t": 0,
               "lnv_g": [0, 0, 0, 0]}
        for g in range(NG):
            sc.mark(f"p2g{g}")
            st2["xg_dma"] = sc.dma("sync", lambda sy, g=g: sy.dma_start(
                xg_sb[:, :].rearrange("p (f t) -> p f t", f=8),
                x_d[:, :, g * GT:(g + 1) * GT].rearrange("f p t -> p f t")),
                sA, waits=[(sV, st2["ep_v"]), (sT, st2["ep_t"])])
            wldG = sc.dma("sync", lambda sy, g=g: sy.dma_start(
                wtbG[:, :].rearrange("p (f j m) -> p f j m", f=NP, j=NCH),
                stWT_d[:, :, g * NCH:(g + 1) * NCH, :]),
                sA, waits=[(sT, st2["wtG_free"])])
            oldG = sc.dma("sync", lambda sy, g=g: sy.dma_start(
                obG[:, :].rearrange("p (f t) -> p f t", f=NP),
                stO_d[:, :, g * GT:(g + 1) * GT].rearrange("f p t -> p f t")),
                sA, waits=[(sV, st2["obG_free"])])
            sschain = st2["stil_v"]
            ocl = smm2 = None
            for j8 in range(NCH):
                for f in range(NP):
                    for (r0, r1, tp) in QUAD:
                        ocl = mm(psA2[r0:r1, f * 64:(f + 1) * 64],
                                 stil[r0:r1, f * 64:(f + 1) * 64],
                                 wtbG[r0:r1, (f * NCH + j8) * 128: (f * NCH + j8) * 128 + 64],
                                 True, True,
                                 waits=([(sA, wldG), (sV, sschain),
                                         (sV, st2["of_ev"])] if (f == 0 and r0 == 0)
                                        else ()), tp=tp)
                for f in range(NP):
                    for (r0, r1, tp) in QUAD:
                        smm2 = mm(psB2t[r0:r1, f * 64:(f + 1) * 64],
                                  wtbG[r0:r1, (f * NCH + j8) * 128 + 64:
                                       (f * NCH + j8) * 128 + 128],
                                  stil[r0:r1, f * 64:(f + 1) * 64], True, True,
                                  inc=sT if (f == NP - 1 and r0 == 64) else None, tp=tp)
                ev = sc.post("vector", lambda v, j8=j8: v.tensor_add(
                    _sv(ofin, 64, NP, GT, j8 * 64), _sv(psA2, 64, NP, 64),
                    _sv(obG, 64, NP, GT, j8 * 64)),
                    waits=[(sT, smm2), (sA, oldG)], inc=sV)
                sschain = sc.post("vector", lambda v: v.tensor_copy(
                    stil[:, :], psB2t[:, 0:NP * 64]), inc=sV)
            st2["stil_v"] = sschain
            st2["wtG_free"] = sT.n
            st2["obG_free"] = sschain
            st2["of_ev"] = sschain
            gc = sc.post("scalar", lambda sy, g=g: sy.copy(
                gpb[0:64, 0:GT], gpre_sb[0:64, g * GT:(g + 1) * GT]),
                waits=[(sV, st2["of_ev"])], inc=sC)
            gmul = st2["of_ev"]
            gm_t = 0
            for f8 in range(8):
                half = f8 % 2
                gmm = mm(psW2[:, half * GT:(half + 1) * GT],
                         wog2_sb[0:64, f8 * 128:(f8 + 1) * 128], gpb[0:64, 0:GT],
                         True, True, waits=[(sC, gc), (sV, gmul)] if f8 < 2 else
                         [(sV, gmul)], inc=sT)
                gsig = sc.post("scalar", lambda sy, half=half: sy.activation(
                    gbuf[:, 0:GT], psW2[:, half * GT:(half + 1) * GT], AF.Sigmoid),
                    waits=[(sT, gmm)], inc=sC)
                gmul = sc.post("vector", lambda v, f8=f8: v.tensor_mul(
                    ofin[:, f8 * GT:(f8 + 1) * GT], ofin[:, f8 * GT:(f8 + 1) * GT],
                    gbuf[:, 0:GT]), waits=[(sC, gsig)], inc=sV)
            if P2_STAGE == "gate":
                st2["ep_v"], st2["ep_t"] = sV.n, sT.n
                continue
            sq2 = sc.post("scalar", lambda sy: sy.activation(bbuf[:, :], ofin[:, :],
                                                             AF.Square),
                          waits=[(sV, gmul)], inc=sC)
            cp2 = sc.post("scalar", lambda sy: sy.copy(ogb[:, :], ofin[:, :]), inc=sC)
            stv = None
            for f8 in range(8):
                mm(psU2[0:65, 0:GT], sel64[:, 0:65], bbuf[:, f8 * GT:(f8 + 1) * GT],
                   f8 == 0, False, waits=[(sC, sq2), (sV, gmul)] if f8 == 0 else ())
            for f8 in range(8):
                stv = mm(psU2[0:65, 0:GT], ones65[:, 0:65], ogb[:, f8 * GT:(f8 + 1) * GT],
                         False, f8 == 7, waits=[(sC, cp2)] if f8 == 0 else (),
                         inc=sT if f8 == 7 else None)
            v1 = sc.post("vector", lambda v: v.tensor_scalar_mul(
                r2a[0:1, 0:GT], psU2[0:1, 0:GT], 1.0 / D), waits=[(sT, stv)], inc=sV)
            sc.post("vector", lambda v: v.tensor_copy(mur2[0:1, 0:GT], r2a[0:1, 0:GT]))
            sc.post("vector", lambda v: v.tensor_scalar_mul(
                r2b[0:1, 0:GT], psU2[64:65, 0:GT], 1.0 / D))
            sc.post("vector", lambda v: v.tensor_mul(
                r2a[0:1, 0:GT], r2a[0:1, 0:GT], r2a[0:1, 0:GT]))
            v4 = sc.post("vector", lambda v: v.tensor_sub(
                r2b[0:1, 0:GT], r2b[0:1, 0:GT], r2a[0:1, 0:GT]), inc=sV)
            c1 = sc.post("scalar", lambda sy: sy.activation(
                r2a[0:1, 0:GT], r2b[0:1, 0:GT], AF.Ln, bias=EPS), waits=[(sV, v4)], inc=sC)
            v5 = sc.post("scalar", lambda sy: sy.activation(
                mur2[0:1, GT:2 * GT], r2a[0:1, 0:GT], AF.Exp, scale=-0.5), inc=sC)
            bc = None
            for hh in range(2):
                bc = mm(psW2[:, hh * GT:(hh + 1) * GT], ones1[0:1, :],
                        mur2[0:1, hh * GT:(hh + 1) * GT], True, True,
                        waits=[(sC, v5), (sV, v4)] if hh == 0 else (),
                        inc=sT if hh == 1 else None)
            onv = None
            for f8 in range(8):
                sl = slice(f8 * GT, (f8 + 1) * GT)
                sc.post("vector", lambda v, sl=sl: v.tensor_sub(
                    eb2[:, 0:GT], ofin[:, sl], psW2[:, 0:GT]),
                    waits=[(sT, bc)] if f8 == 0 else ())
                onv = sc.post("vector", lambda v, sl=sl: v.tensor_mul(
                    ogb[:, sl], eb2[:, 0:GT], psW2[:, GT:2 * GT]), inc=sV)
            if P2_STAGE == "stats":
                st2["ep_v"], st2["ep_t"] = sV.n, sT.n
                continue
            prev_ev = 0
            x1v = None
            for cc in range(8):
                mv = None
                for kk in range(8):
                    mv = mm(psO2[:, 0:GT],
                            wo_sb[:, cc * 1024 + kk * 128: cc * 1024 + (kk + 1) * 128],
                            ogb[:, kk * GT:(kk + 1) * GT], kk == 0, kk == 7,
                            waits=([(sV, onv), (sA, st2["xg_dma"])] if cc == 0 else
                                   [(sV, prev_ev)]) if kk == 0 else (),
                            inc=sT if kk == 7 else None)
                x1v = sc.post("vector", lambda v, cc=cc: v.tensor_add(
                    x1g[:, cc * GT:(cc + 1) * GT], psO2[:, 0:GT],
                    xg_sb[:, cc * GT:(cc + 1) * GT]), waits=[(sT, mv)], inc=sV)
                prev_ev = x1v
            st2["x1_dma"] = sc.dma("sync", lambda sy, g=g: sy.dma_start(
                x1_d[:, :, g * GT:(g + 1) * GT].rearrange("f p t -> p f t"),
                x1g[:, :].rearrange("p (f t) -> p f t", f=8)),
                sA, waits=[(sV, x1v)])
            if "x1" in taps:
                tap("x1", (NG, 128, 8 * GT), F32)
                st8["tap_dma"] = sc.dma("sync", lambda sy, g=g: sy.dma_start(
                    tap_d["x1"][g, :, :], x1g[:, :]), sA, waits=[(sV, x1v)])
            if P2_STAGE == "wo":
                st2["ep_v"], st2["ep_t"] = sV.n, sT.n
                continue
            sq3 = sc.post("scalar", lambda sy: sy.activation(bbuf[:, :], x1g[:, :],
                                                             AF.Square),
                          waits=[(sV, x1v)], inc=sC)
            cp3 = sc.post("scalar", lambda sy: sy.copy(ogb[:, :], x1g[:, :]), inc=sC)
            stv = None
            for f8 in range(8):
                mm(psU2[0:65, 0:GT], sel64[:, 0:65], bbuf[:, f8 * GT:(f8 + 1) * GT],
                   f8 == 0, False, waits=[(sC, sq3)] if f8 == 0 else ())
            for f8 in range(8):
                stv = mm(psU2[0:65, 0:GT], ones65[:, 0:65], ogb[:, f8 * GT:(f8 + 1) * GT],
                         False, f8 == 7, waits=[(sC, cp3)] if f8 == 0 else (),
                         inc=sT if f8 == 7 else None)
            v1 = sc.post("vector", lambda v: v.tensor_scalar_mul(
                r2a[0:1, 0:GT], psU2[0:1, 0:GT], 1.0 / D), waits=[(sT, stv)], inc=sV)
            sc.post("vector", lambda v: v.tensor_copy(mur2[0:1, 0:GT], r2a[0:1, 0:GT]))
            sc.post("vector", lambda v: v.tensor_scalar_mul(
                r2b[0:1, 0:GT], psU2[64:65, 0:GT], 1.0 / D))
            sc.post("vector", lambda v: v.tensor_mul(
                r2a[0:1, 0:GT], r2a[0:1, 0:GT], r2a[0:1, 0:GT]))
            v4 = sc.post("vector", lambda v: v.tensor_sub(
                r2b[0:1, 0:GT], r2b[0:1, 0:GT], r2a[0:1, 0:GT]), inc=sV)
            c1 = sc.post("scalar", lambda sy: sy.activation(
                r2a[0:1, 0:GT], r2b[0:1, 0:GT], AF.Ln, bias=EPS), waits=[(sV, v4)], inc=sC)
            v5 = sc.post("scalar", lambda sy: sy.activation(
                mur2[0:1, GT:2 * GT], r2a[0:1, 0:GT], AF.Exp, scale=-0.5), inc=sC)
            bc = None
            for hh in range(2):
                bc = mm(psW2[:, hh * GT:(hh + 1) * GT], ones1[0:1, :],
                        mur2[0:1, hh * GT:(hh + 1) * GT], True, True,
                        waits=[(sC, v5), (sV, v4)] if hh == 0 else (),
                        inc=sT if hh == 1 else None)
            lnv = None
            for f8 in range(8):
                sl = slice(f8 * GT, (f8 + 1) * GT)
                sc.post("vector", lambda v, sl=sl: v.tensor_sub(
                    eb2[:, 0:GT], x1g[:, sl], psW2[:, 0:GT]),
                    waits=[(sT, bc)] if f8 == 0 else ())
                lnv = sc.post("vector", lambda v, f8=f8, g=g: v.tensor_mul(
                    x1ln_sb[:, f8 * TC + g * GT: f8 * TC + (g + 1) * GT],
                    eb2[:, 0:GT], psW2[:, GT:2 * GT]), inc=sV)
            st2["ep_v"] = lnv
            st2["ep_t"] = sT.n
            st2["lnv_g"][g] = lnv
        if "x1ln" in taps:
            tap("x1ln", (128, 8 * TC), BF16)
            st8["tap_dma"] = sc.dma("sync", lambda sy: sy.dma_start(
                tap_d["x1ln"][:, :], x1ln_sb[:, :]), sA, waits=[(sV, st2["ep_v"])])
        p2_end_v, p2_end_t, p2_end_c = sV.n, sT.n, sC.n
        p2stack.close()
        p12stack.close()

    if "p3" in phases:
        p3stack = contextlib.ExitStack()

        def sb3(name, shape, dtype):
            return p3stack.enter_context(nc.sbuf_tensor(name, list(shape), dtype))

        def ps3_(name, shape, dtype=F32):
            return p3stack.enter_context(nc.psum_tensor(name, list(shape), dtype))

        HT = TC // 2
        NS = HT // GT
        g1b = sb3("g1b", (128, 32 * HT), BF16)
        gHb = sb3("gHb", (128, 32 * HT), BF16)
        wsa = [sb3(f"wsa{i}", (128, 4096), BF16) for i in range(2)]
        x1r = [sb3(f"x1r{i}", (128, GT), F32) for i in range(2)]
        outbd = [sb3(f"outb{i}", (128, GT), F32) for i in range(2)]
        sc1_sb = sb3("sc1_sb", (128, 32), F32)
        sc2_sb = sb3("sc2_sb", (128, 32), F32)
        sch1_sb = sb3("sch1_sb", (128, 32), F32)
        sch2_sb = sb3("sch2_sb", (128, 32), F32)
        sc3_sb = sb3("sc3_sb", (128, 8), F32)
        teb = sb3("teb", (128, GT), F32)
        pp = [ps3_(f"pp{i}", (128, 512), F32) for i in range(4)]

        p2v = p2_end_v if "p2" in phases else sV.n
        first = [(sV, p2v)]
        for nm, t in [("s1", sc1_sb), ("s2", sc2_sb), ("sh1", sch1_sb),
                      ("sh2", sch2_sb), ("s3", sc3_sb)]:
            sc.dma("sync", lambda sy, t=t, nm=nm: sy.dma_start(t[:, :], din[nm][:, :]),
                   sA, waits=first)
            first = []
        sc_ld = sA.n
        st3 = {"wd": [0, 0], "wd2": [0, 0], "pev": [0, 0, 0, 0], "pevC": [0, 0, 0, 0],
               "out_dma": 0, "rfree": [0, 0], "ob_dma": [0, 0],
               "bar_v": (p2_end_v if "p2" in phases else sV.n),
               "bar_c": (p2_end_c if "p2" in phases else sC.n)}

        def halves():
            return range(P3_HALVES)

        for h in halves():
            th = h * HT
            if "p2" in phases:
                st3["half_v"] = max(st2["lnv_g"][2 * h], st2["lnv_g"][2 * h + 1])
            else:
                st3["half_v"] = st3["bar_v"]
            sc.mark(f"p3A{h}")
            for c in range(32):
                par = c % 2
                w1l = sc.dma("gpsimd", lambda g, c=c, par=par: g.dma_start(
                    wsb_[par][:, 0:1024], din["w1"][:, c * 1024:(c + 1) * 1024]),
                    sW, waits=[(sT, st3["wd"][par])] + ([(sV, p2v)] if (h == 0 and c < 2) else []))
                w2l = sc.dma("gpsimd", lambda g, c=c, par=par: g.dma_start(
                    wsb_[par][:, 1024:2048], din["w2"][:, c * 1024:(c + 1) * 1024]),
                    sW, waits=[])
                for sp in range(NS):
                    pa, pb = pp[2 * par], pp[2 * par + 1]
                    tsl = slice(th + sp * GT, th + (sp + 1) * GT)
                    mv1 = mv2 = None
                    for kk in range(8):
                        w0 = []
                        if kk == 0:
                            w0 = [(sW, w2l), (sC, st3["pevC"][2 * par]),
                                  (sV, st3["pev"][2 * par]), (sA, sc_ld),
                                  (sV, st3["half_v"])]
                        mv1 = mm(pa[:, 0:GT], wsb_[par][:, kk * 128: kk * 128 + 128],
                                 x1ln_sb[:, kk * TC + tsl.start: kk * TC + tsl.stop],
                                 kk == 0, kk == 7, waits=w0,
                                 inc=sT if kk == 7 else None)
                    for kk in range(8):
                        mv2 = mm(pb[:, 0:GT], wsb_[par][:, 1024 + kk * 128: 1024 + kk * 128 + 128],
                                 x1ln_sb[:, kk * TC + tsl.start: kk * TC + tsl.stop],
                                 kk == 0, kk == 7,
                                 waits=[(sC, st3["pevC"][2 * par + 1]),
                                        (sV, st3["pev"][2 * par + 1])] if kk == 0 else (),
                                 inc=sT if kk == 7 else None)
                    e1 = sc.post("scalar", lambda sy, pa=pa, c=c: sy.activation(
                        teb[:, 0:GT], pa[:, 0:GT], AF.Silu, scale=sc1_sb[:, c:c + 1]),
                        waits=[(sT, mv1)], inc=sC)
                    st3["pevC"][2 * par] = e1
                    gv = sc.post("vector", lambda v, pb=pb, c=c, sp=sp, h=h: v.scalar_tensor_tensor(
                        g1b[:, c * HT + sp * GT: c * HT + (sp + 1) * GT],
                        pb[:, 0:GT], sc2_sb[:, c:c + 1], teb[:, 0:GT],
                        ALU.mult, ALU.mult), waits=[(sT, mv2), (sC, e1)], inc=sV)
                    st3["pev"][2 * par + 1] = gv
                st3["wd"][par] = sT.n
            st3["bar_v"], st3["bar_c"] = sV.n, sC.n
            for (stage, wnm, ssb, srcb, dstb) in [("B", "h1", sch1_sb, g1b, gHb),
                                                  ("C", "h2", sch2_sb, gHb, g1b)]:
                sc.mark(f"p3{stage}{h}")
                for c in range(32):
                    par = c % 2
                    wl = sc.dma("gpsimd", lambda g, c=c, par=par, wnm=wnm: g.dma_start(
                        wsa[par][:, :], din[wnm][:, c * 4096:(c + 1) * 4096]),
                        sW, waits=[(sT, st3["wd2"][par])])
                    pa, pb = pp[2 * par], pp[2 * par + 1]
                    mv = None
                    for kk in range(32):
                        w0 = ()
                        if kk == 0:
                            w0 = [(sW, wl), (sC, st3["pevC"][2 * par]),
                                  (sC, st3["pevC"][2 * par + 1]),
                                  (sV, st3["bar_v"]), (sC, st3["bar_c"])]
                        for sp in range(NS):
                            mv = mm(pa[:, 0:GT] if sp == 0 else pb[:, 0:GT],
                                    wsa[par][:, kk * 128:(kk + 1) * 128],
                                    srcb[:, kk * HT + sp * GT: kk * HT + (sp + 1) * GT],
                                    kk == 0, kk == 31, waits=w0 if sp == 0 else (),
                                    inc=sT if (kk == 31 and sp == NS - 1) else None)
                            w0 = ()
                    for sp in range(NS):
                        px = pa if sp == 0 else pb
                        ge = sc.post("scalar", lambda sy, px=px, c=c, sp=sp, ssb=ssb,
                                     dstb=dstb: sy.activation(
                            dstb[:, c * HT + sp * GT: c * HT + (sp + 1) * GT],
                            px[:, 0:GT], AF.Gelu, scale=ssb[:, c:c + 1]),
                            waits=[(sT, mv)] if sp == 0 else (), inc=sC)
                        st3["pevC"][2 * par + sp] = ge
                    st3["wd2"][par] = sT.n
                st3["bar_v"], st3["bar_c"] = sV.n, sC.n
            sc.mark(f"p3D{h}")
            for c in range(8):
                par = c % 2
                w3l = sc.dma("gpsimd", lambda g, c=c, par=par: g.dma_start(
                    wsa[par][:, :], din["w3"][:, c * 4096:(c + 1) * 4096]),
                    sW, waits=[(sT, st3["wd2"][par])])
                x1ls = []
                for sp in range(NS):
                    rp = sp
                    x1ls.append(sc.dma("sync", lambda sy, c=c, sp=sp, th=th, rp=rp:
                                       sy.dma_start(
                        x1r[rp][:, :], x1_d[c, :, th + sp * GT: th + (sp + 1) * GT]),
                        sA, waits=[(sV, st3["rfree"][rp])]))
                pa, pb = pp[2 * par], pp[2 * par + 1]
                mv = None
                for kk in range(32):
                    w0 = ()
                    if kk == 0:
                        w0 = [(sW, w3l), (sV, st3["pev"][2 * par]),
                              (sV, st3["pev"][2 * par + 1]),
                              (sV, st3["bar_v"]), (sC, st3["bar_c"])]
                    for sp in range(NS):
                        mv = mm(pa[:, 0:GT] if sp == 0 else pb[:, 0:GT],
                                wsa[par][:, kk * 128:(kk + 1) * 128],
                                g1b[:, kk * HT + sp * GT: kk * HT + (sp + 1) * GT],
                                kk == 0, kk == 31, waits=w0 if sp == 0 else (),
                                inc=sT if (kk == 31 and sp == NS - 1) else None)
                        w0 = ()
                for sp in range(NS):
                    px = pa if sp == 0 else pb
                    rp = sp
                    op = sp
                    ov = sc.post("vector", lambda v, px=px, c=c, rp=rp, op=op:
                                 v.scalar_tensor_tensor(
                        outbd[op][:, 0:GT], px[:, 0:GT], sc3_sb[:, c:c + 1],
                        x1r[rp][:, :], ALU.mult, ALU.add),
                        waits=[(sT, mv), (sA, x1ls[sp]), (sS, st3["ob_dma"][op])], inc=sV)
                    st3["pev"][2 * par + sp] = ov
                    st3["rfree"][rp] = ov
                    st3["ob_dma"][op] = sc.dma("scalar", lambda sy, c=c, sp=sp, th=th,
                                               op=op: sy.dma_start(
                        out_d[c, :, th + sp * GT: th + (sp + 1) * GT], outbd[op][:, 0:GT]),
                        sS, waits=[(sV, ov)])
                st3["wd2"][par] = sT.n
            st3["wd2"][0] = st3["wd2"][1] = sT.n
            st3["bar_v"], st3["bar_c"] = sV.n, sC.n
        sc.mark(None)
        p3stack.close()
    else:
        pass


    if "send" in taps:
        tap("send", (128, NP * 64), F32)
        sbuf_send = sb("sbuf_send", (128, NP * 64), F32)
        cvv = sc.post("vector", lambda v: v.tensor_copy(sbuf_send[:, :], scur[:, :]),
                      waits=[(sV, st8["prev_group_v"])], inc=sV)
        sc.dma("sync", lambda s: s.dma_start(tap_d["send"][:, :], sbuf_send[:, :]),
               sA, waits=[(sV, cvv)])

    sc.post("sync", None, waits=[(sA, sA.n), (sW, sW.n), (sS, sS.n)])

    with nc.Block() as block:
        for s in sc.sems.values():
            s.h = stack.enter_context(nc.semaphore(s.name))

        @block.sync
        def _(e):
            sc.replay("sync", e, nc)

        @block.tensor
        def _(e):
            sc.replay("tensor", e, nc)

        @block.vector
        def _(e):
            sc.replay("vector", e, nc)

        @block.scalar
        def _(e):
            sc.replay("scalar", e, nc)

        @block.gpsimd
        def _(e):
            sc.replay("gpsimd", e, nc)

    stack.close()
    return nc, tap_d



def kernel(**inputs):
    nc, tap_d = build()
    in_maps = [_prep_inputs(i, inputs) for i in range(NCORE)]
    res = run_bass_kernel_spmd(nc, in_maps, core_ids=list(range(NCORE)))
    out = np.empty((B, T, D), np.float32)
    for i in range(NCORE):
        b, half = i // 2, i % 2
        o = res.results[i]["out"].reshape(8 * 128, TC)
        out[b, half * TC:(half + 1) * TC, :] = o.T
    return out


def run_taps(inputs, taps, phases=None, trace=False):
    nc, tap_d = build(taps=set(taps), phases=set(phases or PHASES))
    in_maps = [_prep_inputs(i, inputs) for i in range(NCORE)]
    res = run_bass_kernel_spmd(nc, in_maps, core_ids=list(range(NCORE)), trace=trace)
    return res
